# version 13
# baseline (speedup 1.0000x reference)
import contextlib
import numpy as np
import ml_dtypes
import concourse.bass as bass
import concourse.mybir as mybir
from concourse.bass_utils import run_bass_kernel_spmd

F32 = mybir.dt.float32
BF16 = mybir.dt.bfloat16
AF = mybir.ActivationFunctionType
ALU = mybir.AluOpType
AX = mybir.AxisListType

NCORES = 8
BLOC = 2
T = 2048
D = 2048
H = 16
II = 8192
NK = D // 128
NQ = T // 128
NC_I = II // 128
NIG = 16
SCALE = float(1.0 / np.sqrt(float(T)))
EPS = 1e-5
QOFF = [128 * (q * (q + 1) // 2) for q in range(NQ + 1)]
ATT_W = QOFF[NQ]

DBG_SEQS = None
DBG_HEADS = H
DBG_PHASES = "TABCD"
DBG_DUMP = False
DBG_DUMP_AB = False
DBG_DUMP_D = False


class Sem:
    __slots__ = ("h", "v")

    def __init__(self, h):
        self.h = h
        self.v = 0


class Prog:
    ENGINES = ("tensor", "vector", "scalar", "gpsimd", "sync")

    def __init__(self):
        self.q = {e: [] for e in self.ENGINES}
        self.lastwait = {e: {} for e in self.ENGINES}
        self.last = {e: None for e in self.ENGINES}

    def instr(self, eng, fn):
        ent = {"fn": fn, "inc": None, "amt": 0}
        self.q[eng].append(ent)
        self.last[eng] = ent

    def mark(self, eng, sem, amt=1):
        ent = self.last[eng]
        assert ent is not None and ent["inc"] is None
        sem.v += amt
        ent["inc"] = sem
        ent["amt"] = amt
        return (sem, sem.v)

    def wait(self, eng, tkt):
        if tkt is None:
            return
        if isinstance(tkt, (list, tuple)) and tkt and isinstance(tkt[0], (list, tuple, type(None))):
            for t in tkt:
                self.wait(eng, t)
            return
        sem, v = tkt
        key = id(sem)
        if self.lastwait[eng].get(key, -1) >= v:
            return
        self.lastwait[eng][key] = v
        ent = {"fn": None, "inc": None, "amt": 0, "wait": (sem.h, v)}
        self.q[eng].append(ent)
        self.last[eng] = None

    def flush(self, block):
        for eng in self.ENGINES:
            ents = self.q[eng]
            if not ents:
                continue
            dec = getattr(block, eng)

            def section(E, ents=ents):
                for ent in ents:
                    if "wait" in ent:
                        s, v = ent["wait"]
                        E.wait_ge(s, v)
                    elif ent["inc"] is not None:
                        ent["fn"](E).then_inc(ent["inc"].h, ent["amt"])
                    else:
                        ent["fn"](E)

            dec(section)
        self.q = {e: [] for e in self.ENGINES}
        self.last = {e: None for e in self.ENGINES}


def build_nc():
    nc = bass.Bass("TRN2", target_bir_lowering=False)

    x_in = nc.dram_tensor("x", [BLOC, T, D], F32, kind="ExternalInput").ap()
    wq_in = nc.dram_tensor("wq", [D, D], F32, kind="ExternalInput").ap()
    wk_in = nc.dram_tensor("wk", [D, D], F32, kind="ExternalInput").ap()
    wv_in = nc.dram_tensor("wv", [D, D], F32, kind="ExternalInput").ap()
    wo_in = nc.dram_tensor("wo", [D, D], F32, kind="ExternalInput").ap()
    wg_in = nc.dram_tensor("wg", [D, II], F32, kind="ExternalInput").ap()
    wu_in = nc.dram_tensor("wu", [D, II], F32, kind="ExternalInput").ap()
    wd_in = nc.dram_tensor("wd", [II, D], F32, kind="ExternalInput").ap()
    c_id_in = nc.dram_tensor("c_id", [128, 128], BF16, kind="ExternalInput").ap()
    c_tril_in = nc.dram_tensor("c_tril", [128, 128], F32, kind="ExternalInput").ap()
    c_eps_in = nc.dram_tensor("c_eps", [128, 1], F32, kind="ExternalInput").ap()
    c_cnt2_in = nc.dram_tensor("c_cnt2", [128, 16], F32, kind="ExternalInput").ap()
    y_out = nc.dram_tensor("y", [BLOC, T, D], F32, kind="ExternalOutput").ap()
    scr_kind = "ExternalOutput" if DBG_DUMP else "Internal"
    o_scr = nc.dram_tensor("o_scr", [H, 128, T], BF16, kind=scr_kind).ap()
    wg16 = nc.dram_tensor("wg16", [D, II], BF16, kind="Internal").ap()
    wu16 = nc.dram_tensor("wu16", [D, II], BF16, kind="Internal").ap()
    wd16 = nc.dram_tensor("wd16", [II, D], BF16, kind="Internal").ap()
    wo16 = nc.dram_tensor("wo16", [D, D], BF16, kind="Internal").ap()
    xn_scr = nc.dram_tensor("xn_scr", [8, 128, D], F32, kind=scr_kind).ap()
    if DBG_DUMP_D:
        dbgd = {nm: nc.dram_tensor(f"dbgd_{nm}", shp, dt, kind="ExternalOutput").ap()
                for nm, shp, dt in [
                    ("xnfm", [128, NK * 8 * 128], BF16), ("acc", [128, 8 * D], F32),
                    ("g", [128, 512], F32), ("h", [128, 512], F32),
                    ("a", [128, 512], F32), ("sec", [128, 128], F32),
                    ("msk", [128, 512], BF16), ("hs", [128, 512], BF16),
                    ("hsT", [128, 512], BF16), ("gu", [128, 1024], F32)]}
    if DBG_DUMP_AB:
        dbg_outs = {nm: nc.dram_tensor(f"dbg_{nm}", shp, dt, kind="ExternalOutput").ap()
                    for nm, shp, dt in [
                        ("qfm", [128, T], BF16), ("kfm", [128, T], BF16),
                        ("vfm", [128, T], BF16), ("vtm", [128, NK * 128], BF16),
                        ("sufft", [1, NQ * 128], BF16), ("alpha", [1, NQ * 128], BF16),
                        ("attn", [128, T], BF16), ("attnT", [128, ATT_W], BF16),
                        ("ebuf", [128, T], F32), ("xfm", [128, NK * T], BF16)]}

    wq_v = wq_in.rearrange("(k p) (h j) -> p k h j", p=128, j=128)
    wk_v = wk_in.rearrange("(k p) (h j) -> p k h j", p=128, j=128)
    wv_v = wv_in.rearrange("(k p) (h j) -> p k h j", p=128, j=128)
    wo_v = wo16.rearrange("(k p) n -> p k n", p=128)
    wg_v = wg16.rearrange("(k p) n -> p k n", p=128)
    wu_v = wu16.rearrange("(k p) n -> p k n", p=128)
    wd_v = wd16.rearrange("(c p) n -> p c n", p=128)
    o_scr_v = o_scr.rearrange("h p t -> p h t")

    es0 = contextlib.ExitStack()
    with es0:
        names = ["dC", "dX", "dW", "dWO", "dOS", "dWF", "dXN",
                 "dSO", "dSX", "dOUT", "dCA", "sP", "sV", "sS", "sG"]
        S = {n: Sem(es0.enter_context(nc.semaphore(n))) for n in names}

        c_id = es0.enter_context(nc.sbuf_tensor("c_id_s", [128, 128], BF16))
        c_tril = es0.enter_context(nc.sbuf_tensor("c_tril_s", [128, 128], F32))
        c_eps = es0.enter_context(nc.sbuf_tensor("c_eps_s", [128, 1], F32))
        c_cnt2 = es0.enter_context(nc.sbuf_tensor("c_cnt2_s", [128, 16], F32))

        P_A = es0.enter_context(nc.psum_tensor("P_A", [128, 1024], F32))
        P_B = es0.enter_context(nc.psum_tensor("P_B", [128, 1024], F32))
        P_C = es0.enter_context(nc.psum_tensor("P_C", [128, 1024], F32))
        P_T = es0.enter_context(nc.psum_tensor("P_T", [128, 2048], BF16))

        block = es0.enter_context(nc.Block())
        P = Prog()

        P.instr("sync", lambda E: E.dma_start(out=c_id[:], in_=c_id_in))
        P.mark("sync", S["dC"], 16)
        P.instr("sync", lambda E: E.dma_start(out=c_tril[:], in_=c_tril_in))
        P.mark("sync", S["dC"], 16)
        P.instr("sync", lambda E: E.dma_start(out=c_eps[:], in_=c_eps_in))
        P.mark("sync", S["dC"], 16)
        P.instr("sync", lambda E: E.dma_start(out=c_cnt2[:], in_=c_cnt2_in))
        t_consts = P.mark("sync", S["dC"], 16)

        t_PA_free = None
        t_PB_free = None
        t_PC_free = None
        t_PT_free = [None, None]
        t_oscr_stored = None
        t_oscr_consumed = None
        t_xn_stored = [None] * 8
        t_xn_consumed = None

        def transpose_group(src_sb, src_off, n_blk, dst_sb, dst_off, ps, t_src, extra_waits=()):
            nonlocal t_PT_free
            P.wait("tensor", t_src)
            for w in extra_waits:
                P.wait("tensor", w)
            P.wait("tensor", t_PT_free[ps])
            P.wait("tensor", t_consts)
            for j in range(n_blk):
                P.instr("tensor",
                        lambda E, d=P_T[:, ps * 1024 + j * 128:ps * 1024 + (j + 1) * 128],
                        s=src_sb[:, src_off + j * 128:src_off + (j + 1) * 128]:
                        E.transpose(d, s, c_id[:]))
            t_tr = P.mark("tensor", S["sP"])
            P.wait("vector", t_tr)
            P.instr("vector",
                    lambda E, d=dst_sb[:, dst_off:dst_off + n_blk * 128],
                    s=P_T[:, ps * 1024:ps * 1024 + n_blk * 128]:
                    E.tensor_copy(d, s))
            t_cp = P.mark("vector", S["sV"])
            t_PT_free[ps] = t_cp
            return t_cp

        def transpose_group_strided(src_sb, blocks, dst_view, ps, t_src):
            nonlocal t_PT_free
            P.wait("tensor", t_src)
            P.wait("tensor", t_PT_free[ps])
            P.wait("tensor", t_consts)
            for j, off in enumerate(blocks):
                P.instr("tensor",
                        lambda E, d=P_T[:, ps * 1024 + j * 128:ps * 1024 + (j + 1) * 128],
                        s=src_sb[:, off:off + 128]:
                        E.transpose(d, s, c_id[:]))
            t_tr = P.mark("tensor", S["sP"])
            P.wait("vector", t_tr)
            nb = len(blocks)
            P.instr("vector",
                    lambda E, d=dst_view,
                    s=P_T[:, ps * 1024:ps * 1024 + nb * 128].rearrange("p (k t) -> p k t", t=128):
                    E.tensor_copy(d, s))
            t_cp = P.mark("vector", S["sV"])
            t_PT_free[ps] = t_cp
            return t_cp

        def vsync():
            t = P.mark("vector", S["sV"])
            P.wait("vector", t)
            return t

        def barrier_all():
            for eng in Prog.ENGINES:
                for sem in S.values():
                    if sem.v:
                        P.wait(eng, (sem, sem.v))

        n_out = 0
        nonlocal_n_out = [0]

        cast_jobs = []
        for src, dst, nrows, rchunk in ((wo_in, wo16, D, 256), (wd_in, wd16, II, 1024),
                                        (wg_in, wg16, D, 256), (wu_in, wu16, D, 256)):
            for r0 in range(0, nrows, rchunk):
                cast_jobs.append((src, dst, r0, r0 + rchunk))

        def emit_casts(njobs):
            t = None
            for _ in range(njobs):
                if not cast_jobs:
                    break
                src, dst, r0, r1 = cast_jobs.pop(0)
                P.instr("gpsimd",
                        lambda E, s2=src[r0:r1, :], d=dst[r0:r1, :]: E.dma_start(out=d, in_=s2))
                t = P.mark("gpsimd", S["dCA"], 16)
            return t

        t_cast_all = None
        seqs = DBG_SEQS if DBG_SEQS is not None else list(range(BLOC))
        for seq in seqs:
            barrier_all()
            with contextlib.ExitStack() as esAB:
                sb = lambda name, shape, dt: esAB.enter_context(
                    nc.sbuf_tensor(f"{name}_{seq}", shape, dt))
                xt = [sb(f"xt{i}", [128, D], F32) for i in range(2)]
                xb = [sb(f"xb{i}", [128, D], BF16) for i in range(2)]
                xfm = sb("xfm", [128, NK * T], BF16)
                wsl = {w: sb(f"w{w}0", [128, NK * 128], BF16) for w in "qkv"}
                qfm = sb("qfm", [128, T], BF16)
                kfm = sb("kfm", [128, T], BF16)
                vfm = sb("vfm", [128, T], BF16)
                vtm = sb("vtm", [128, NK * 128], BF16)
                cs_c = sb("cs", [128, 16], F32)
                suf_c = sb("suf", [128, 16], F32)
                suf_b = sb("sufb", [128, 16], BF16)
                sufft = sb("sufft", [1, NQ * 128], BF16)
                alpha = sb("alpha", [1, NQ * 128], BF16)
                e_pack = sb("epack", [128, ATT_W], BF16)
                zdm = sb("zdm", [128, 256], F32)
                rs_all = sb("rsall", [128, 96], F32)
                rsum_all = sb("rsumall", [128, 16], F32)
                rden_all = sb("rdenall", [128, 16], F32)
                rdb_all = sb("rdball", [128, 16], BF16)
                attnT = sb("attnT", [128, ATT_W], BF16)
                ofm = sb("ofm", [128, T], BF16)

                t_xt_free = [None, None]
                t_xb_free = [None, None]
                t_ofm_free = None
                t_epack_free = None

                for w, wvv in (("q", wq_v), ("k", wk_v), ("v", wv_v)):
                    P.instr("gpsimd",
                            lambda E, d=wsl[w], s=wvv[:, :, 0, :]:
                            E.dma_start(out=d[:].rearrange("p (k j) -> p k j", j=128), in_=s))
                    P.mark("gpsimd", S["dW"], 16)

                xfm_r = xfm[:].rearrange("p (k t) -> p k t", t=T)
                t_xfm = None
                for i in range(NQ):
                    sl = i % 2
                    P.wait("sync", t_xt_free[sl])
                    P.instr("sync",
                            lambda E, d=xt[sl], s=x_in[seq, i * 128:(i + 1) * 128, :]:
                            E.dma_start(out=d[:], in_=s))
                    t_ld = P.mark("sync", S["dX"], 16)
                    P.wait("vector", t_ld)
                    P.wait("vector", t_xb_free[sl])
                    P.instr("vector", lambda E, d=xb[sl], s=xt[sl]: E.tensor_copy(d[:], s[:]))
                    t_cast = P.mark("vector", S["sV"])
                    t_xt_free[sl] = t_cast
                    for g in range(4):
                        t_xfm = transpose_group_strided(
                            xb[sl], [(4 * g + j) * 128 for j in range(4)],
                            xfm_r[:, 4 * g:4 * g + 4, i * 128:(i + 1) * 128],
                            g % 2, t_cast)
                    t_xb_free[sl] = t_xfm

                t_cs_prev = None
                t_head = {}

                def fn_proj(h):
                    nonlocal t_PA_free, t_PC_free, t_cs_prev
                    t_w_ready = (S["dW"], 16 * 3 * (h + 1))

                    def proj(dst_sb, wbuf, pt, t_pfree, extra_evac_wait=None):
                        t_evac = None
                        for half in range(2):
                            P.wait("tensor", t_w_ready)
                            P.wait("tensor", t_xfm)
                            P.wait("tensor", t_pfree if half == 0 else t_evac)
                            for k in range(NK):
                                for s2 in range(2):
                                    P.instr("tensor",
                                            lambda E, o=pt[:, s2 * 512:(s2 + 1) * 512],
                                            l=wbuf[:, k * 128:(k + 1) * 128],
                                            r=xfm[:, k * T + half * 1024 + s2 * 512:
                                                  k * T + half * 1024 + (s2 + 1) * 512],
                                            st=(k == 0), sp=(k == NK - 1):
                                            E.matmul(o, l, r, start=st, stop=sp))
                            t_mm = P.mark("tensor", S["sP"])
                            P.wait("scalar", t_mm)
                            if extra_evac_wait is not None:
                                P.wait("scalar", extra_evac_wait)
                            P.instr("scalar",
                                    lambda E, d=dst_sb[:, half * 1024:(half + 1) * 1024], s=pt:
                                    E.copy(out=d, in_=s[:]))
                            t_evac = P.mark("scalar", S["sS"])
                        return t_evac

                    t_q = proj(qfm, wsl["q"], P_C, t_PC_free)
                    t_k = proj(kfm, wsl["k"], P_A, t_PA_free)
                    t_v = proj(vfm, wsl["v"], P_C, t_q, extra_evac_wait=t_cs_prev)
                    t_PA_free = t_k
                    t_PC_free = t_v
                    t_head[h] = {"q": t_q, "k": t_k, "v": t_v}
                    if h + 1 < DBG_HEADS:
                        for w, wvv, tk in (("q", wq_v, t_q), ("k", wk_v, t_k), ("v", wv_v, t_v)):
                            P.wait("gpsimd", tk)
                            P.instr("gpsimd",
                                    lambda E, d=wsl[w], s=wvv[:, :, h + 1, :]:
                                    E.dma_start(out=d[:].rearrange("p (k j) -> p k j", j=128), in_=s))
                            P.mark("gpsimd", S["dW"], 16)

                def fn_vprep(h):
                    nonlocal t_cs_prev
                    t_v = t_head[h]["v"]
                    t_vtm = None
                    for g in range(4):
                        t_vtm = transpose_group(vfm, g * 512, 4, vtm, g * 512, g % 2, t_v)
                    t_head[h]["vtm"] = t_vtm
                    P.wait("vector", t_v)
                    t_cs = None
                    for j in range(NK):
                        P.instr("vector",
                                lambda E, d=cs_c[:, j:j + 1], s=vfm[:, j * 128:(j + 1) * 128]:
                                E.reduce_sum(d, s, axis=AX.X))
                        t_cs = P.mark("vector", S["sV"])
                    P.wait("vector", t_cs)
                    for q in range(NQ - 1):
                        P.instr("vector",
                                lambda E, d=suf_c[:, q:q + 1], a=cs_c[:, q + 1:16]:
                                E.reduce_sum(d, a, axis=AX.X))
                    vsync()
                    P.instr("vector", lambda E: E.tensor_copy(suf_b[:], suf_c[:]))
                    t_suf = P.mark("vector", S["sV"])
                    t_cs_prev = t_suf
                    for q in range(NQ - 1):
                        ps = q % 2
                        P.wait("tensor", t_suf)
                        P.wait("tensor", t_PT_free[ps])
                        P.instr("tensor",
                                lambda E, d=P_T[0:1, ps * 1024 + 512:ps * 1024 + 640],
                                s=suf_b[:, q:q + 1]:
                                E.transpose(d, s, c_id[:]))
                        t_tr = P.mark("tensor", S["sP"])
                        P.wait("vector", t_tr)
                        P.instr("vector",
                                lambda E, d=sufft[0:1, q * 128:(q + 1) * 128],
                                s=P_T[0:1, ps * 1024 + 512:ps * 1024 + 640]:
                                E.tensor_copy(d, s))
                        t_PT_free[ps] = P.mark("vector", S["sV"])

                def fn_S1(h):
                    nonlocal t_PA_free
                    t_q = t_head[h]["q"]; t_k = t_head[h]["k"]
                    t_PA_slot = [t_PA_free, t_PA_free]
                    t_zde_q = [None] * NQ
                    t_exps_all = []
                    for q in range(NQ):
                        par = q % 2
                        seg_bounds = []
                        c0 = 0
                        while c0 < q * 128:
                            c1 = min(c0 + 512, q * 128)
                            seg_bounds.append((c0, c1))
                            c0 = c1
                        used = 0
                        nseg = len(seg_bounds)
                        for si in range(nseg + 1):
                            ps = si % 2
                            P.wait("tensor", t_q)
                            P.wait("tensor", t_k)
                            P.wait("tensor", t_PA_slot[ps])
                            if si < nseg:
                                c0, c1 = seg_bounds[si]
                                P.instr("tensor",
                                        lambda E, o=P_A[:, ps * 512:ps * 512 + (c1 - c0)],
                                        l=qfm[:, q * 128:(q + 1) * 128], r=kfm[:, c0:c1]:
                                        E.matmul(o, l, r, start=True, stop=True))
                                t_z = P.mark("tensor", S["sP"])
                                P.wait("scalar", t_z)
                                P.wait("scalar", t_epack_free)
                                P.instr("scalar",
                                        lambda E, d=e_pack[:, QOFF[q] + c0:QOFF[q] + c1],
                                        s=P_A[:, ps * 512:ps * 512 + (c1 - c0)],
                                        a=rs_all[:, q * 6 + used:q * 6 + used + 1]:
                                        E.activation(d, s, AF.Exp, scale=SCALE, accum_out=a))
                                t_e = P.mark("scalar", S["sS"])
                                t_PA_slot[ps] = t_e
                            else:
                                P.instr("tensor",
                                        lambda E, o=P_A[:, ps * 512:ps * 512 + 128],
                                        l=qfm[:, q * 128:(q + 1) * 128],
                                        r=kfm[:, q * 128:(q + 1) * 128]:
                                        E.matmul(o, l, r, start=True, stop=True))
                                t_z = P.mark("tensor", S["sP"])
                                P.wait("vector", t_z)
                                P.wait("vector", t_consts)
                                if q >= 2:
                                    P.wait("vector", t_zde_q[q - 2])
                                P.instr("vector",
                                        lambda E, d=zdm[:, par * 128:(par + 1) * 128],
                                        s=P_A[:, ps * 512:ps * 512 + 128]:
                                        E.tensor_tensor(d, s, c_tril[:], ALU.mult))
                                t_zd = P.mark("vector", S["sV"])
                                t_PA_slot[ps] = t_zd
                                P.wait("scalar", t_zd)
                                P.wait("scalar", t_epack_free)
                                P.instr("scalar",
                                        lambda E, d=e_pack[:, QOFF[q] + q * 128:QOFF[q] + (q + 1) * 128],
                                        sz=zdm[:, par * 128:(par + 1) * 128],
                                        a=rs_all[:, q * 6 + used:q * 6 + used + 1]:
                                        E.activation(d, sz, AF.Exp, scale=1.0, accum_out=a))
                                t_e = P.mark("scalar", S["sS"])
                                t_zde_q[q] = t_e
                            used += 1
                        t_exps_all.append((t_e, used))
                    t_PA_free = [t_PA_slot[0], t_PA_slot[1]]
                    t_head[h]["exps"] = t_exps_all

                def fn_S2S3(h):
                    nonlocal t_epack_free
                    t_exps_all = t_head[h]["exps"]
                    t_alpha_done = [None] * NQ
                    t_attnT_done = [None] * NQ
                    for q in range(NQ):
                        P.wait("vector", t_exps_all[q][0])
                    for q in range(NQ):
                        P.instr("vector",
                                lambda E, q=q, u=t_exps_all[q][1]:
                                E.reduce_sum(rsum_all[:, q:q + 1],
                                             rs_all[:, q * 6:q * 6 + u], axis=AX.X))
                    vsync()
                    P.instr("vector",
                            lambda E: E.tensor_tensor(rsum_all[:], rsum_all[:], c_cnt2[:], ALU.add))
                    vsync()
                    P.instr("vector", lambda E: E.reciprocal(rden_all[:], rsum_all[:]))
                    vsync()
                    P.instr("vector", lambda E: E.tensor_copy(rdb_all[:], rden_all[:]))
                    t_rdb = P.mark("vector", S["sV"])
                    for q in range(NQ):
                        psr = q % 2
                        P.wait("tensor", t_rdb)
                        P.wait("tensor", t_PT_free[psr])
                        P.instr("tensor",
                                lambda E, d=P_T[0:1, psr * 1024 + 512:psr * 1024 + 640],
                                s=rdb_all[:, q:q + 1]:
                                E.transpose(d, s, c_id[:]))
                        t_tra = P.mark("tensor", S["sP"])
                        P.wait("vector", t_tra)
                        P.instr("vector",
                                lambda E, d=alpha[0:1, q * 128:(q + 1) * 128],
                                s=P_T[0:1, psr * 1024 + 512:psr * 1024 + 640]:
                                E.tensor_copy(d, s))
                        t_PT_free[psr] = P.mark("vector", S["sV"])
                        t_alpha_done[q] = t_PT_free[psr]
                    for q in range(NQ):
                        P.wait("vector", t_rdb)
                        P.instr("vector",
                                lambda E, q=q, w=(q + 1) * 128:
                                E.tensor_scalar(e_pack[:, QOFF[q]:QOFF[q] + w],
                                                e_pack[:, QOFF[q]:QOFF[q] + w],
                                                rden_all[:, q:q + 1], None, ALU.mult))
                        t_nrm = P.mark("vector", S["sV"])
                        t_last = None
                        for g in range((q + 1 + 3) // 4):
                            nj = min(4, q + 1 - 4 * g)
                            t_last = transpose_group(e_pack, QOFF[q] + g * 512, nj, attnT,
                                                     QOFF[q] + g * 512, g % 2, t_nrm)
                        t_attnT_done[q] = t_last
                    t_epack_free = (S["sV"], S["sV"].v)
                    t_head[h]["alpha"] = t_alpha_done
                    t_head[h]["attnT"] = t_attnT_done

                def fn_AV(h):
                    nonlocal t_PB_free, t_ofm_free
                    t_vtm = t_head[h]["vtm"]
                    t_alpha_done = t_head[h]["alpha"]
                    t_attnT_done = t_head[h]["attnT"]
                    t_evO = [None, None]
                    for th in range(2):
                        qb = th * 8
                        P.wait("tensor", t_vtm)
                        for q in range(qb, qb + 8):
                            P.wait("tensor", t_attnT_done[q])
                            P.wait("tensor", t_alpha_done[q])
                        P.wait("tensor", t_PB_free)
                        for j in range(qb + 8):
                            for q in range(max(j, qb), qb + 8):
                                P.instr("tensor",
                                        lambda E, o=P_B[:, (q - qb) * 128:(q - qb + 1) * 128],
                                        l=vtm[:, j * 128:(j + 1) * 128],
                                        r=attnT[:, QOFF[q] + j * 128:QOFF[q] + (j + 1) * 128],
                                        st=(j == 0 and (q - qb) % 4 == 0),
                                        sp=(j == NQ - 1 and q == NQ - 1):
                                        E.matmul(o, l, r, start=st, stop=sp))
                        for q in range(qb, qb + 8):
                            if q == NQ - 1:
                                continue
                            P.instr("tensor",
                                    lambda E, o=P_B[:, (q - qb) * 128:(q - qb + 1) * 128],
                                    l=sufft[0:1, q * 128:(q + 1) * 128],
                                    r=alpha[0:1, q * 128:(q + 1) * 128]:
                                    E.matmul(o, l, r, start=False, stop=True))
                        t_av = P.mark("tensor", S["sP"])
                        P.wait("scalar", t_av)
                        P.instr("scalar",
                                lambda E, d=ofm[:, th * 1024:(th + 1) * 1024]:
                                E.copy(out=d, in_=P_B[:]))
                        t_evO[th] = P.mark("scalar", S["sS"])
                        t_PB_free = t_evO[th]
                    P.wait("sync", t_evO[0])
                    P.wait("sync", t_evO[1])
                    P.wait("sync", t_ofm_free)
                    P.wait("sync", t_oscr_consumed)
                    P.instr("sync",
                            lambda E, d=o_scr[h, :, :], s=ofm: E.dma_start(out=d, in_=s[:]))
                    t_ofm_free = P.mark("sync", S["dSO"], 16)

                if "A" in DBG_PHASES and DBG_HEADS > 0:
                    fn_proj(0)
                    emit_casts(2)
                    fn_vprep(0)
                    for h in range(DBG_HEADS):
                        fn_S1(h)
                        if h + 1 < DBG_HEADS:
                            fn_proj(h + 1)
                        emit_casts(2)
                        fn_S2S3(h)
                        fn_AV(h)
                        if h + 1 < DBG_HEADS:
                            fn_vprep(h + 1)
                    emit_casts(len(cast_jobs))
                    t_cast_all = (S["dCA"], S["dCA"].v)

                t_oscr_stored = (S["dSO"], S["dSO"].v)
                P.flush(block)

            if "C" not in DBG_PHASES:
                continue
            with contextlib.ExitStack() as esCD:
                sb2 = lambda name, shape, dt: esCD.enter_context(
                    nc.sbuf_tensor(f"{name}_{seq}", shape, dt))
                xnfm = sb2("xnfm", [128, NK * 8 * 128], BF16)
                accb = sb2("acc", [128, 8 * D], F32)

                for half in range(2):
                    barrier_all()
                    with contextlib.ExitStack() as esC:
                        sb3 = lambda name, shape, dt: esC.enter_context(
                            nc.sbuf_tensor(f"{name}_{seq}_{half}", shape, dt))
                        wo_sb = sb3("wo_sb", [128, NK * 2048], BF16)
                        osl = [sb3("osl0", [128, NK * 128], BF16),
                               sb3("osl1", [128, NK * 128], BF16)]
                        xct = [sb3(f"xct{i}", [128, D], F32) for i in range(2)]
                        r_b = sb3("r_b", [128, D], F32)
                        xn_b = sb3("xn_b", [128, D], F32)
                        xnbb = sb3("xnbb", [128, D], BF16)
                        rsm = sb3("rsm", [128, 2], F32)
                        sqm = sb3("sqm", [128, 2], F32)
                        ln1 = sb3("ln1", [128, 1], F32)
                        ln2 = sb3("ln2", [128, 1], F32)
                        ln3 = sb3("ln3", [128, 1], F32)

                        P.wait("gpsimd", t_oscr_stored)
                        P.wait("gpsimd", t_cast_all)
                        wo_r = wo_sb[:].rearrange("p (k n) -> p k n", n=2048)
                        P.instr("gpsimd",
                                lambda E: E.dma_start(out=wo_r[:, :, 0:1024], in_=wo_v[:, :, 0:1024]))
                        t_wo0 = P.mark("gpsimd", S["dWO"], 16)
                        P.instr("gpsimd",
                                lambda E: E.dma_start(out=wo_r[:, :, 1024:2048], in_=wo_v[:, :, 1024:2048]))
                        t_wo1 = P.mark("gpsimd", S["dWO"], 16)
                        t_wo_h = [t_wo0, t_wo1]

                        t_osl_free = [None, None]
                        t_os_t = [None, None]
                        t_xct_free = [None, None]
                        t_sq_prev = None
                        P.wait("sync", t_oscr_stored)
                        P.instr("sync",
                                lambda E, d=osl[0][:].rearrange("p (k j) -> p k j", j=128),
                                s=o_scr_v[:, :, half * 1024:half * 1024 + 128]:
                                E.dma_start(out=d, in_=s))
                        t_os_t[0] = P.mark("sync", S["dOS"], 16)
                        for ts in range(8):
                            tg = half * 8 + ts
                            osb = osl[ts % 2]
                            t_os = t_os_t[ts % 2]
                            if ts + 1 < 8:
                                P.wait("sync", t_osl_free[(ts + 1) % 2])
                                P.instr("sync",
                                        lambda E, d=osl[(ts + 1) % 2][:].rearrange("p (k j) -> p k j", j=128),
                                        s=o_scr_v[:, :, (tg + 1) * 128:(tg + 2) * 128]:
                                        E.dma_start(out=d, in_=s))
                                t_os_t[(ts + 1) % 2] = P.mark("sync", S["dOS"], 16)
                            sl = ts % 2
                            P.wait("sync", t_xct_free[sl])
                            P.instr("sync",
                                    lambda E, d=xct[sl], s=x_in[seq, tg * 128:(tg + 1) * 128, :]:
                                    E.dma_start(out=d[:], in_=s))
                            t_xl = P.mark("sync", S["dX"], 16)
                            t_r = [None, None]
                            for ddh in range(2):
                                pt = P_A if ddh == 0 else P_C
                                P.wait("tensor", t_wo_h[ddh])
                                P.wait("tensor", t_os)
                                P.wait("tensor", t_PA_free if ddh == 0 else t_PC_free)
                                for k in range(NK):
                                    for s2 in range(2):
                                        P.instr("tensor",
                                                lambda E, o=pt[:, s2 * 512:(s2 + 1) * 512],
                                                l=osb[:, k * 128:(k + 1) * 128],
                                                r=wo_sb[:, k * 2048 + ddh * 1024 + s2 * 512:
                                                      k * 2048 + ddh * 1024 + (s2 + 1) * 512],
                                                st=(k == 0), sp=(k == NK - 1):
                                                E.matmul(o, l, r, start=st, stop=sp))
                                t_mm = P.mark("tensor", S["sP"])
                                P.wait("vector", t_mm)
                                P.wait("vector", t_xl)
                                P.wait("vector", t_sq_prev)
                                P.instr("vector",
                                        lambda E, d=r_b[:, ddh * 1024:(ddh + 1) * 1024],
                                        x2=xct[sl][:, ddh * 1024:(ddh + 1) * 1024],
                                        a=rsm[:, ddh:ddh + 1], sp2=pt:
                                        E.scalar_tensor_tensor(d, sp2[:], 1.0, x2,
                                                               ALU.mult, ALU.add, accum_out=a))
                                t_r[ddh] = P.mark("vector", S["sV"])
                                if ddh == 0:
                                    t_PA_free = t_r[0]
                                else:
                                    t_PC_free = t_r[1]
                            t_osl_free[ts % 2] = t_r[1]
                            t_xct_free[sl] = t_r[1]
                            for ddh in range(2):
                                P.wait("scalar", t_r[ddh])
                                P.instr("scalar",
                                        lambda E, d=xn_b[:, ddh * 1024:(ddh + 1) * 1024],
                                        s=r_b[:, ddh * 1024:(ddh + 1) * 1024],
                                        a=sqm[:, ddh:ddh + 1]:
                                        E.activation(d, s, AF.Square, accum_out=a))
                            t_sq = P.mark("scalar", S["sS"])
                            t_sq_prev = t_sq
                            P.wait("vector", t_sq)
                            P.instr("vector",
                                    lambda E: E.tensor_tensor(ln1[:], rsm[:, 0:1], rsm[:, 1:2], ALU.add))
                            vsync()
                            P.instr("vector",
                                    lambda E: E.tensor_scalar(ln1[:], ln1[:], 1.0 / D, None, ALU.mult))
                            P.instr("vector",
                                    lambda E: E.tensor_tensor(ln2[:], sqm[:, 0:1], sqm[:, 1:2], ALU.add))
                            vsync()
                            P.instr("vector",
                                    lambda E: E.tensor_scalar(ln2[:], ln2[:], 1.0 / D, None, ALU.mult))
                            vsync()
                            P.instr("vector",
                                    lambda E: E.tensor_tensor(ln3[:], ln1[:], ln1[:], ALU.mult))
                            vsync()
                            P.instr("vector",
                                    lambda E: E.tensor_tensor(ln2[:], ln2[:], ln3[:], ALU.subtract))
                            t_var = P.mark("vector", S["sV"])
                            P.wait("scalar", t_var)
                            P.instr("scalar",
                                    lambda E: E.activation(ln2[:], ln2[:], AF.Sqrt,
                                                           bias=c_eps[:], scale=1.0))
                            t_sqr = P.mark("scalar", S["sS"])
                            P.wait("vector", t_sqr)
                            P.instr("vector", lambda E: E.reciprocal(ln3[:], ln2[:]))
                            vsync()
                            P.instr("vector",
                                    lambda E: E.tensor_scalar(xn_b[:], r_b[:], ln1[:], None,
                                                              ALU.subtract))
                            P.instr("vector",
                                    lambda E: E.tensor_scalar(r_b[:], xn_b[:], ln3[:], None,
                                                              ALU.mult))
                            P.instr("vector", lambda E: E.tensor_copy(xnbb[:], r_b[:]))
                            t_xnb = P.mark("vector", S["sV"])
                            P.wait("sync", t_xnb)
                            P.wait("sync", t_xn_consumed)
                            P.instr("sync",
                                    lambda E, d=xn_scr[ts, :, :]: E.dma_start(out=d, in_=r_b[:]))
                            t_xn_stored[ts] = P.mark("sync", S["dSX"], 16)
                            xnfm_r = xnfm[:].rearrange("p (k t) -> p k t", t=8 * 128)
                            for g in range(4):
                                transpose_group_strided(
                                    xnbb, [(4 * g + j) * 128 for j in range(4)],
                                    xnfm_r[:, 4 * g:4 * g + 4, ts * 128:(ts + 1) * 128],
                                    g % 2, t_xnb)
                        t_xnfm = t_PT_free[1] if t_PT_free[1] else t_PT_free[0]
                        t_xnfm = (S["sV"], S["sV"].v)
                        P.flush(block)

                    if "D" not in DBG_PHASES:
                        continue
                    barrier_all()
                    with contextlib.ExitStack() as esD:
                        sb4 = lambda name, shape, dt: esD.enter_context(
                            nc.sbuf_tensor(f"{name}_{seq}_{half}", shape, dt))
                        wgb = [sb4(f"wgb{i}", [128, NK * 512], BF16) for i in range(2)]
                        wub = [sb4(f"wub{i}", [128, NK * 512], BF16) for i in range(2)]
                        wdb = sb4("wdb", [128, 4 * 2048], BF16)
                        g_b = sb4("g_b", [128, 512], F32)
                        h_b = sb4("h_b", [128, 512], F32)
                        a_b = sb4("a_b", [128, 512], F32)
                        tA = sb4("tA", [128, 128], F32)
                        tB = sb4("tB", [128, 128], F32)
                        tC2 = sb4("tC2", [128, 128], F32)
                        tD2 = sb4("tD2", [128, 128], F32)
                        sec = sb4("sec", [128, 128], F32)
                        msk = sb4("msk", [128, 512], BF16)
                        hs_b = sb4("hs_b", [128, 512], BF16)
                        hsT = sb4("hsT", [128, 512], BF16)
                        xtf = [sb4("xtf0", [128, D], F32)]
                        outb = [sb4("outb0", [128, D], F32)]
                        r2sm = sb4("r2sm", [128, 1], F32)
                        q2sm = sb4("q2sm", [128, 1], F32)
                        l21 = sb4("l21", [128, 1], F32)
                        l22 = sb4("l22", [128, 1], F32)
                        l23 = sb4("l23", [128, 1], F32)

                        a_g = a_b[:].rearrange("p (g four) -> p g four", four=4)
                        h_g = h_b[:].rearrange("p (g four) -> p g four", four=4)
                        m_g = msk[:].rearrange("p (g four) -> p g four", four=4)

                        t_wgb_ready = [None, None]

                        def load_wgrp(ig, buf_i, t_free_g, t_free_u):
                            P.wait("gpsimd", t_free_g)
                            P.instr("gpsimd",
                                    lambda E, d=wgb[buf_i][:].rearrange("p (k n) -> p k n", n=512),
                                    s=wg_v[:, :, ig * 512:(ig + 1) * 512]:
                                    E.dma_start(out=d, in_=s))
                            P.mark("gpsimd", S["dWF"], 16)
                            P.wait("gpsimd", t_free_u)
                            P.instr("gpsimd",
                                    lambda E, d=wub[buf_i][:].rearrange("p (k n) -> p k n", n=512),
                                    s=wu_v[:, :, ig * 512:(ig + 1) * 512]:
                                    E.dma_start(out=d, in_=s))
                            t_wgb_ready[buf_i] = P.mark("gpsimd", S["dWF"], 16)
                            return t_wgb_ready[buf_i]

                        t_wg_free = [None, None]
                        load_wgrp(0, 0, None, None)
                        t_wd_free = None
                        t_wd = None
                        t_msk_prev = None
                        t_hs_prev = None
                        t_hs_free = None
                        t_hsT_free = None
                        t_acc_done = [None] * 8
                        t_psum_free = [t_PB_free, t_PA_free]
                        t_g_free = None
                        t_down_done = None
                        NU = NIG * 8

                        def emitA(u):
                            nonlocal t_wd, t_wd_free
                            ig, ts = u // 8, u % 8
                            bi = ig % 2
                            par = u % 2
                            pt = P_B if par == 0 else P_A
                            P.wait("tensor", t_wgb_ready[bi])
                            P.wait("tensor", t_xnfm)
                            P.wait("tensor", t_psum_free[par])
                            for k in range(NK):
                                P.instr("tensor",
                                        lambda E, o=pt[:, 0:512],
                                        l=xnfm[:, k * 1024 + ts * 128:k * 1024 + (ts + 1) * 128],
                                        r=wgb[bi][:, k * 512:(k + 1) * 512],
                                        st=(k == 0), sp=(k == NK - 1):
                                        E.matmul(o, l, r, start=st, stop=sp))
                                P.instr("tensor",
                                        lambda E, o=pt[:, 512:1024],
                                        l=xnfm[:, k * 1024 + ts * 128:k * 1024 + (ts + 1) * 128],
                                        r=wub[bi][:, k * 512:(k + 1) * 512],
                                        st=(k == 0), sp=(k == NK - 1):
                                        E.matmul(o, l, r, start=st, stop=sp))
                            return P.mark("tensor", S["sP"]), pt

                        def emitRest(u, t_gu, pt):
                            nonlocal t_msk_prev, t_hs_prev, t_hs_free, t_hsT_free
                            nonlocal t_psum_free, t_g_free, t_down_done, t_wd_free
                            ig, ts = u // 8, u % 8
                            par = u % 2
                            P.wait("scalar", t_gu)
                            P.wait("scalar", t_g_free)
                            P.instr("scalar",
                                    lambda E: E.activation(g_b[:], pt[:, 0:512], AF.Silu))
                            t_si = P.mark("scalar", S["sS"])
                            P.wait("vector", t_si)
                            P.instr("vector",
                                    lambda E: E.tensor_tensor(h_b[:], g_b[:], pt[:, 512:1024],
                                                              ALU.mult))
                            t_h = P.mark("vector", S["sV"])
                            t_psum_free[par] = t_h
                            t_g_free = t_h
                            P.wait("scalar", t_h)
                            P.wait("scalar", t_msk_prev)
                            P.instr("scalar",
                                    lambda E: E.activation(a_b[:], h_b[:], AF.Square))
                            t_a = P.mark("scalar", S["sS"])
                            P.wait("vector", t_a)
                            P.instr("vector",
                                    lambda E: E.tensor_tensor(tA[:], a_g[:, :, 0], a_g[:, :, 1], ALU.max))
                            P.instr("vector",
                                    lambda E: E.tensor_tensor(tB[:], a_g[:, :, 2], a_g[:, :, 3], ALU.max))
                            P.instr("vector",
                                    lambda E: E.tensor_tensor(tC2[:], a_g[:, :, 0], a_g[:, :, 1], ALU.min))
                            P.instr("vector",
                                    lambda E: E.tensor_tensor(tD2[:], a_g[:, :, 2], a_g[:, :, 3], ALU.min))
                            P.instr("vector",
                                    lambda E: E.tensor_tensor(tA[:], tA[:], tB[:], ALU.min))
                            P.instr("vector",
                                    lambda E: E.tensor_tensor(tC2[:], tC2[:], tD2[:], ALU.max))
                            P.instr("vector",
                                    lambda E: E.tensor_tensor(sec[:], tA[:], tC2[:], ALU.max))
                            for j4 in range(4):
                                P.instr("vector",
                                        lambda E, j4=j4:
                                        E.tensor_tensor(m_g[:, :, j4], a_g[:, :, j4], sec[:],
                                                        ALU.is_ge))
                            t_msk = P.mark("vector", S["sV"])
                            t_msk_prev = t_msk
                            P.wait("vector", t_hs_free)
                            P.instr("vector",
                                    lambda E: E.tensor_tensor(hs_b[:], h_b[:], msk[:], ALU.mult))
                            t_hs = P.mark("vector", S["sV"])
                            t_hs_prev = t_hs
                            if DBG_DUMP_D and ig == 0 and ts == 0:
                                P.wait("vector", t_gu)
                                P.instr("vector", lambda E: E.tensor_copy(xtf[0][:, 0:1024], pt[:]))
                                t_dgu = P.mark("vector", S["sV"])
                                for nm, src in [("g", g_b), ("h", h_b), ("a", a_b),
                                                ("sec", sec), ("msk", msk), ("hs", hs_b)]:
                                    P.wait("sync", t_hs)
                                    P.wait("sync", t_dgu)
                                    P.instr("sync",
                                            lambda E, d=dbgd[nm], s2=src:
                                            E.dma_start(out=d, in_=s2[:]))
                                    P.mark("sync", S["dSO"], 16)
                                P.instr("sync", lambda E: E.dma_start(out=dbgd["gu"], in_=xtf[0][:, 0:1024]))
                                t_dgu2 = P.mark("sync", S["dSO"], 16)
                                P.wait("sync", t_dgu2)
                            t_hsT = transpose_group(hs_b, 0, 4, hsT, 0, ig % 2, t_hs)
                            t_hs_free = t_hsT
                            if DBG_DUMP_D and ig == 0 and ts == 0:
                                P.wait("sync", t_hsT)
                                P.instr("sync", lambda E: E.dma_start(out=dbgd["hsT"], in_=hsT[:]))
                                P.mark("sync", S["dSO"], 16)
                            t_ac = None
                            for qt in range(4):
                                bk = qt % 2
                                P.wait("tensor", t_hsT)
                                P.wait("tensor", t_wd)
                                P.wait("tensor", t_PC_free_l[bk])
                                for c in range(4):
                                    P.instr("tensor",
                                            lambda E, o=P_C[:, bk * 512:(bk + 1) * 512],
                                            l=hsT[:, c * 128:(c + 1) * 128],
                                            r=wdb[:, c * 2048 + qt * 512:
                                                  c * 2048 + (qt + 1) * 512],
                                            st=(c == 0), sp=(c == 3):
                                            E.matmul(o, l, r, start=st, stop=sp))
                                t_dn = P.mark("tensor", S["sP"])
                                P.wait("vector", t_dn)
                                if ig == 0:
                                    P.instr("vector",
                                            lambda E, d=accb[:, ts * D + qt * 512:
                                                           ts * D + (qt + 1) * 512],
                                            sp2=P_C[:, bk * 512:(bk + 1) * 512]:
                                            E.tensor_copy(d, sp2))
                                else:
                                    P.instr("vector",
                                            lambda E, d=accb[:, ts * D + qt * 512:
                                                           ts * D + (qt + 1) * 512],
                                            sp2=P_C[:, bk * 512:(bk + 1) * 512]:
                                            E.tensor_tensor(d, d, sp2, ALU.add))
                                t_ac = P.mark("vector", S["sV"])
                                t_PC_free_l[bk] = t_ac
                            if ig == NIG - 1:
                                t_acc_done[ts] = t_ac
                            t_down_done = (S["sP"], S["sP"].v)

                        t_PC_free_l = [t_PC_free, t_PC_free]

                        t_xtf_free = [None]
                        t_ob_free = [None]

                        def emit_final(ts):
                            tg = half * 8 + ts
                            sl = 0
                            P.wait("sync", t_xn_stored[ts])
                            P.wait("sync", t_xtf_free[sl])
                            P.instr("sync",
                                    lambda E, d=xtf[sl], s=xn_scr[ts, :, :]:
                                    E.dma_start(out=d[:], in_=s))
                            t_xnl = P.mark("sync", S["dXN"], 16)
                            P.wait("vector", t_xnl)
                            P.wait("vector", t_acc_done[ts])
                            P.wait("vector", t_ob_free[0])
                            ob = outb[0]
                            P.instr("vector",
                                    lambda E, d=ob, a=accb[:, ts * D:(ts + 1) * D], x2=xtf[sl]:
                                    E.scalar_tensor_tensor(d[:], a, 1.0, x2[:], ALU.mult, ALU.add,
                                                           accum_out=r2sm[:]))
                            t_r2 = P.mark("vector", S["sV"])
                            P.wait("scalar", t_r2)
                            P.instr("scalar",
                                    lambda E, s=ob, d=accb[:, ts * D:(ts + 1) * D]:
                                    E.activation(d, s[:], AF.Square, accum_out=q2sm[:]))
                            t_q2 = P.mark("scalar", S["sS"])
                            P.wait("vector", t_q2)
                            P.instr("vector",
                                    lambda E: E.tensor_scalar(l21[:], r2sm[:], 1.0 / D, None, ALU.mult))
                            P.instr("vector",
                                    lambda E: E.tensor_scalar(l22[:], q2sm[:], 1.0 / D, None, ALU.mult))
                            vsync()
                            P.instr("vector",
                                    lambda E: E.tensor_tensor(l23[:], l21[:], l21[:], ALU.mult))
                            vsync()
                            P.instr("vector",
                                    lambda E: E.tensor_tensor(l22[:], l22[:], l23[:], ALU.subtract))
                            t_v2 = P.mark("vector", S["sV"])
                            P.wait("scalar", t_v2)
                            P.instr("scalar",
                                    lambda E: E.activation(l22[:], l22[:], AF.Sqrt,
                                                           bias=c_eps[:], scale=1.0))
                            t_s2 = P.mark("scalar", S["sS"])
                            P.wait("vector", t_s2)
                            P.instr("vector", lambda E: E.reciprocal(l23[:], l22[:]))
                            vsync()
                            P.instr("vector",
                                    lambda E, d=xtf[sl], s=ob: E.tensor_scalar(d[:], s[:], l21[:],
                                                                               None, ALU.subtract))
                            P.instr("vector",
                                    lambda E, d=ob, s=xtf[sl]: E.tensor_scalar(d[:], s[:], l23[:],
                                                                               None, ALU.mult))
                            t_o = P.mark("vector", S["sV"])
                            t_xtf_free[sl] = t_o
                            P.wait("sync", t_o)
                            P.instr("sync",
                                    lambda E, d=y_out[seq, tg * 128:(tg + 1) * 128, :], s=ob:
                                    E.dma_start(out=d, in_=s[:]))
                            t_ob_free[0] = P.mark("sync", S["dOUT"], 16)
                            nonlocal_n_out[0] += 1

                        prev = None
                        for u in range(NU):
                            ig, ts = u // 8, u % 8
                            cur = emitA(u)
                            if (u == 0) or (ts == 1 and ig >= 1):
                                P.wait("gpsimd", t_wd_free)
                                P.instr("gpsimd",
                                        lambda E, d=wdb[:].rearrange("p (c n) -> p c n", n=2048),
                                        s=wd_v[:, 4 * ig:4 * ig + 4, :]:
                                        E.dma_start(out=d, in_=s))
                                t_wd = P.mark("gpsimd", S["dWF"], 16)
                                if ig + 1 < NIG:
                                    load_wgrp(ig + 1, (ig + 1) % 2,
                                              t_wg_free[(ig + 1) % 2], t_wg_free[(ig + 1) % 2])
                            if prev is not None:
                                emitRest(*prev)
                                if ts == 0:
                                    t_wg_free[(ig + 1) % 2] = prev[1]
                                    t_wd_free = t_down_done
                                if prev[0] // 8 == NIG - 1:
                                    emit_final(prev[0] % 8)
                            prev = (u, cur[0], cur[1])
                        emitRest(*prev)
                        emit_final(prev[0] % 8)
                        t_PC_free = [t_PC_free_l[0], t_PC_free_l[1]]
                        t_PA_free = t_psum_free[1]
                        t_PB_free = t_psum_free[0]

                        if DBG_DUMP_D and half == 0:
                            P.wait("sync", (S["sV"], S["sV"].v))
                            P.instr("sync", lambda E: E.dma_start(out=dbgd["acc"], in_=accb[:]))
                            P.mark("sync", S["dSO"], 16)
                            P.instr("sync", lambda E: E.dma_start(out=dbgd["xnfm"], in_=xnfm[:]))
                            P.mark("sync", S["dSO"], 16)
                        t_xn_consumed = (S["dXN"], S["dXN"].v)
                        P.flush(block)

                t_oscr_consumed = (S["dOS"], S["dOS"].v)

        n_out = nonlocal_n_out[0]
        for nm in ("dOUT", "dSO", "dSX"):
            if S[nm].v:
                P.wait("sync", (S[nm], S[nm].v))
        for nm in ("sP", "sV", "sS", "sG"):
            if S[nm].v:
                P.wait("sync", (S[nm], S[nm].v))
        P.flush(block)

    return nc


_NC_CACHE = None


def _get_nc():
    global _NC_CACHE
    if _NC_CACHE is None:
        _NC_CACHE = build_nc()
    return _NC_CACHE


def kernel(L0_attn_x, L0_attn_Wq, L0_attn_Wk, L0_attn_Wv, L0_attn_Wo,
           L0_ffn_w_gate, L0_ffn_w_up, L0_ffn_w_down):
    x = np.ascontiguousarray(np.asarray(L0_attn_x, dtype=np.float32))
    wq = np.ascontiguousarray(np.asarray(L0_attn_Wq, dtype=np.float32))
    wk = np.ascontiguousarray(np.asarray(L0_attn_Wk, dtype=np.float32))
    wv = np.ascontiguousarray(np.asarray(L0_attn_Wv, dtype=np.float32))
    wo = np.ascontiguousarray(np.asarray(L0_attn_Wo, dtype=np.float32))
    wg = np.ascontiguousarray(np.asarray(L0_ffn_w_gate, dtype=np.float32))
    wu = np.ascontiguousarray(np.asarray(L0_ffn_w_up, dtype=np.float32))
    wd = np.ascontiguousarray(np.asarray(L0_ffn_w_down, dtype=np.float32))

    c_id = np.eye(128, dtype=ml_dtypes.bfloat16)
    c_tril = (np.tril(np.ones((128, 128), dtype=np.float32)) * SCALE).astype(np.float32)
    c_eps = np.full((128, 1), EPS, dtype=np.float32)
    c_cnt2 = np.tile(np.array([float(T - (q + 1) * 128) for q in range(NQ)],
                              dtype=np.float32)[None, :], (128, 1))

    nc = _get_nc()
    in_maps = []
    for c in range(NCORES):
        in_maps.append({
            "x": x[c * BLOC:(c + 1) * BLOC],
            "wq": wq, "wk": wk, "wv": wv, "wo": wo,
            "wg": wg, "wu": wu, "wd": wd,
            "c_id": c_id, "c_tril": c_tril, "c_eps": c_eps, "c_cnt2": c_cnt2,
        })
    res = run_bass_kernel_spmd(nc, in_maps, core_ids=list(range(NCORES)))
    out = np.concatenate([r["y"] for r in res.results], axis=0)
    return out.astype(np.float32)


# version 14
# speedup vs baseline: 1.0894x; 1.0894x over previous
import contextlib
import numpy as np
import ml_dtypes
import concourse.bass as bass
import concourse.mybir as mybir
from concourse.bass_utils import run_bass_kernel_spmd

F32 = mybir.dt.float32
BF16 = mybir.dt.bfloat16
AF = mybir.ActivationFunctionType
ALU = mybir.AluOpType
AX = mybir.AxisListType

NCORES = 8
BLOC = 2
T = 2048
D = 2048
H = 16
II = 8192
NK = D // 128
NQ = T // 128
NC_I = II // 128
NIG = 16
SCALE = float(1.0 / np.sqrt(float(T)))
EPS = 1e-5
QOFF = [128 * (q * (q + 1) // 2) for q in range(NQ + 1)]
ATT_W = QOFF[NQ]

DBG_SEQS = None
DBG_HEADS = H
DBG_PHASES = "TABCD"
DBG_DUMP = False
DBG_DUMP_AB = False
DBG_DUMP_D = False


class Sem:
    __slots__ = ("h", "v")

    def __init__(self, h):
        self.h = h
        self.v = 0


class Prog:
    ENGINES = ("tensor", "vector", "scalar", "gpsimd", "sync")

    def __init__(self):
        self.q = {e: [] for e in self.ENGINES}
        self.lastwait = {e: {} for e in self.ENGINES}
        self.last = {e: None for e in self.ENGINES}

    def instr(self, eng, fn):
        ent = {"fn": fn, "inc": None, "amt": 0}
        self.q[eng].append(ent)
        self.last[eng] = ent

    def mark(self, eng, sem, amt=1):
        ent = self.last[eng]
        assert ent is not None and ent["inc"] is None
        sem.v += amt
        ent["inc"] = sem
        ent["amt"] = amt
        return (sem, sem.v)

    def wait(self, eng, tkt):
        if tkt is None:
            return
        if isinstance(tkt, (list, tuple)) and tkt and isinstance(tkt[0], (list, tuple, type(None))):
            for t in tkt:
                self.wait(eng, t)
            return
        sem, v = tkt
        key = id(sem)
        if self.lastwait[eng].get(key, -1) >= v:
            return
        self.lastwait[eng][key] = v
        ent = {"fn": None, "inc": None, "amt": 0, "wait": (sem.h, v)}
        self.q[eng].append(ent)
        self.last[eng] = None

    def flush(self, block):
        for eng in self.ENGINES:
            ents = self.q[eng]
            if not ents:
                continue
            dec = getattr(block, eng)

            def section(E, ents=ents):
                for ent in ents:
                    if "wait" in ent:
                        s, v = ent["wait"]
                        E.wait_ge(s, v)
                    elif ent["inc"] is not None:
                        ent["fn"](E).then_inc(ent["inc"].h, ent["amt"])
                    else:
                        ent["fn"](E)

            dec(section)
        self.q = {e: [] for e in self.ENGINES}
        self.last = {e: None for e in self.ENGINES}


def build_nc():
    nc = bass.Bass("TRN2", target_bir_lowering=False)

    x_in = nc.dram_tensor("x", [BLOC, T, D], F32, kind="ExternalInput").ap()
    wq_in = nc.dram_tensor("wq", [D, D], F32, kind="ExternalInput").ap()
    wk_in = nc.dram_tensor("wk", [D, D], F32, kind="ExternalInput").ap()
    wv_in = nc.dram_tensor("wv", [D, D], F32, kind="ExternalInput").ap()
    wo_in = nc.dram_tensor("wo", [D, D], F32, kind="ExternalInput").ap()
    wg_in = nc.dram_tensor("wg", [D, II], F32, kind="ExternalInput").ap()
    wu_in = nc.dram_tensor("wu", [D, II], F32, kind="ExternalInput").ap()
    wd_in = nc.dram_tensor("wd", [II, D], F32, kind="ExternalInput").ap()
    c_id_in = nc.dram_tensor("c_id", [128, 128], BF16, kind="ExternalInput").ap()
    c_tril_in = nc.dram_tensor("c_tril", [128, 128], F32, kind="ExternalInput").ap()
    c_eps_in = nc.dram_tensor("c_eps", [128, 1], F32, kind="ExternalInput").ap()
    c_cnt2_in = nc.dram_tensor("c_cnt2", [128, 16], F32, kind="ExternalInput").ap()
    y_out = nc.dram_tensor("y", [BLOC, T, D], F32, kind="ExternalOutput").ap()
    scr_kind = "ExternalOutput" if DBG_DUMP else "Internal"
    o_scr = nc.dram_tensor("o_scr", [H, 128, T], BF16, kind=scr_kind).ap()
    wg16 = nc.dram_tensor("wg16", [D, II], BF16, kind="Internal").ap()
    wu16 = nc.dram_tensor("wu16", [D, II], BF16, kind="Internal").ap()
    wd16 = nc.dram_tensor("wd16", [II, D], BF16, kind="Internal").ap()
    wo16 = nc.dram_tensor("wo16", [D, D], BF16, kind="Internal").ap()
    xn_scr = nc.dram_tensor("xn_scr", [8, 128, D], F32, kind=scr_kind).ap()
    if DBG_DUMP_D:
        dbgd = {nm: nc.dram_tensor(f"dbgd_{nm}", shp, dt, kind="ExternalOutput").ap()
                for nm, shp, dt in [
                    ("xnfm", [128, NK * 8 * 128], BF16), ("acc", [128, 8 * D], F32),
                    ("g", [128, 512], F32), ("h", [128, 512], F32),
                    ("a", [128, 512], F32), ("sec", [128, 128], F32),
                    ("msk", [128, 512], BF16), ("hs", [128, 512], BF16),
                    ("hsT", [128, 512], BF16), ("gu", [128, 1024], F32)]}
    if DBG_DUMP_AB:
        dbg_outs = {nm: nc.dram_tensor(f"dbg_{nm}", shp, dt, kind="ExternalOutput").ap()
                    for nm, shp, dt in [
                        ("qfm", [128, T], BF16), ("kfm", [128, T], BF16),
                        ("vfm", [128, T], BF16), ("vtm", [128, NK * 128], BF16),
                        ("sufft", [1, NQ * 128], BF16), ("alpha", [1, NQ * 128], BF16),
                        ("attn", [128, T], BF16), ("attnT", [128, ATT_W], BF16),
                        ("ebuf", [128, T], F32), ("xfm", [128, NK * T], BF16)]}

    wq_v = wq_in.rearrange("(k p) (h j) -> p k h j", p=128, j=128)
    wk_v = wk_in.rearrange("(k p) (h j) -> p k h j", p=128, j=128)
    wv_v = wv_in.rearrange("(k p) (h j) -> p k h j", p=128, j=128)
    wo_v = wo16.rearrange("(k p) n -> p k n", p=128)
    wg_v = wg16.rearrange("(k p) n -> p k n", p=128)
    wu_v = wu16.rearrange("(k p) n -> p k n", p=128)
    wd_v = wd16.rearrange("(c p) n -> p c n", p=128)
    o_scr_v = o_scr.rearrange("h p t -> p h t")

    es0 = contextlib.ExitStack()
    with es0:
        names = ["dC", "dX", "dW", "dWO", "dOS", "dWF", "dXN",
                 "dSO", "dSX", "dOUT", "dCA", "sP", "sV", "sS", "sG"]
        S = {n: Sem(es0.enter_context(nc.semaphore(n))) for n in names}

        c_id = es0.enter_context(nc.sbuf_tensor("c_id_s", [128, 128], BF16))
        c_tril = es0.enter_context(nc.sbuf_tensor("c_tril_s", [128, 128], F32))
        c_eps = es0.enter_context(nc.sbuf_tensor("c_eps_s", [128, 1], F32))
        c_cnt2 = es0.enter_context(nc.sbuf_tensor("c_cnt2_s", [128, 16], F32))

        P_A = es0.enter_context(nc.psum_tensor("P_A", [128, 1024], F32))
        P_B = es0.enter_context(nc.psum_tensor("P_B", [128, 1024], F32))
        P_C = es0.enter_context(nc.psum_tensor("P_C", [128, 1024], F32))
        P_T = es0.enter_context(nc.psum_tensor("P_T", [128, 2048], BF16))

        block = es0.enter_context(nc.Block())
        P = Prog()

        P.instr("sync", lambda E: E.dma_start(out=c_id[:], in_=c_id_in))
        P.mark("sync", S["dC"], 16)
        P.instr("sync", lambda E: E.dma_start(out=c_tril[:], in_=c_tril_in))
        P.mark("sync", S["dC"], 16)
        P.instr("sync", lambda E: E.dma_start(out=c_eps[:], in_=c_eps_in))
        P.mark("sync", S["dC"], 16)
        P.instr("sync", lambda E: E.dma_start(out=c_cnt2[:], in_=c_cnt2_in))
        t_consts = P.mark("sync", S["dC"], 16)

        t_PA_free = None
        t_PB_free = None
        t_PC_free = None
        t_PT_free = [None, None]
        t_oscr_stored = None
        t_oscr_consumed = None
        t_xn_stored = [None] * 8
        t_xn_consumed = None

        def transpose_group(src_sb, src_off, n_blk, dst_sb, dst_off, ps, t_src, extra_waits=()):
            nonlocal t_PT_free
            P.wait("tensor", t_src)
            for w in extra_waits:
                P.wait("tensor", w)
            P.wait("tensor", t_PT_free[ps])
            P.wait("tensor", t_consts)
            for j in range(n_blk):
                P.instr("tensor",
                        lambda E, d=P_T[:, ps * 1024 + j * 128:ps * 1024 + (j + 1) * 128],
                        s=src_sb[:, src_off + j * 128:src_off + (j + 1) * 128]:
                        E.transpose(d, s, c_id[:]))
            t_tr = P.mark("tensor", S["sP"])
            P.wait("vector", t_tr)
            P.instr("vector",
                    lambda E, d=dst_sb[:, dst_off:dst_off + n_blk * 128],
                    s=P_T[:, ps * 1024:ps * 1024 + n_blk * 128]:
                    E.tensor_copy(d, s))
            t_cp = P.mark("vector", S["sV"])
            t_PT_free[ps] = t_cp
            return t_cp

        def transpose_group_strided(src_sb, blocks, dst_view, ps, t_src):
            nonlocal t_PT_free
            P.wait("tensor", t_src)
            P.wait("tensor", t_PT_free[ps])
            P.wait("tensor", t_consts)
            for j, off in enumerate(blocks):
                P.instr("tensor",
                        lambda E, d=P_T[:, ps * 1024 + j * 128:ps * 1024 + (j + 1) * 128],
                        s=src_sb[:, off:off + 128]:
                        E.transpose(d, s, c_id[:]))
            t_tr = P.mark("tensor", S["sP"])
            P.wait("vector", t_tr)
            nb = len(blocks)
            P.instr("vector",
                    lambda E, d=dst_view,
                    s=P_T[:, ps * 1024:ps * 1024 + nb * 128].rearrange("p (k t) -> p k t", t=128):
                    E.tensor_copy(d, s))
            t_cp = P.mark("vector", S["sV"])
            t_PT_free[ps] = t_cp
            return t_cp

        def vsync():
            t = P.mark("vector", S["sV"])
            P.wait("vector", t)
            return t

        def barrier_all():
            for eng in Prog.ENGINES:
                for sem in S.values():
                    if sem.v:
                        P.wait(eng, (sem, sem.v))

        n_out = 0
        nonlocal_n_out = [0]

        cast_jobs = []
        for src, dst, nrows, rchunk in ((wo_in, wo16, D, 256), (wd_in, wd16, II, 1024),
                                        (wg_in, wg16, D, 256), (wu_in, wu16, D, 256)):
            for r0 in range(0, nrows, rchunk):
                cast_jobs.append((src, dst, r0, r0 + rchunk))

        def emit_casts(njobs):
            t = None
            for _ in range(njobs):
                if not cast_jobs:
                    break
                src, dst, r0, r1 = cast_jobs.pop(0)
                P.instr("gpsimd",
                        lambda E, s2=src[r0:r1, :], d=dst[r0:r1, :]: E.dma_start(out=d, in_=s2))
                t = P.mark("gpsimd", S["dCA"], 16)
            return t

        t_cast_all = None
        seqs = DBG_SEQS if DBG_SEQS is not None else list(range(BLOC))
        for seq in seqs:
            barrier_all()
            with contextlib.ExitStack() as esAB:
                sb = lambda name, shape, dt: esAB.enter_context(
                    nc.sbuf_tensor(f"{name}_{seq}", shape, dt))
                xt = [sb(f"xt{i}", [128, D], F32) for i in range(2)]
                xb = [sb(f"xb{i}", [128, D], BF16) for i in range(2)]
                xfm = sb("xfm", [128, NK * T], BF16)
                wsl = {w: sb(f"w{w}0", [128, NK * 128], BF16) for w in "qkv"}
                qfm = sb("qfm", [128, T], BF16)
                kfm = sb("kfm", [128, T], BF16)
                vfm = sb("vfm", [128, T], BF16)
                vtm = sb("vtm", [128, NK * 128], BF16)
                cs_c = sb("cs", [128, 16], F32)
                suf_c = sb("suf", [128, 16], F32)
                suf_b = sb("sufb", [128, 16], BF16)
                sufft = sb("sufft", [1, NQ * 128], BF16)
                alpha = sb("alpha", [1, NQ * 128], BF16)
                e_pack = sb("epack", [128, ATT_W], BF16)
                zdm = sb("zdm", [128, 256], F32)
                rs_all = sb("rsall", [128, 96], F32)
                rsum_all = sb("rsumall", [128, 16], F32)
                rden_all = sb("rdenall", [128, 16], F32)
                rdb_all = sb("rdball", [128, 16], BF16)
                attnT = sb("attnT", [128, ATT_W], BF16)
                ofm = sb("ofm", [128, T], BF16)

                t_xt_free = [None, None]
                t_xb_free = [None, None]
                t_ofm_free = None
                t_epack_free = None

                for w, wvv in (("q", wq_v), ("k", wk_v), ("v", wv_v)):
                    P.instr("gpsimd",
                            lambda E, d=wsl[w], s=wvv[:, :, 0, :]:
                            E.dma_start(out=d[:].rearrange("p (k j) -> p k j", j=128), in_=s))
                    P.mark("gpsimd", S["dW"], 16)

                xfm_r = xfm[:].rearrange("p (k t) -> p k t", t=T)
                t_xfm = None
                for i in range(NQ):
                    sl = i % 2
                    P.wait("sync", t_xt_free[sl])
                    P.instr("sync",
                            lambda E, d=xt[sl], s=x_in[seq, i * 128:(i + 1) * 128, :]:
                            E.dma_start(out=d[:], in_=s))
                    t_ld = P.mark("sync", S["dX"], 16)
                    P.wait("vector", t_ld)
                    P.wait("vector", t_xb_free[sl])
                    P.instr("vector", lambda E, d=xb[sl], s=xt[sl]: E.tensor_copy(d[:], s[:]))
                    t_cast = P.mark("vector", S["sV"])
                    t_xt_free[sl] = t_cast
                    for g in range(4):
                        t_xfm = transpose_group_strided(
                            xb[sl], [(4 * g + j) * 128 for j in range(4)],
                            xfm_r[:, 4 * g:4 * g + 4, i * 128:(i + 1) * 128],
                            g % 2, t_cast)
                    t_xb_free[sl] = t_xfm

                t_cs_prev = None
                t_head = {}

                def fn_proj(h):
                    nonlocal t_PA_free, t_PC_free, t_cs_prev
                    t_w_ready = (S["dW"], 16 * 3 * (h + 1))

                    def proj(dst_sb, wbuf, pt, t_pfree, extra_evac_wait=None):
                        t_evac = None
                        for half in range(2):
                            P.wait("tensor", t_w_ready)
                            P.wait("tensor", t_xfm)
                            P.wait("tensor", t_pfree if half == 0 else t_evac)
                            for k in range(NK):
                                for s2 in range(2):
                                    P.instr("tensor",
                                            lambda E, o=pt[:, s2 * 512:(s2 + 1) * 512],
                                            l=wbuf[:, k * 128:(k + 1) * 128],
                                            r=xfm[:, k * T + half * 1024 + s2 * 512:
                                                  k * T + half * 1024 + (s2 + 1) * 512],
                                            st=(k == 0), sp=(k == NK - 1):
                                            E.matmul(o, l, r, start=st, stop=sp))
                            t_mm = P.mark("tensor", S["sP"])
                            P.wait("scalar", t_mm)
                            if extra_evac_wait is not None:
                                P.wait("scalar", extra_evac_wait)
                            P.instr("scalar",
                                    lambda E, d=dst_sb[:, half * 1024:(half + 1) * 1024], s=pt:
                                    E.copy(out=d, in_=s[:]))
                            t_evac = P.mark("scalar", S["sS"])
                        return t_evac

                    t_q = proj(qfm, wsl["q"], P_C, t_PC_free)
                    t_k = proj(kfm, wsl["k"], P_A, t_PA_free)
                    t_v = proj(vfm, wsl["v"], P_C, t_q, extra_evac_wait=t_cs_prev)
                    t_PA_free = t_k
                    t_PC_free = t_v
                    t_head[h] = {"q": t_q, "k": t_k, "v": t_v}
                    if h + 1 < DBG_HEADS:
                        for w, wvv, tk in (("q", wq_v, t_q), ("k", wk_v, t_k), ("v", wv_v, t_v)):
                            P.wait("gpsimd", tk)
                            P.instr("gpsimd",
                                    lambda E, d=wsl[w], s=wvv[:, :, h + 1, :]:
                                    E.dma_start(out=d[:].rearrange("p (k j) -> p k j", j=128), in_=s))
                            P.mark("gpsimd", S["dW"], 16)

                def fn_vprep(h):
                    nonlocal t_cs_prev
                    t_v = t_head[h]["v"]
                    t_vtm = None
                    for g in range(4):
                        t_vtm = transpose_group(vfm, g * 512, 4, vtm, g * 512, g % 2, t_v)
                    t_head[h]["vtm"] = t_vtm
                    P.wait("vector", t_v)
                    t_cs = None
                    for j in range(NK):
                        P.instr("vector",
                                lambda E, d=cs_c[:, j:j + 1], s=vfm[:, j * 128:(j + 1) * 128]:
                                E.reduce_sum(d, s, axis=AX.X))
                        t_cs = P.mark("vector", S["sV"])
                    P.wait("vector", t_cs)
                    for q in range(NQ - 1):
                        P.instr("vector",
                                lambda E, d=suf_c[:, q:q + 1], a=cs_c[:, q + 1:16]:
                                E.reduce_sum(d, a, axis=AX.X))
                    vsync()
                    P.instr("vector", lambda E: E.tensor_copy(suf_b[:], suf_c[:]))
                    t_suf = P.mark("vector", S["sV"])
                    t_cs_prev = t_suf
                    for q in range(NQ - 1):
                        ps = q % 2
                        P.wait("tensor", t_suf)
                        P.wait("tensor", t_PT_free[ps])
                        P.instr("tensor",
                                lambda E, d=P_T[0:1, ps * 1024 + 512:ps * 1024 + 640],
                                s=suf_b[:, q:q + 1]:
                                E.transpose(d, s, c_id[:]))
                        t_tr = P.mark("tensor", S["sP"])
                        P.wait("vector", t_tr)
                        P.instr("vector",
                                lambda E, d=sufft[0:1, q * 128:(q + 1) * 128],
                                s=P_T[0:1, ps * 1024 + 512:ps * 1024 + 640]:
                                E.tensor_copy(d, s))
                        t_PT_free[ps] = P.mark("vector", S["sV"])

                def fn_S1(h):
                    nonlocal t_PA_free
                    t_q = t_head[h]["q"]; t_k = t_head[h]["k"]
                    t_PA_slot = [t_PA_free, t_PA_free]
                    t_zde_q = [None] * NQ
                    t_exps_all = []
                    for q in range(NQ):
                        par = q % 2
                        seg_bounds = []
                        c0 = 0
                        while c0 < q * 128:
                            c1 = min(c0 + 512, q * 128)
                            seg_bounds.append((c0, c1))
                            c0 = c1
                        used = 0
                        nseg = len(seg_bounds)
                        for si in range(nseg + 1):
                            ps = si % 2
                            P.wait("tensor", t_q)
                            P.wait("tensor", t_k)
                            P.wait("tensor", t_PA_slot[ps])
                            if si < nseg:
                                c0, c1 = seg_bounds[si]
                                P.instr("tensor",
                                        lambda E, o=P_A[:, ps * 512:ps * 512 + (c1 - c0)],
                                        l=qfm[:, q * 128:(q + 1) * 128], r=kfm[:, c0:c1]:
                                        E.matmul(o, l, r, start=True, stop=True))
                                t_z = P.mark("tensor", S["sP"])
                                P.wait("scalar", t_z)
                                P.wait("scalar", t_epack_free)
                                P.instr("scalar",
                                        lambda E, d=e_pack[:, QOFF[q] + c0:QOFF[q] + c1],
                                        s=P_A[:, ps * 512:ps * 512 + (c1 - c0)],
                                        a=rs_all[:, q * 6 + used:q * 6 + used + 1]:
                                        E.activation(d, s, AF.Exp, scale=SCALE, accum_out=a))
                                t_e = P.mark("scalar", S["sS"])
                                t_PA_slot[ps] = t_e
                            else:
                                P.instr("tensor",
                                        lambda E, o=P_A[:, ps * 512:ps * 512 + 128],
                                        l=qfm[:, q * 128:(q + 1) * 128],
                                        r=kfm[:, q * 128:(q + 1) * 128]:
                                        E.matmul(o, l, r, start=True, stop=True))
                                t_z = P.mark("tensor", S["sP"])
                                P.wait("vector", t_z)
                                P.wait("vector", t_consts)
                                if q >= 2:
                                    P.wait("vector", t_zde_q[q - 2])
                                P.instr("vector",
                                        lambda E, d=zdm[:, par * 128:(par + 1) * 128],
                                        s=P_A[:, ps * 512:ps * 512 + 128]:
                                        E.tensor_tensor(d, s, c_tril[:], ALU.mult))
                                t_zd = P.mark("vector", S["sV"])
                                t_PA_slot[ps] = t_zd
                                P.wait("scalar", t_zd)
                                P.wait("scalar", t_epack_free)
                                P.instr("scalar",
                                        lambda E, d=e_pack[:, QOFF[q] + q * 128:QOFF[q] + (q + 1) * 128],
                                        sz=zdm[:, par * 128:(par + 1) * 128],
                                        a=rs_all[:, q * 6 + used:q * 6 + used + 1]:
                                        E.activation(d, sz, AF.Exp, scale=1.0, accum_out=a))
                                t_e = P.mark("scalar", S["sS"])
                                t_zde_q[q] = t_e
                            used += 1
                        t_exps_all.append((t_e, used))
                    t_PA_free = [t_PA_slot[0], t_PA_slot[1]]
                    t_head[h]["exps"] = t_exps_all

                def fn_S2S3AV(h):
                    nonlocal t_epack_free, t_PB_free, t_ofm_free
                    t_exps_all = t_head[h]["exps"]
                    t_vtm = t_head[h]["vtm"]
                    t_alpha_done = [None] * NQ
                    for q in range(NQ):
                        P.wait("vector", t_exps_all[q][0])
                    for q in range(NQ):
                        P.instr("vector",
                                lambda E, q=q, u=t_exps_all[q][1]:
                                E.reduce_sum(rsum_all[:, q:q + 1],
                                             rs_all[:, q * 6:q * 6 + u], axis=AX.X))
                    vsync()
                    P.instr("vector",
                            lambda E: E.tensor_tensor(rsum_all[:], rsum_all[:], c_cnt2[:], ALU.add))
                    vsync()
                    P.instr("vector", lambda E: E.reciprocal(rden_all[:], rsum_all[:]))
                    vsync()
                    P.instr("vector", lambda E: E.tensor_copy(rdb_all[:], rden_all[:]))
                    t_rdb = P.mark("vector", S["sV"])
                    t_evO = [None, None]
                    for q in range(NQ):
                        psr = q % 2
                        P.wait("tensor", t_rdb)
                        P.wait("tensor", t_PT_free[psr])
                        P.instr("tensor",
                                lambda E, d=P_T[0:1, psr * 1024 + 512:psr * 1024 + 640],
                                s=rdb_all[:, q:q + 1]:
                                E.transpose(d, s, c_id[:]))
                        t_tra = P.mark("tensor", S["sP"])
                        P.wait("vector", t_tra)
                        P.instr("vector",
                                lambda E, d=alpha[0:1, q * 128:(q + 1) * 128],
                                s=P_T[0:1, psr * 1024 + 512:psr * 1024 + 640]:
                                E.tensor_copy(d, s))
                        t_PT_free[psr] = P.mark("vector", S["sV"])
                        t_alpha_done[q] = t_PT_free[psr]
                        P.wait("vector", t_rdb)
                        P.instr("vector",
                                lambda E, q=q, w=(q + 1) * 128:
                                E.tensor_scalar(e_pack[:, QOFF[q]:QOFF[q] + w],
                                                e_pack[:, QOFF[q]:QOFF[q] + w],
                                                rden_all[:, q:q + 1], None, ALU.mult))
                        t_nrm = P.mark("vector", S["sV"])
                        t_last = None
                        for g in range((q + 1 + 3) // 4):
                            nj = min(4, q + 1 - 4 * g)
                            t_last = transpose_group(e_pack, QOFF[q] + g * 512, nj, attnT,
                                                     QOFF[q] + g * 512, g % 2, t_nrm)
                        qb = (q // 8) * 8
                        P.wait("tensor", t_vtm)
                        P.wait("tensor", t_last)
                        P.wait("tensor", t_alpha_done[q])
                        if q % 8 == 0:
                            P.wait("tensor", t_PB_free)
                        for j in range(q + 1):
                            P.instr("tensor",
                                    lambda E, o=P_B[:, (q - qb) * 128:(q - qb + 1) * 128],
                                    l=vtm[:, j * 128:(j + 1) * 128],
                                    r=attnT[:, QOFF[q] + j * 128:QOFF[q] + (j + 1) * 128],
                                    st=(j == 0 and q % 4 == 0), sp=False:
                                    E.matmul(o, l, r, start=st, stop=False))
                        if q != NQ - 1:
                            P.instr("tensor",
                                    lambda E, o=P_B[:, (q - qb) * 128:(q - qb + 1) * 128],
                                    l=sufft[0:1, q * 128:(q + 1) * 128],
                                    r=alpha[0:1, q * 128:(q + 1) * 128]:
                                    E.matmul(o, l, r, start=False, stop=(q % 8 == 7)))
                        t_av = P.mark("tensor", S["sP"])
                        if q % 8 == 7:
                            th = q // 8
                            P.wait("scalar", t_av)
                            P.instr("scalar",
                                    lambda E, d=ofm[:, th * 1024:(th + 1) * 1024]:
                                    E.copy(out=d, in_=P_B[:]))
                            t_evO[th] = P.mark("scalar", S["sS"])
                            t_PB_free = t_evO[th]
                    t_epack_free = (S["sV"], S["sV"].v)
                    P.wait("sync", t_evO[0])
                    P.wait("sync", t_evO[1])
                    P.wait("sync", t_ofm_free)
                    P.wait("sync", t_oscr_consumed)
                    P.instr("sync",
                            lambda E, d=o_scr[h, :, :], s=ofm: E.dma_start(out=d, in_=s[:]))
                    t_ofm_free = P.mark("sync", S["dSO"], 16)

                if "A" in DBG_PHASES and DBG_HEADS > 0:
                    fn_proj(0)
                    emit_casts(2)
                    fn_vprep(0)
                    for h in range(DBG_HEADS):
                        fn_S1(h)
                        if h + 1 < DBG_HEADS:
                            fn_proj(h + 1)
                        emit_casts(2)
                        fn_S2S3AV(h)
                        if h + 1 < DBG_HEADS:
                            fn_vprep(h + 1)
                    emit_casts(len(cast_jobs))
                    t_cast_all = (S["dCA"], S["dCA"].v)

                t_oscr_stored = (S["dSO"], S["dSO"].v)
                P.flush(block)

            if "C" not in DBG_PHASES:
                continue
            with contextlib.ExitStack() as esCD:
                sb2 = lambda name, shape, dt: esCD.enter_context(
                    nc.sbuf_tensor(f"{name}_{seq}", shape, dt))
                xnfm = sb2("xnfm", [128, NK * 8 * 128], BF16)
                accb = sb2("acc", [128, 8 * D], F32)

                for half in range(2):
                    barrier_all()
                    with contextlib.ExitStack() as esC:
                        sb3 = lambda name, shape, dt: esC.enter_context(
                            nc.sbuf_tensor(f"{name}_{seq}_{half}", shape, dt))
                        wo_sb = sb3("wo_sb", [128, NK * 2048], BF16)
                        osl = [sb3("osl0", [128, NK * 128], BF16),
                               sb3("osl1", [128, NK * 128], BF16)]
                        xct = [sb3(f"xct{i}", [128, D], F32) for i in range(2)]
                        r_b = sb3("r_b", [128, D], F32)
                        xn_b = sb3("xn_b", [128, D], F32)
                        xnbb = sb3("xnbb", [128, D], BF16)
                        rsm = sb3("rsm", [128, 2], F32)
                        sqm = sb3("sqm", [128, 2], F32)
                        ln1 = sb3("ln1", [128, 1], F32)
                        ln2 = sb3("ln2", [128, 1], F32)
                        ln3 = sb3("ln3", [128, 1], F32)

                        P.wait("gpsimd", t_oscr_stored)
                        P.wait("gpsimd", t_cast_all)
                        wo_r = wo_sb[:].rearrange("p (k n) -> p k n", n=2048)
                        P.instr("gpsimd",
                                lambda E: E.dma_start(out=wo_r[:, :, 0:1024], in_=wo_v[:, :, 0:1024]))
                        t_wo0 = P.mark("gpsimd", S["dWO"], 16)
                        P.instr("gpsimd",
                                lambda E: E.dma_start(out=wo_r[:, :, 1024:2048], in_=wo_v[:, :, 1024:2048]))
                        t_wo1 = P.mark("gpsimd", S["dWO"], 16)
                        t_wo_h = [t_wo0, t_wo1]

                        t_osl_free = [None, None]
                        t_os_t = [None, None]
                        t_xct_free = [None, None]
                        t_sq_prev = None
                        P.wait("sync", t_oscr_stored)
                        P.instr("sync",
                                lambda E, d=osl[0][:].rearrange("p (k j) -> p k j", j=128),
                                s=o_scr_v[:, :, half * 1024:half * 1024 + 128]:
                                E.dma_start(out=d, in_=s))
                        t_os_t[0] = P.mark("sync", S["dOS"], 16)
                        for ts in range(8):
                            tg = half * 8 + ts
                            osb = osl[ts % 2]
                            t_os = t_os_t[ts % 2]
                            if ts + 1 < 8:
                                P.wait("sync", t_osl_free[(ts + 1) % 2])
                                P.instr("sync",
                                        lambda E, d=osl[(ts + 1) % 2][:].rearrange("p (k j) -> p k j", j=128),
                                        s=o_scr_v[:, :, (tg + 1) * 128:(tg + 2) * 128]:
                                        E.dma_start(out=d, in_=s))
                                t_os_t[(ts + 1) % 2] = P.mark("sync", S["dOS"], 16)
                            sl = ts % 2
                            P.wait("sync", t_xct_free[sl])
                            P.instr("sync",
                                    lambda E, d=xct[sl], s=x_in[seq, tg * 128:(tg + 1) * 128, :]:
                                    E.dma_start(out=d[:], in_=s))
                            t_xl = P.mark("sync", S["dX"], 16)
                            t_r = [None, None]
                            for ddh in range(2):
                                pt = P_A if ddh == 0 else P_C
                                P.wait("tensor", t_wo_h[ddh])
                                P.wait("tensor", t_os)
                                P.wait("tensor", t_PA_free if ddh == 0 else t_PC_free)
                                for k in range(NK):
                                    for s2 in range(2):
                                        P.instr("tensor",
                                                lambda E, o=pt[:, s2 * 512:(s2 + 1) * 512],
                                                l=osb[:, k * 128:(k + 1) * 128],
                                                r=wo_sb[:, k * 2048 + ddh * 1024 + s2 * 512:
                                                      k * 2048 + ddh * 1024 + (s2 + 1) * 512],
                                                st=(k == 0), sp=(k == NK - 1):
                                                E.matmul(o, l, r, start=st, stop=sp))
                                t_mm = P.mark("tensor", S["sP"])
                                P.wait("vector", t_mm)
                                P.wait("vector", t_xl)
                                P.wait("vector", t_sq_prev)
                                P.instr("vector",
                                        lambda E, d=r_b[:, ddh * 1024:(ddh + 1) * 1024],
                                        x2=xct[sl][:, ddh * 1024:(ddh + 1) * 1024],
                                        a=rsm[:, ddh:ddh + 1], sp2=pt:
                                        E.scalar_tensor_tensor(d, sp2[:], 1.0, x2,
                                                               ALU.mult, ALU.add, accum_out=a))
                                t_r[ddh] = P.mark("vector", S["sV"])
                                if ddh == 0:
                                    t_PA_free = t_r[0]
                                else:
                                    t_PC_free = t_r[1]
                            t_osl_free[ts % 2] = t_r[1]
                            t_xct_free[sl] = t_r[1]
                            for ddh in range(2):
                                P.wait("scalar", t_r[ddh])
                                P.instr("scalar",
                                        lambda E, d=xn_b[:, ddh * 1024:(ddh + 1) * 1024],
                                        s=r_b[:, ddh * 1024:(ddh + 1) * 1024],
                                        a=sqm[:, ddh:ddh + 1]:
                                        E.activation(d, s, AF.Square, accum_out=a))
                            t_sq = P.mark("scalar", S["sS"])
                            t_sq_prev = t_sq
                            P.wait("vector", t_sq)
                            P.instr("vector",
                                    lambda E: E.tensor_tensor(ln1[:], rsm[:, 0:1], rsm[:, 1:2], ALU.add))
                            vsync()
                            P.instr("vector",
                                    lambda E: E.tensor_scalar(ln1[:], ln1[:], 1.0 / D, None, ALU.mult))
                            P.instr("vector",
                                    lambda E: E.tensor_tensor(ln2[:], sqm[:, 0:1], sqm[:, 1:2], ALU.add))
                            vsync()
                            P.instr("vector",
                                    lambda E: E.tensor_scalar(ln2[:], ln2[:], 1.0 / D, None, ALU.mult))
                            vsync()
                            P.instr("vector",
                                    lambda E: E.tensor_tensor(ln3[:], ln1[:], ln1[:], ALU.mult))
                            vsync()
                            P.instr("vector",
                                    lambda E: E.tensor_tensor(ln2[:], ln2[:], ln3[:], ALU.subtract))
                            t_var = P.mark("vector", S["sV"])
                            P.wait("scalar", t_var)
                            P.instr("scalar",
                                    lambda E: E.activation(ln2[:], ln2[:], AF.Sqrt,
                                                           bias=c_eps[:], scale=1.0))
                            t_sqr = P.mark("scalar", S["sS"])
                            P.wait("vector", t_sqr)
                            P.instr("vector", lambda E: E.reciprocal(ln3[:], ln2[:]))
                            vsync()
                            P.instr("vector",
                                    lambda E: E.tensor_scalar(xn_b[:], r_b[:], ln1[:], None,
                                                              ALU.subtract))
                            P.instr("vector",
                                    lambda E: E.tensor_scalar(r_b[:], xn_b[:], ln3[:], None,
                                                              ALU.mult))
                            P.instr("vector", lambda E: E.tensor_copy(xnbb[:], r_b[:]))
                            t_xnb = P.mark("vector", S["sV"])
                            P.wait("sync", t_xnb)
                            P.wait("sync", t_xn_consumed)
                            P.instr("sync",
                                    lambda E, d=xn_scr[ts, :, :]: E.dma_start(out=d, in_=r_b[:]))
                            t_xn_stored[ts] = P.mark("sync", S["dSX"], 16)
                            xnfm_r = xnfm[:].rearrange("p (k t) -> p k t", t=8 * 128)
                            for g in range(4):
                                transpose_group_strided(
                                    xnbb, [(4 * g + j) * 128 for j in range(4)],
                                    xnfm_r[:, 4 * g:4 * g + 4, ts * 128:(ts + 1) * 128],
                                    g % 2, t_xnb)
                        t_xnfm = t_PT_free[1] if t_PT_free[1] else t_PT_free[0]
                        t_xnfm = (S["sV"], S["sV"].v)
                        P.flush(block)

                    if "D" not in DBG_PHASES:
                        continue
                    barrier_all()
                    with contextlib.ExitStack() as esD:
                        sb4 = lambda name, shape, dt: esD.enter_context(
                            nc.sbuf_tensor(f"{name}_{seq}_{half}", shape, dt))
                        wgb = [sb4(f"wgb{i}", [128, NK * 512], BF16) for i in range(2)]
                        wub = [sb4(f"wub{i}", [128, NK * 512], BF16) for i in range(2)]
                        wdb = sb4("wdb", [128, 4 * 2048], BF16)
                        g_b = sb4("g_b", [128, 512], F32)
                        h_b = sb4("h_b", [128, 512], F32)
                        a_b = sb4("a_b", [128, 512], F32)
                        tA = sb4("tA", [128, 128], F32)
                        tB = sb4("tB", [128, 128], F32)
                        tC2 = sb4("tC2", [128, 128], F32)
                        tD2 = sb4("tD2", [128, 128], F32)
                        sec = sb4("sec", [128, 128], F32)
                        msk = sb4("msk", [128, 512], BF16)
                        hs_b = sb4("hs_b", [128, 512], BF16)
                        hsT = sb4("hsT", [128, 512], BF16)
                        xtf = [sb4("xtf0", [128, D], F32)]
                        outb = [sb4("outb0", [128, D], F32)]
                        r2sm = sb4("r2sm", [128, 1], F32)
                        q2sm = sb4("q2sm", [128, 1], F32)
                        l21 = sb4("l21", [128, 1], F32)
                        l22 = sb4("l22", [128, 1], F32)
                        l23 = sb4("l23", [128, 1], F32)

                        a_g = a_b[:].rearrange("p (g four) -> p g four", four=4)
                        h_g = h_b[:].rearrange("p (g four) -> p g four", four=4)
                        m_g = msk[:].rearrange("p (g four) -> p g four", four=4)

                        t_wgb_ready = [None, None]

                        def load_wgrp(ig, buf_i, t_free_g, t_free_u):
                            P.wait("gpsimd", t_free_g)
                            P.instr("gpsimd",
                                    lambda E, d=wgb[buf_i][:].rearrange("p (k n) -> p k n", n=512),
                                    s=wg_v[:, :, ig * 512:(ig + 1) * 512]:
                                    E.dma_start(out=d, in_=s))
                            P.mark("gpsimd", S["dWF"], 16)
                            P.wait("gpsimd", t_free_u)
                            P.instr("gpsimd",
                                    lambda E, d=wub[buf_i][:].rearrange("p (k n) -> p k n", n=512),
                                    s=wu_v[:, :, ig * 512:(ig + 1) * 512]:
                                    E.dma_start(out=d, in_=s))
                            t_wgb_ready[buf_i] = P.mark("gpsimd", S["dWF"], 16)
                            return t_wgb_ready[buf_i]

                        t_wg_free = [None, None]
                        load_wgrp(0, 0, None, None)
                        t_wd_free = None
                        t_wd = None
                        t_msk_prev = None
                        t_hs_prev = None
                        t_hs_free = None
                        t_hsT_free = None
                        t_acc_done = [None] * 8
                        t_psum_free = [t_PB_free, t_PA_free]
                        t_g_free = None
                        t_down_done = None
                        NU = NIG * 8

                        def emitA(u):
                            nonlocal t_wd, t_wd_free
                            ig, ts = u // 8, u % 8
                            bi = ig % 2
                            par = u % 2
                            pt = P_B if par == 0 else P_A
                            P.wait("tensor", t_wgb_ready[bi])
                            P.wait("tensor", t_xnfm)
                            P.wait("tensor", t_psum_free[par])
                            for k in range(NK):
                                P.instr("tensor",
                                        lambda E, o=pt[:, 0:512],
                                        l=xnfm[:, k * 1024 + ts * 128:k * 1024 + (ts + 1) * 128],
                                        r=wgb[bi][:, k * 512:(k + 1) * 512],
                                        st=(k == 0), sp=(k == NK - 1):
                                        E.matmul(o, l, r, start=st, stop=sp))
                                P.instr("tensor",
                                        lambda E, o=pt[:, 512:1024],
                                        l=xnfm[:, k * 1024 + ts * 128:k * 1024 + (ts + 1) * 128],
                                        r=wub[bi][:, k * 512:(k + 1) * 512],
                                        st=(k == 0), sp=(k == NK - 1):
                                        E.matmul(o, l, r, start=st, stop=sp))
                            return P.mark("tensor", S["sP"]), pt

                        def emitRest(u, t_gu, pt):
                            nonlocal t_msk_prev, t_hs_prev, t_hs_free, t_hsT_free
                            nonlocal t_psum_free, t_g_free, t_down_done, t_wd_free
                            ig, ts = u // 8, u % 8
                            par = u % 2
                            P.wait("scalar", t_gu)
                            P.wait("scalar", t_g_free)
                            P.instr("scalar",
                                    lambda E: E.activation(g_b[:], pt[:, 0:512], AF.Silu))
                            t_si = P.mark("scalar", S["sS"])
                            P.wait("vector", t_si)
                            P.instr("vector",
                                    lambda E: E.tensor_tensor(h_b[:], g_b[:], pt[:, 512:1024],
                                                              ALU.mult))
                            t_h = P.mark("vector", S["sV"])
                            t_psum_free[par] = t_h
                            t_g_free = t_h
                            P.wait("scalar", t_h)
                            P.wait("scalar", t_msk_prev)
                            P.instr("scalar",
                                    lambda E: E.activation(a_b[:], h_b[:], AF.Square))
                            t_a = P.mark("scalar", S["sS"])
                            P.wait("vector", t_a)
                            P.instr("vector",
                                    lambda E: E.tensor_tensor(tA[:], a_g[:, :, 0], a_g[:, :, 1], ALU.max))
                            P.instr("vector",
                                    lambda E: E.tensor_tensor(tB[:], a_g[:, :, 2], a_g[:, :, 3], ALU.max))
                            P.instr("vector",
                                    lambda E: E.tensor_tensor(tC2[:], a_g[:, :, 0], a_g[:, :, 1], ALU.min))
                            P.instr("vector",
                                    lambda E: E.tensor_tensor(tD2[:], a_g[:, :, 2], a_g[:, :, 3], ALU.min))
                            P.instr("vector",
                                    lambda E: E.tensor_tensor(tA[:], tA[:], tB[:], ALU.min))
                            P.instr("vector",
                                    lambda E: E.tensor_tensor(tC2[:], tC2[:], tD2[:], ALU.max))
                            P.instr("vector",
                                    lambda E: E.tensor_tensor(sec[:], tA[:], tC2[:], ALU.max))
                            for j4 in range(4):
                                P.instr("vector",
                                        lambda E, j4=j4:
                                        E.tensor_tensor(m_g[:, :, j4], a_g[:, :, j4], sec[:],
                                                        ALU.is_ge))
                            t_msk = P.mark("vector", S["sV"])
                            t_msk_prev = t_msk
                            P.wait("vector", t_hs_free)
                            P.instr("vector",
                                    lambda E: E.tensor_tensor(hs_b[:], h_b[:], msk[:], ALU.mult))
                            t_hs = P.mark("vector", S["sV"])
                            t_hs_prev = t_hs
                            if DBG_DUMP_D and ig == 0 and ts == 0:
                                P.wait("vector", t_gu)
                                P.instr("vector", lambda E: E.tensor_copy(xtf[0][:, 0:1024], pt[:]))
                                t_dgu = P.mark("vector", S["sV"])
                                for nm, src in [("g", g_b), ("h", h_b), ("a", a_b),
                                                ("sec", sec), ("msk", msk), ("hs", hs_b)]:
                                    P.wait("sync", t_hs)
                                    P.wait("sync", t_dgu)
                                    P.instr("sync",
                                            lambda E, d=dbgd[nm], s2=src:
                                            E.dma_start(out=d, in_=s2[:]))
                                    P.mark("sync", S["dSO"], 16)
                                P.instr("sync", lambda E: E.dma_start(out=dbgd["gu"], in_=xtf[0][:, 0:1024]))
                                t_dgu2 = P.mark("sync", S["dSO"], 16)
                                P.wait("sync", t_dgu2)
                            t_hsT = transpose_group(hs_b, 0, 4, hsT, 0, ig % 2, t_hs)
                            t_hs_free = t_hsT
                            if DBG_DUMP_D and ig == 0 and ts == 0:
                                P.wait("sync", t_hsT)
                                P.instr("sync", lambda E: E.dma_start(out=dbgd["hsT"], in_=hsT[:]))
                                P.mark("sync", S["dSO"], 16)
                            t_ac = None
                            for qt in range(4):
                                bk = qt % 2
                                P.wait("tensor", t_hsT)
                                P.wait("tensor", t_wd)
                                P.wait("tensor", t_PC_free_l[bk])
                                for c in range(4):
                                    P.instr("tensor",
                                            lambda E, o=P_C[:, bk * 512:(bk + 1) * 512],
                                            l=hsT[:, c * 128:(c + 1) * 128],
                                            r=wdb[:, c * 2048 + qt * 512:
                                                  c * 2048 + (qt + 1) * 512],
                                            st=(c == 0), sp=(c == 3):
                                            E.matmul(o, l, r, start=st, stop=sp))
                                t_dn = P.mark("tensor", S["sP"])
                                P.wait("vector", t_dn)
                                if ig == 0:
                                    P.instr("vector",
                                            lambda E, d=accb[:, ts * D + qt * 512:
                                                           ts * D + (qt + 1) * 512],
                                            sp2=P_C[:, bk * 512:(bk + 1) * 512]:
                                            E.tensor_copy(d, sp2))
                                else:
                                    P.instr("vector",
                                            lambda E, d=accb[:, ts * D + qt * 512:
                                                           ts * D + (qt + 1) * 512],
                                            sp2=P_C[:, bk * 512:(bk + 1) * 512]:
                                            E.tensor_tensor(d, d, sp2, ALU.add))
                                t_ac = P.mark("vector", S["sV"])
                                t_PC_free_l[bk] = t_ac
                            if ig == NIG - 1:
                                t_acc_done[ts] = t_ac
                            t_down_done = (S["sP"], S["sP"].v)

                        t_PC_free_l = [t_PC_free, t_PC_free]

                        t_xtf_free = [None]
                        t_ob_free = [None]

                        def emit_final(ts):
                            tg = half * 8 + ts
                            sl = 0
                            P.wait("sync", t_xn_stored[ts])
                            P.wait("sync", t_xtf_free[sl])
                            P.instr("sync",
                                    lambda E, d=xtf[sl], s=xn_scr[ts, :, :]:
                                    E.dma_start(out=d[:], in_=s))
                            t_xnl = P.mark("sync", S["dXN"], 16)
                            P.wait("vector", t_xnl)
                            P.wait("vector", t_acc_done[ts])
                            P.wait("vector", t_ob_free[0])
                            ob = outb[0]
                            P.instr("vector",
                                    lambda E, d=ob, a=accb[:, ts * D:(ts + 1) * D], x2=xtf[sl]:
                                    E.scalar_tensor_tensor(d[:], a, 1.0, x2[:], ALU.mult, ALU.add,
                                                           accum_out=r2sm[:]))
                            t_r2 = P.mark("vector", S["sV"])
                            P.wait("scalar", t_r2)
                            P.instr("scalar",
                                    lambda E, s=ob, d=accb[:, ts * D:(ts + 1) * D]:
                                    E.activation(d, s[:], AF.Square, accum_out=q2sm[:]))
                            t_q2 = P.mark("scalar", S["sS"])
                            P.wait("vector", t_q2)
                            P.instr("vector",
                                    lambda E: E.tensor_scalar(l21[:], r2sm[:], 1.0 / D, None, ALU.mult))
                            P.instr("vector",
                                    lambda E: E.tensor_scalar(l22[:], q2sm[:], 1.0 / D, None, ALU.mult))
                            vsync()
                            P.instr("vector",
                                    lambda E: E.tensor_tensor(l23[:], l21[:], l21[:], ALU.mult))
                            vsync()
                            P.instr("vector",
                                    lambda E: E.tensor_tensor(l22[:], l22[:], l23[:], ALU.subtract))
                            t_v2 = P.mark("vector", S["sV"])
                            P.wait("scalar", t_v2)
                            P.instr("scalar",
                                    lambda E: E.activation(l22[:], l22[:], AF.Sqrt,
                                                           bias=c_eps[:], scale=1.0))
                            t_s2 = P.mark("scalar", S["sS"])
                            P.wait("vector", t_s2)
                            P.instr("vector", lambda E: E.reciprocal(l23[:], l22[:]))
                            vsync()
                            P.instr("vector",
                                    lambda E, d=xtf[sl], s=ob: E.tensor_scalar(d[:], s[:], l21[:],
                                                                               None, ALU.subtract))
                            P.instr("vector",
                                    lambda E, d=ob, s=xtf[sl]: E.tensor_scalar(d[:], s[:], l23[:],
                                                                               None, ALU.mult))
                            t_o = P.mark("vector", S["sV"])
                            t_xtf_free[sl] = t_o
                            P.wait("sync", t_o)
                            P.instr("sync",
                                    lambda E, d=y_out[seq, tg * 128:(tg + 1) * 128, :], s=ob:
                                    E.dma_start(out=d, in_=s[:]))
                            t_ob_free[0] = P.mark("sync", S["dOUT"], 16)
                            nonlocal_n_out[0] += 1

                        prev = None
                        for u in range(NU):
                            ig, ts = u // 8, u % 8
                            cur = emitA(u)
                            if (u == 0) or (ts == 1 and ig >= 1):
                                P.wait("gpsimd", t_wd_free)
                                P.instr("gpsimd",
                                        lambda E, d=wdb[:].rearrange("p (c n) -> p c n", n=2048),
                                        s=wd_v[:, 4 * ig:4 * ig + 4, :]:
                                        E.dma_start(out=d, in_=s))
                                t_wd = P.mark("gpsimd", S["dWF"], 16)
                                if ig + 1 < NIG:
                                    load_wgrp(ig + 1, (ig + 1) % 2,
                                              t_wg_free[(ig + 1) % 2], t_wg_free[(ig + 1) % 2])
                            if prev is not None:
                                emitRest(*prev)
                                if ts == 0:
                                    t_wg_free[(ig + 1) % 2] = prev[1]
                                    t_wd_free = t_down_done
                                if prev[0] // 8 == NIG - 1:
                                    emit_final(prev[0] % 8)
                            prev = (u, cur[0], cur[1])
                        emitRest(*prev)
                        emit_final(prev[0] % 8)
                        t_PC_free = [t_PC_free_l[0], t_PC_free_l[1]]
                        t_PA_free = t_psum_free[1]
                        t_PB_free = t_psum_free[0]

                        if DBG_DUMP_D and half == 0:
                            P.wait("sync", (S["sV"], S["sV"].v))
                            P.instr("sync", lambda E: E.dma_start(out=dbgd["acc"], in_=accb[:]))
                            P.mark("sync", S["dSO"], 16)
                            P.instr("sync", lambda E: E.dma_start(out=dbgd["xnfm"], in_=xnfm[:]))
                            P.mark("sync", S["dSO"], 16)
                        t_xn_consumed = (S["dXN"], S["dXN"].v)
                        P.flush(block)

                t_oscr_consumed = (S["dOS"], S["dOS"].v)

        n_out = nonlocal_n_out[0]
        for nm in ("dOUT", "dSO", "dSX"):
            if S[nm].v:
                P.wait("sync", (S[nm], S[nm].v))
        for nm in ("sP", "sV", "sS", "sG"):
            if S[nm].v:
                P.wait("sync", (S[nm], S[nm].v))
        P.flush(block)

    return nc


_NC_CACHE = None


def _get_nc():
    global _NC_CACHE
    if _NC_CACHE is None:
        _NC_CACHE = build_nc()
    return _NC_CACHE


def kernel(L0_attn_x, L0_attn_Wq, L0_attn_Wk, L0_attn_Wv, L0_attn_Wo,
           L0_ffn_w_gate, L0_ffn_w_up, L0_ffn_w_down):
    x = np.ascontiguousarray(np.asarray(L0_attn_x, dtype=np.float32))
    wq = np.ascontiguousarray(np.asarray(L0_attn_Wq, dtype=np.float32))
    wk = np.ascontiguousarray(np.asarray(L0_attn_Wk, dtype=np.float32))
    wv = np.ascontiguousarray(np.asarray(L0_attn_Wv, dtype=np.float32))
    wo = np.ascontiguousarray(np.asarray(L0_attn_Wo, dtype=np.float32))
    wg = np.ascontiguousarray(np.asarray(L0_ffn_w_gate, dtype=np.float32))
    wu = np.ascontiguousarray(np.asarray(L0_ffn_w_up, dtype=np.float32))
    wd = np.ascontiguousarray(np.asarray(L0_ffn_w_down, dtype=np.float32))

    c_id = np.eye(128, dtype=ml_dtypes.bfloat16)
    c_tril = (np.tril(np.ones((128, 128), dtype=np.float32)) * SCALE).astype(np.float32)
    c_eps = np.full((128, 1), EPS, dtype=np.float32)
    c_cnt2 = np.tile(np.array([float(T - (q + 1) * 128) for q in range(NQ)],
                              dtype=np.float32)[None, :], (128, 1))

    nc = _get_nc()
    in_maps = []
    for c in range(NCORES):
        in_maps.append({
            "x": x[c * BLOC:(c + 1) * BLOC],
            "wq": wq, "wk": wk, "wv": wv, "wo": wo,
            "wg": wg, "wu": wu, "wd": wd,
            "c_id": c_id, "c_tril": c_tril, "c_eps": c_eps, "c_cnt2": c_cnt2,
        })
    res = run_bass_kernel_spmd(nc, in_maps, core_ids=list(range(NCORES)))
    out = np.concatenate([r["y"] for r in res.results], axis=0)
    return out.astype(np.float32)


# version 15
# speedup vs baseline: 1.1125x; 1.0213x over previous
import contextlib
import numpy as np
import ml_dtypes
import concourse.bass as bass
import concourse.mybir as mybir
from concourse.bass_utils import run_bass_kernel_spmd

F32 = mybir.dt.float32
BF16 = mybir.dt.bfloat16
AF = mybir.ActivationFunctionType
ALU = mybir.AluOpType
AX = mybir.AxisListType

NCORES = 8
BLOC = 2
T = 2048
D = 2048
H = 16
II = 8192
NK = D // 128
NQ = T // 128
NC_I = II // 128
NIG = 16
SCALE = float(1.0 / np.sqrt(float(T)))
EPS = 1e-5
QOFF = [128 * (q * (q + 1) // 2) for q in range(NQ + 1)]
ATT_W = QOFF[NQ]

DBG_SEQS = None
DBG_HEADS = H
DBG_PHASES = "TABCD"
DBG_DUMP = False
DBG_DUMP_AB = False
DBG_DUMP_D = False


class Sem:
    __slots__ = ("h", "v")

    def __init__(self, h):
        self.h = h
        self.v = 0


class Prog:
    ENGINES = ("tensor", "vector", "scalar", "gpsimd", "sync")

    def __init__(self):
        self.q = {e: [] for e in self.ENGINES}
        self.lastwait = {e: {} for e in self.ENGINES}
        self.last = {e: None for e in self.ENGINES}

    def instr(self, eng, fn):
        ent = {"fn": fn, "inc": None, "amt": 0}
        self.q[eng].append(ent)
        self.last[eng] = ent

    def mark(self, eng, sem, amt=1):
        ent = self.last[eng]
        assert ent is not None and ent["inc"] is None
        sem.v += amt
        ent["inc"] = sem
        ent["amt"] = amt
        return (sem, sem.v)

    def wait(self, eng, tkt):
        if tkt is None:
            return
        if isinstance(tkt, (list, tuple)) and tkt and isinstance(tkt[0], (list, tuple, type(None))):
            for t in tkt:
                self.wait(eng, t)
            return
        sem, v = tkt
        key = id(sem)
        if self.lastwait[eng].get(key, -1) >= v:
            return
        self.lastwait[eng][key] = v
        ent = {"fn": None, "inc": None, "amt": 0, "wait": (sem.h, v)}
        self.q[eng].append(ent)
        self.last[eng] = None

    def flush(self, block):
        for eng in self.ENGINES:
            ents = self.q[eng]
            if not ents:
                continue
            dec = getattr(block, eng)

            def section(E, ents=ents):
                for ent in ents:
                    if "wait" in ent:
                        s, v = ent["wait"]
                        E.wait_ge(s, v)
                    elif ent["inc"] is not None:
                        ent["fn"](E).then_inc(ent["inc"].h, ent["amt"])
                    else:
                        ent["fn"](E)

            dec(section)
        self.q = {e: [] for e in self.ENGINES}
        self.last = {e: None for e in self.ENGINES}


def build_nc():
    nc = bass.Bass("TRN2", target_bir_lowering=False)

    x_in = nc.dram_tensor("x", [BLOC, T, D], F32, kind="ExternalInput").ap()
    wq_in = nc.dram_tensor("wq", [D, D], F32, kind="ExternalInput").ap()
    wk_in = nc.dram_tensor("wk", [D, D], F32, kind="ExternalInput").ap()
    wv_in = nc.dram_tensor("wv", [D, D], F32, kind="ExternalInput").ap()
    wo_in = nc.dram_tensor("wo", [D, D], F32, kind="ExternalInput").ap()
    wg_in = nc.dram_tensor("wg", [D, II], F32, kind="ExternalInput").ap()
    wu_in = nc.dram_tensor("wu", [D, II], F32, kind="ExternalInput").ap()
    wd_in = nc.dram_tensor("wd", [II, D], F32, kind="ExternalInput").ap()
    c_id_in = nc.dram_tensor("c_id", [128, 128], BF16, kind="ExternalInput").ap()
    c_tril_in = nc.dram_tensor("c_tril", [128, 128], F32, kind="ExternalInput").ap()
    c_eps_in = nc.dram_tensor("c_eps", [128, 1], F32, kind="ExternalInput").ap()
    c_cnt2_in = nc.dram_tensor("c_cnt2", [128, 16], F32, kind="ExternalInput").ap()
    y_out = nc.dram_tensor("y", [BLOC, T, D], F32, kind="ExternalOutput").ap()
    scr_kind = "ExternalOutput" if DBG_DUMP else "Internal"
    o_scr = nc.dram_tensor("o_scr", [H, 128, T], BF16, kind=scr_kind).ap()
    wg16 = nc.dram_tensor("wg16", [D, II], BF16, kind="Internal").ap()
    wu16 = nc.dram_tensor("wu16", [D, II], BF16, kind="Internal").ap()
    wd16 = nc.dram_tensor("wd16", [II, D], BF16, kind="Internal").ap()
    wo16 = nc.dram_tensor("wo16", [D, D], BF16, kind="Internal").ap()
    xn_scr = nc.dram_tensor("xn_scr", [8, 128, D], F32, kind=scr_kind).ap()
    if DBG_DUMP_D:
        dbgd = {nm: nc.dram_tensor(f"dbgd_{nm}", shp, dt, kind="ExternalOutput").ap()
                for nm, shp, dt in [
                    ("xnfm", [128, NK * 8 * 128], BF16), ("acc", [128, 8 * D], F32),
                    ("g", [128, 512], F32), ("h", [128, 512], F32),
                    ("a", [128, 512], F32), ("sec", [128, 128], F32),
                    ("msk", [128, 512], BF16), ("hs", [128, 512], BF16),
                    ("hsT", [128, 512], BF16), ("gu", [128, 1024], F32)]}
    if DBG_DUMP_AB:
        dbg_outs = {nm: nc.dram_tensor(f"dbg_{nm}", shp, dt, kind="ExternalOutput").ap()
                    for nm, shp, dt in [
                        ("qfm", [128, T], BF16), ("kfm", [128, T], BF16),
                        ("vfm", [128, T], BF16), ("vtm", [128, NK * 128], BF16),
                        ("sufft", [1, NQ * 128], BF16), ("alpha", [1, NQ * 128], BF16),
                        ("attn", [128, T], BF16), ("attnT", [128, ATT_W], BF16),
                        ("ebuf", [128, T], F32), ("xfm", [128, NK * T], BF16)]}

    wq_v = wq_in.rearrange("(k p) (h j) -> p k h j", p=128, j=128)
    wk_v = wk_in.rearrange("(k p) (h j) -> p k h j", p=128, j=128)
    wv_v = wv_in.rearrange("(k p) (h j) -> p k h j", p=128, j=128)
    wo_v = wo16.rearrange("(k p) n -> p k n", p=128)
    wg_v = wg16.rearrange("(k p) n -> p k n", p=128)
    wu_v = wu16.rearrange("(k p) n -> p k n", p=128)
    wd_v = wd16.rearrange("(c p) n -> p c n", p=128)
    o_scr_v = o_scr.rearrange("h p t -> p h t")

    es0 = contextlib.ExitStack()
    with es0:
        names = ["dC", "dX", "dW", "dWO", "dOS", "dWF", "dXN",
                 "dSO", "dSX", "dOUT", "dCA", "sP", "sV", "sS", "sG"]
        S = {n: Sem(es0.enter_context(nc.semaphore(n))) for n in names}

        c_id = es0.enter_context(nc.sbuf_tensor("c_id_s", [128, 128], BF16))
        c_tril = es0.enter_context(nc.sbuf_tensor("c_tril_s", [128, 128], F32))
        c_eps = es0.enter_context(nc.sbuf_tensor("c_eps_s", [128, 1], F32))
        c_cnt2 = es0.enter_context(nc.sbuf_tensor("c_cnt2_s", [128, 16], F32))

        P_A = es0.enter_context(nc.psum_tensor("P_A", [128, 1024], F32))
        P_B = es0.enter_context(nc.psum_tensor("P_B", [128, 1024], F32))
        P_C = es0.enter_context(nc.psum_tensor("P_C", [128, 1024], F32))
        P_T = es0.enter_context(nc.psum_tensor("P_T", [128, 2048], BF16))

        block = es0.enter_context(nc.Block())
        P = Prog()

        P.instr("sync", lambda E: E.dma_start(out=c_id[:], in_=c_id_in))
        P.mark("sync", S["dC"], 16)
        P.instr("sync", lambda E: E.dma_start(out=c_tril[:], in_=c_tril_in))
        P.mark("sync", S["dC"], 16)
        P.instr("sync", lambda E: E.dma_start(out=c_eps[:], in_=c_eps_in))
        P.mark("sync", S["dC"], 16)
        P.instr("sync", lambda E: E.dma_start(out=c_cnt2[:], in_=c_cnt2_in))
        t_consts = P.mark("sync", S["dC"], 16)

        t_PA_free = None
        t_PB_free = None
        t_PC_free = None
        t_PT_free = [None, None]
        t_oscr_stored = None
        t_oscr_consumed = None
        t_xn_stored = [None] * 8
        t_xn_consumed = None

        def transpose_group(src_sb, src_off, n_blk, dst_sb, dst_off, ps, t_src, extra_waits=()):
            nonlocal t_PT_free
            P.wait("tensor", t_src)
            for w in extra_waits:
                P.wait("tensor", w)
            P.wait("tensor", t_PT_free[ps])
            P.wait("tensor", t_consts)
            for j in range(n_blk):
                P.instr("tensor",
                        lambda E, d=P_T[:, ps * 1024 + j * 128:ps * 1024 + (j + 1) * 128],
                        s=src_sb[:, src_off + j * 128:src_off + (j + 1) * 128]:
                        E.transpose(d, s, c_id[:]))
            t_tr = P.mark("tensor", S["sP"])
            P.wait("vector", t_tr)
            P.instr("vector",
                    lambda E, d=dst_sb[:, dst_off:dst_off + n_blk * 128],
                    s=P_T[:, ps * 1024:ps * 1024 + n_blk * 128]:
                    E.tensor_copy(d, s))
            t_cp = P.mark("vector", S["sV"])
            t_PT_free[ps] = t_cp
            return t_cp

        def transpose_group_strided(src_sb, blocks, dst_view, ps, t_src):
            nonlocal t_PT_free
            P.wait("tensor", t_src)
            P.wait("tensor", t_PT_free[ps])
            P.wait("tensor", t_consts)
            for j, off in enumerate(blocks):
                P.instr("tensor",
                        lambda E, d=P_T[:, ps * 1024 + j * 128:ps * 1024 + (j + 1) * 128],
                        s=src_sb[:, off:off + 128]:
                        E.transpose(d, s, c_id[:]))
            t_tr = P.mark("tensor", S["sP"])
            P.wait("vector", t_tr)
            nb = len(blocks)
            P.instr("vector",
                    lambda E, d=dst_view,
                    s=P_T[:, ps * 1024:ps * 1024 + nb * 128].rearrange("p (k t) -> p k t", t=128):
                    E.tensor_copy(d, s))
            t_cp = P.mark("vector", S["sV"])
            t_PT_free[ps] = t_cp
            return t_cp

        def vsync():
            t = P.mark("vector", S["sV"])
            P.wait("vector", t)
            return t

        def barrier_all():
            for eng in Prog.ENGINES:
                for sem in S.values():
                    if sem.v:
                        P.wait(eng, (sem, sem.v))

        n_out = 0
        nonlocal_n_out = [0]

        cast_jobs = []
        for src, dst, nrows, rchunk in ((wo_in, wo16, D, 256), (wd_in, wd16, II, 1024),
                                        (wg_in, wg16, D, 256), (wu_in, wu16, D, 256)):
            for r0 in range(0, nrows, rchunk):
                cast_jobs.append((src, dst, r0, r0 + rchunk))

        def emit_casts(njobs):
            t = None
            for _ in range(njobs):
                if not cast_jobs:
                    break
                src, dst, r0, r1 = cast_jobs.pop(0)
                P.instr("gpsimd",
                        lambda E, s2=src[r0:r1, :], d=dst[r0:r1, :]: E.dma_start(out=d, in_=s2))
                t = P.mark("gpsimd", S["dCA"], 16)
            return t

        t_cast_all = None
        seqs = DBG_SEQS if DBG_SEQS is not None else list(range(BLOC))
        for seq in seqs:
            barrier_all()
            with contextlib.ExitStack() as esAB:
                sb = lambda name, shape, dt: esAB.enter_context(
                    nc.sbuf_tensor(f"{name}_{seq}", shape, dt))
                xt = [sb(f"xt{i}", [128, D], F32) for i in range(2)]
                xb = [sb(f"xb{i}", [128, D], BF16) for i in range(2)]
                xfm = sb("xfm", [128, NK * T], BF16)
                wsl = {w: sb(f"w{w}0", [128, NK * 128], BF16) for w in "qkv"}
                qfm = sb("qfm", [128, T], BF16)
                kfm = sb("kfm", [128, T], BF16)
                vfm = sb("vfm", [128, T], BF16)
                vtm = sb("vtm", [128, NK * 128], BF16)
                cs_c = sb("cs", [128, 16], F32)
                suf_c = sb("suf", [128, 16], F32)
                suf_b = sb("sufb", [128, 16], BF16)
                sufft = sb("sufft", [1, NQ * 128], BF16)
                alpha = sb("alpha", [1, NQ * 128], BF16)
                e_pack = sb("epack", [128, ATT_W], BF16)
                zdm = sb("zdm", [128, 256], F32)
                rs_all = sb("rsall", [128, 96], F32)
                rsum_all = sb("rsumall", [128, 16], F32)
                rden_all = sb("rdenall", [128, 16], F32)
                rdb_all = sb("rdball", [128, 16], BF16)
                attnT = sb("attnT", [128, ATT_W], BF16)
                ofm = sb("ofm", [128, T], BF16)

                t_xt_free = [None, None]
                t_xb_free = [None, None]
                t_ofm_free = None
                t_epack_free = None

                for w, wvv in (("q", wq_v), ("k", wk_v), ("v", wv_v)):
                    P.instr("gpsimd",
                            lambda E, d=wsl[w], s=wvv[:, :, 0, :]:
                            E.dma_start(out=d[:].rearrange("p (k j) -> p k j", j=128), in_=s))
                    P.mark("gpsimd", S["dW"], 16)

                xfm_r = xfm[:].rearrange("p (k t) -> p k t", t=T)
                t_xfm = None
                for i in range(NQ):
                    sl = i % 2
                    P.wait("sync", t_xt_free[sl])
                    P.instr("sync",
                            lambda E, d=xt[sl], s=x_in[seq, i * 128:(i + 1) * 128, :]:
                            E.dma_start(out=d[:], in_=s))
                    t_ld = P.mark("sync", S["dX"], 16)
                    P.wait("vector", t_ld)
                    P.wait("vector", t_xb_free[sl])
                    P.instr("vector", lambda E, d=xb[sl], s=xt[sl]: E.tensor_copy(d[:], s[:]))
                    t_cast = P.mark("vector", S["sV"])
                    t_xt_free[sl] = t_cast
                    for g in range(4):
                        t_xfm = transpose_group_strided(
                            xb[sl], [(4 * g + j) * 128 for j in range(4)],
                            xfm_r[:, 4 * g:4 * g + 4, i * 128:(i + 1) * 128],
                            g % 2, t_cast)
                    t_xb_free[sl] = t_xfm

                t_cs_prev = None
                t_head = {}

                def fn_proj(h):
                    nonlocal t_PA_free, t_PC_free, t_cs_prev
                    t_w_ready = (S["dW"], 16 * 3 * (h + 1))

                    def proj(dst_sb, wbuf, pt, t_pfree, extra_evac_wait=None):
                        t_evac = None
                        for half in range(2):
                            P.wait("tensor", t_w_ready)
                            P.wait("tensor", t_xfm)
                            P.wait("tensor", t_pfree if half == 0 else t_evac)
                            for k in range(NK):
                                for s2 in range(2):
                                    P.instr("tensor",
                                            lambda E, o=pt[:, s2 * 512:(s2 + 1) * 512],
                                            l=wbuf[:, k * 128:(k + 1) * 128],
                                            r=xfm[:, k * T + half * 1024 + s2 * 512:
                                                  k * T + half * 1024 + (s2 + 1) * 512],
                                            st=(k == 0), sp=(k == NK - 1):
                                            E.matmul(o, l, r, start=st, stop=sp))
                            t_mm = P.mark("tensor", S["sP"])
                            P.wait("scalar", t_mm)
                            if extra_evac_wait is not None:
                                P.wait("scalar", extra_evac_wait)
                            P.instr("scalar",
                                    lambda E, d=dst_sb[:, half * 1024:(half + 1) * 1024], s=pt:
                                    E.copy(out=d, in_=s[:]))
                            t_evac = P.mark("scalar", S["sS"])
                        return t_evac

                    t_q = proj(qfm, wsl["q"], P_C, t_PC_free)
                    t_k = proj(kfm, wsl["k"], P_A, t_PA_free)
                    t_v = proj(vfm, wsl["v"], P_C, t_q, extra_evac_wait=t_cs_prev)
                    t_PA_free = t_k
                    t_PC_free = t_v
                    t_head[h] = {"q": t_q, "k": t_k, "v": t_v}
                    if h + 1 < DBG_HEADS:
                        for w, wvv, tk in (("q", wq_v, t_q), ("k", wk_v, t_k), ("v", wv_v, t_v)):
                            P.wait("gpsimd", tk)
                            P.instr("gpsimd",
                                    lambda E, d=wsl[w], s=wvv[:, :, h + 1, :]:
                                    E.dma_start(out=d[:].rearrange("p (k j) -> p k j", j=128), in_=s))
                            P.mark("gpsimd", S["dW"], 16)

                def fn_vprep(h):
                    nonlocal t_cs_prev
                    t_v = t_head[h]["v"]
                    t_vtm = None
                    for g in range(4):
                        t_vtm = transpose_group(vfm, g * 512, 4, vtm, g * 512, g % 2, t_v)
                    t_head[h]["vtm"] = t_vtm
                    P.wait("vector", t_v)
                    t_cs = None
                    for j in range(NK):
                        P.instr("vector",
                                lambda E, d=cs_c[:, j:j + 1], s=vfm[:, j * 128:(j + 1) * 128]:
                                E.reduce_sum(d, s, axis=AX.X))
                        t_cs = P.mark("vector", S["sV"])
                    P.wait("vector", t_cs)
                    for q in range(NQ - 1):
                        P.instr("vector",
                                lambda E, d=suf_c[:, q:q + 1], a=cs_c[:, q + 1:16]:
                                E.reduce_sum(d, a, axis=AX.X))
                    vsync()
                    P.instr("vector", lambda E: E.tensor_copy(suf_b[:], suf_c[:]))
                    t_suf = P.mark("vector", S["sV"])
                    t_cs_prev = t_suf
                    for q in range(NQ - 1):
                        ps = q % 2
                        P.wait("tensor", t_suf)
                        P.wait("tensor", t_PT_free[ps])
                        P.instr("tensor",
                                lambda E, d=P_T[0:1, ps * 1024 + 512:ps * 1024 + 640],
                                s=suf_b[:, q:q + 1]:
                                E.transpose(d, s, c_id[:]))
                        t_tr = P.mark("tensor", S["sP"])
                        P.wait("vector", t_tr)
                        P.instr("vector",
                                lambda E, d=sufft[0:1, q * 128:(q + 1) * 128],
                                s=P_T[0:1, ps * 1024 + 512:ps * 1024 + 640]:
                                E.tensor_copy(d, s))
                        t_PT_free[ps] = P.mark("vector", S["sV"])

                def fn_S1(h):
                    nonlocal t_PA_free
                    t_q = t_head[h]["q"]; t_k = t_head[h]["k"]
                    t_PA_slot = [t_PA_free, t_PA_free]
                    t_zde_q = [None] * NQ
                    t_exps_all = []
                    for q in range(NQ):
                        par = q % 2
                        seg_bounds = []
                        c0 = 0
                        while c0 < q * 128:
                            c1 = min(c0 + 512, q * 128)
                            seg_bounds.append((c0, c1))
                            c0 = c1
                        used = 0
                        nseg = len(seg_bounds)
                        for si in range(nseg + 1):
                            ps = si % 2
                            P.wait("tensor", t_q)
                            P.wait("tensor", t_k)
                            P.wait("tensor", t_PA_slot[ps])
                            if si < nseg:
                                c0, c1 = seg_bounds[si]
                                P.instr("tensor",
                                        lambda E, o=P_A[:, ps * 512:ps * 512 + (c1 - c0)],
                                        l=qfm[:, q * 128:(q + 1) * 128], r=kfm[:, c0:c1]:
                                        E.matmul(o, l, r, start=True, stop=True))
                                t_z = P.mark("tensor", S["sP"])
                                P.wait("scalar", t_z)
                                P.wait("scalar", t_epack_free)
                                P.instr("scalar",
                                        lambda E, d=e_pack[:, QOFF[q] + c0:QOFF[q] + c1],
                                        s=P_A[:, ps * 512:ps * 512 + (c1 - c0)],
                                        a=rs_all[:, q * 6 + used:q * 6 + used + 1]:
                                        E.activation(d, s, AF.Exp, scale=SCALE, accum_out=a))
                                t_e = P.mark("scalar", S["sS"])
                                t_PA_slot[ps] = t_e
                            else:
                                P.instr("tensor",
                                        lambda E, o=P_A[:, ps * 512:ps * 512 + 128],
                                        l=qfm[:, q * 128:(q + 1) * 128],
                                        r=kfm[:, q * 128:(q + 1) * 128]:
                                        E.matmul(o, l, r, start=True, stop=True))
                                t_z = P.mark("tensor", S["sP"])
                                P.wait("vector", t_z)
                                P.wait("vector", t_consts)
                                if q >= 2:
                                    P.wait("vector", t_zde_q[q - 2])
                                P.instr("vector",
                                        lambda E, d=zdm[:, par * 128:(par + 1) * 128],
                                        s=P_A[:, ps * 512:ps * 512 + 128]:
                                        E.tensor_tensor(d, s, c_tril[:], ALU.mult))
                                t_zd = P.mark("vector", S["sV"])
                                t_PA_slot[ps] = t_zd
                                P.wait("scalar", t_zd)
                                P.wait("scalar", t_epack_free)
                                P.instr("scalar",
                                        lambda E, d=e_pack[:, QOFF[q] + q * 128:QOFF[q] + (q + 1) * 128],
                                        sz=zdm[:, par * 128:(par + 1) * 128],
                                        a=rs_all[:, q * 6 + used:q * 6 + used + 1]:
                                        E.activation(d, sz, AF.Exp, scale=1.0, accum_out=a))
                                t_e = P.mark("scalar", S["sS"])
                                t_zde_q[q] = t_e
                            used += 1
                        t_exps_all.append((t_e, used))
                    t_PA_free = [t_PA_slot[0], t_PA_slot[1]]
                    t_head[h]["exps"] = t_exps_all

                def fn_S2S3(h):
                    nonlocal t_epack_free
                    t_exps_all = t_head[h]["exps"]
                    t_alpha_done = [None] * NQ
                    t_attnT_done = [None] * NQ
                    for q in range(NQ):
                        P.wait("vector", t_exps_all[q][0])
                    for q in range(NQ):
                        P.instr("vector",
                                lambda E, q=q, u=t_exps_all[q][1]:
                                E.reduce_sum(rsum_all[:, q:q + 1],
                                             rs_all[:, q * 6:q * 6 + u], axis=AX.X))
                    vsync()
                    P.instr("vector",
                            lambda E: E.tensor_tensor(rsum_all[:], rsum_all[:], c_cnt2[:], ALU.add))
                    vsync()
                    P.instr("vector", lambda E: E.reciprocal(rden_all[:], rsum_all[:]))
                    vsync()
                    P.instr("vector", lambda E: E.tensor_copy(rdb_all[:], rden_all[:]))
                    t_rdb = P.mark("vector", S["sV"])
                    for q in range(NQ):
                        psr = q % 2
                        P.wait("tensor", t_rdb)
                        P.wait("tensor", t_PT_free[psr])
                        P.instr("tensor",
                                lambda E, d=P_T[0:1, psr * 1024 + 512:psr * 1024 + 640],
                                s=rdb_all[:, q:q + 1]:
                                E.transpose(d, s, c_id[:]))
                        t_tra = P.mark("tensor", S["sP"])
                        P.wait("vector", t_tra)
                        P.instr("vector",
                                lambda E, d=alpha[0:1, q * 128:(q + 1) * 128],
                                s=P_T[0:1, psr * 1024 + 512:psr * 1024 + 640]:
                                E.tensor_copy(d, s))
                        t_PT_free[psr] = P.mark("vector", S["sV"])
                        t_alpha_done[q] = t_PT_free[psr]
                    t_nrm_q = {}
                    P.wait("vector", t_rdb)
                    for q in range(NQ):
                        P.instr("vector",
                                lambda E, q=q, w=(q + 1) * 128:
                                E.tensor_scalar(e_pack[:, QOFF[q]:QOFF[q] + w],
                                                e_pack[:, QOFF[q]:QOFF[q] + w],
                                                rden_all[:, q:q + 1], None, ALU.mult))
                        t_nrm_q[q] = P.mark("vector", S["sV"])
                    for q in range(NQ):
                        t_last = None
                        for g in range((q + 1 + 3) // 4):
                            nj = min(4, q + 1 - 4 * g)
                            t_last = transpose_group(e_pack, QOFF[q] + g * 512, nj, attnT,
                                                     QOFF[q] + g * 512, g % 2, t_nrm_q[q])
                        t_attnT_done[q] = t_last
                    t_epack_free = (S["sV"], S["sV"].v)
                    t_head[h]["alpha"] = t_alpha_done
                    t_head[h]["attnT"] = t_attnT_done

                def fn_AV(h):
                    nonlocal t_PB_free, t_ofm_free
                    t_vtm = t_head[h]["vtm"]
                    t_alpha_done = t_head[h]["alpha"]
                    t_attnT_done = t_head[h]["attnT"]
                    t_evO = [None, None]
                    for th in range(2):
                        qb = th * 8
                        P.wait("tensor", t_vtm)
                        for q in range(qb, qb + 8):
                            P.wait("tensor", t_attnT_done[q])
                            P.wait("tensor", t_alpha_done[q])
                        P.wait("tensor", t_PB_free)
                        for j in range(qb + 8):
                            for q in range(max(j, qb), qb + 8):
                                P.instr("tensor",
                                        lambda E, o=P_B[:, (q - qb) * 128:(q - qb + 1) * 128],
                                        l=vtm[:, j * 128:(j + 1) * 128],
                                        r=attnT[:, QOFF[q] + j * 128:QOFF[q] + (j + 1) * 128],
                                        st=(j == 0 and (q - qb) % 4 == 0),
                                        sp=(j == NQ - 1 and q == NQ - 1):
                                        E.matmul(o, l, r, start=st, stop=sp))
                        for q in range(qb, qb + 8):
                            if q == NQ - 1:
                                continue
                            P.instr("tensor",
                                    lambda E, o=P_B[:, (q - qb) * 128:(q - qb + 1) * 128],
                                    l=sufft[0:1, q * 128:(q + 1) * 128],
                                    r=alpha[0:1, q * 128:(q + 1) * 128]:
                                    E.matmul(o, l, r, start=False, stop=True))
                        t_av = P.mark("tensor", S["sP"])
                        P.wait("scalar", t_av)
                        P.instr("scalar",
                                lambda E, d=ofm[:, th * 1024:(th + 1) * 1024]:
                                E.copy(out=d, in_=P_B[:]))
                        t_evO[th] = P.mark("scalar", S["sS"])
                        t_PB_free = t_evO[th]
                    P.wait("sync", t_evO[0])
                    P.wait("sync", t_evO[1])
                    P.wait("sync", t_ofm_free)
                    P.wait("sync", t_oscr_consumed)
                    P.instr("sync",
                            lambda E, d=o_scr[h, :, :], s=ofm: E.dma_start(out=d, in_=s[:]))
                    t_ofm_free = P.mark("sync", S["dSO"], 16)

                if "A" in DBG_PHASES and DBG_HEADS > 0:
                    fn_proj(0)
                    emit_casts(2)
                    fn_vprep(0)
                    for h in range(DBG_HEADS):
                        fn_S1(h)
                        if h + 1 < DBG_HEADS:
                            fn_proj(h + 1)
                        emit_casts(2)
                        fn_S2S3(h)
                        fn_AV(h)
                        if h + 1 < DBG_HEADS:
                            fn_vprep(h + 1)
                    emit_casts(len(cast_jobs))
                    t_cast_all = (S["dCA"], S["dCA"].v)

                t_oscr_stored = (S["dSO"], S["dSO"].v)
                P.flush(block)

            if "C" not in DBG_PHASES:
                continue
            with contextlib.ExitStack() as esCD:
                sb2 = lambda name, shape, dt: esCD.enter_context(
                    nc.sbuf_tensor(f"{name}_{seq}", shape, dt))
                xnfm = sb2("xnfm", [128, NK * 8 * 128], BF16)
                accb = sb2("acc", [128, 8 * D], F32)

                for half in range(2):
                    barrier_all()
                    with contextlib.ExitStack() as esC:
                        sb3 = lambda name, shape, dt: esC.enter_context(
                            nc.sbuf_tensor(f"{name}_{seq}_{half}", shape, dt))
                        wo_sb = sb3("wo_sb", [128, NK * 2048], BF16)
                        osl = sb3("osl", [128, NK * 128], BF16)
                        xct = [sb3(f"xct{i}", [128, D], F32) for i in range(2)]
                        r_b = sb3("r_b", [128, D], F32)
                        xn_b = sb3("xn_b", [128, D], F32)
                        xnbb = sb3("xnbb", [128, D], BF16)
                        rsm = sb3("rsm", [128, 2], F32)
                        sqm = sb3("sqm", [128, 2], F32)
                        ln1 = sb3("ln1", [128, 1], F32)
                        ln2 = sb3("ln2", [128, 1], F32)
                        ln3 = sb3("ln3", [128, 1], F32)

                        P.wait("gpsimd", t_oscr_stored)
                        P.wait("gpsimd", t_cast_all)
                        wo_r = wo_sb[:].rearrange("p (k n) -> p k n", n=2048)
                        P.instr("gpsimd",
                                lambda E: E.dma_start(out=wo_r[:, :, 0:1024], in_=wo_v[:, :, 0:1024]))
                        t_wo0 = P.mark("gpsimd", S["dWO"], 16)
                        P.instr("gpsimd",
                                lambda E: E.dma_start(out=wo_r[:, :, 1024:2048], in_=wo_v[:, :, 1024:2048]))
                        t_wo1 = P.mark("gpsimd", S["dWO"], 16)
                        t_wo_h = [t_wo0, t_wo1]

                        t_osl_free = None
                        t_xct_free = [None, None]
                        t_sq_prev = None
                        for ts in range(8):
                            tg = half * 8 + ts
                            P.wait("sync", t_oscr_stored)
                            P.wait("sync", t_osl_free)
                            P.instr("sync",
                                    lambda E, d=osl[:].rearrange("p (k j) -> p k j", j=128),
                                    s=o_scr_v[:, :, tg * 128:(tg + 1) * 128]:
                                    E.dma_start(out=d, in_=s))
                            t_os = P.mark("sync", S["dOS"], 16)
                            sl = ts % 2
                            P.wait("sync", t_xct_free[sl])
                            P.instr("sync",
                                    lambda E, d=xct[sl], s=x_in[seq, tg * 128:(tg + 1) * 128, :]:
                                    E.dma_start(out=d[:], in_=s))
                            t_xl = P.mark("sync", S["dX"], 16)
                            t_r = [None, None]
                            for ddh in range(2):
                                pt = P_A if ddh == 0 else P_C
                                P.wait("tensor", t_wo_h[ddh])
                                P.wait("tensor", t_os)
                                P.wait("tensor", t_PA_free if ddh == 0 else t_PC_free)
                                for k in range(NK):
                                    for s2 in range(2):
                                        P.instr("tensor",
                                                lambda E, o=pt[:, s2 * 512:(s2 + 1) * 512],
                                                l=osl[:, k * 128:(k + 1) * 128],
                                                r=wo_sb[:, k * 2048 + ddh * 1024 + s2 * 512:
                                                      k * 2048 + ddh * 1024 + (s2 + 1) * 512],
                                                st=(k == 0), sp=(k == NK - 1):
                                                E.matmul(o, l, r, start=st, stop=sp))
                                t_mm = P.mark("tensor", S["sP"])
                                P.wait("vector", t_mm)
                                P.wait("vector", t_xl)
                                P.wait("vector", t_sq_prev)
                                P.instr("vector",
                                        lambda E, d=r_b[:, ddh * 1024:(ddh + 1) * 1024],
                                        x2=xct[sl][:, ddh * 1024:(ddh + 1) * 1024],
                                        a=rsm[:, ddh:ddh + 1], sp2=pt:
                                        E.scalar_tensor_tensor(d, sp2[:], 1.0, x2,
                                                               ALU.mult, ALU.add, accum_out=a))
                                t_r[ddh] = P.mark("vector", S["sV"])
                                if ddh == 0:
                                    t_PA_free = t_r[0]
                                else:
                                    t_PC_free = t_r[1]
                            t_osl_free = t_r[1]
                            t_xct_free[sl] = t_r[1]
                            for ddh in range(2):
                                P.wait("scalar", t_r[ddh])
                                P.instr("scalar",
                                        lambda E, d=xn_b[:, ddh * 1024:(ddh + 1) * 1024],
                                        s=r_b[:, ddh * 1024:(ddh + 1) * 1024],
                                        a=sqm[:, ddh:ddh + 1]:
                                        E.activation(d, s, AF.Square, accum_out=a))
                            t_sq = P.mark("scalar", S["sS"])
                            t_sq_prev = t_sq
                            P.wait("vector", t_sq)
                            P.instr("vector",
                                    lambda E: E.tensor_tensor(ln1[:], rsm[:, 0:1], rsm[:, 1:2], ALU.add))
                            vsync()
                            P.instr("vector",
                                    lambda E: E.tensor_scalar(ln1[:], ln1[:], 1.0 / D, None, ALU.mult))
                            P.instr("vector",
                                    lambda E: E.tensor_tensor(ln2[:], sqm[:, 0:1], sqm[:, 1:2], ALU.add))
                            vsync()
                            P.instr("vector",
                                    lambda E: E.tensor_scalar(ln2[:], ln2[:], 1.0 / D, None, ALU.mult))
                            vsync()
                            P.instr("vector",
                                    lambda E: E.tensor_tensor(ln3[:], ln1[:], ln1[:], ALU.mult))
                            vsync()
                            P.instr("vector",
                                    lambda E: E.tensor_tensor(ln2[:], ln2[:], ln3[:], ALU.subtract))
                            t_var = P.mark("vector", S["sV"])
                            P.wait("scalar", t_var)
                            P.instr("scalar",
                                    lambda E: E.activation(ln2[:], ln2[:], AF.Sqrt,
                                                           bias=c_eps[:], scale=1.0))
                            t_sqr = P.mark("scalar", S["sS"])
                            P.wait("vector", t_sqr)
                            P.instr("vector", lambda E: E.reciprocal(ln3[:], ln2[:]))
                            vsync()
                            P.instr("vector",
                                    lambda E: E.tensor_scalar(xn_b[:], r_b[:], ln1[:], None,
                                                              ALU.subtract))
                            P.instr("vector",
                                    lambda E: E.tensor_scalar(r_b[:], xn_b[:], ln3[:], None,
                                                              ALU.mult))
                            P.instr("vector", lambda E: E.tensor_copy(xnbb[:], r_b[:]))
                            t_xnb = P.mark("vector", S["sV"])
                            P.wait("sync", t_xnb)
                            P.wait("sync", t_xn_consumed)
                            P.instr("sync",
                                    lambda E, d=xn_scr[ts, :, :]: E.dma_start(out=d, in_=r_b[:]))
                            t_xn_stored[ts] = P.mark("sync", S["dSX"], 16)
                            xnfm_r = xnfm[:].rearrange("p (k t) -> p k t", t=8 * 128)
                            for g in range(4):
                                transpose_group_strided(
                                    xnbb, [(4 * g + j) * 128 for j in range(4)],
                                    xnfm_r[:, 4 * g:4 * g + 4, ts * 128:(ts + 1) * 128],
                                    g % 2, t_xnb)
                        t_xnfm = t_PT_free[1] if t_PT_free[1] else t_PT_free[0]
                        t_xnfm = (S["sV"], S["sV"].v)
                        P.flush(block)

                    if "D" not in DBG_PHASES:
                        continue
                    barrier_all()
                    with contextlib.ExitStack() as esD:
                        sb4 = lambda name, shape, dt: esD.enter_context(
                            nc.sbuf_tensor(f"{name}_{seq}_{half}", shape, dt))
                        wgb = [sb4(f"wgb{i}", [128, NK * 512], BF16) for i in range(2)]
                        wub = [sb4(f"wub{i}", [128, NK * 512], BF16) for i in range(2)]
                        wdb = sb4("wdb", [128, 4 * 2048], BF16)
                        g_b = sb4("g_b", [128, 512], F32)
                        h_b = sb4("h_b", [128, 512], F32)
                        a_b = sb4("a_b", [128, 512], F32)
                        tA = sb4("tA", [128, 128], F32)
                        tB = sb4("tB", [128, 128], F32)
                        tC2 = sb4("tC2", [128, 128], F32)
                        tD2 = sb4("tD2", [128, 128], F32)
                        sec = sb4("sec", [128, 128], F32)
                        msk = sb4("msk", [128, 512], BF16)
                        hs_b = sb4("hs_b", [128, 512], BF16)
                        hsT = sb4("hsT", [128, 512], BF16)
                        xtf = [sb4("xtf0", [128, D], F32)]
                        outb = [sb4("outb0", [128, D], F32)]
                        r2sm = sb4("r2sm", [128, 1], F32)
                        q2sm = sb4("q2sm", [128, 1], F32)
                        l21 = sb4("l21", [128, 1], F32)
                        l22 = sb4("l22", [128, 1], F32)
                        l23 = sb4("l23", [128, 1], F32)

                        a_g = a_b[:].rearrange("p (g four) -> p g four", four=4)
                        h_g = h_b[:].rearrange("p (g four) -> p g four", four=4)
                        m_g = msk[:].rearrange("p (g four) -> p g four", four=4)

                        t_wgb_ready = [None, None]

                        def load_wgrp(ig, buf_i, t_free_g, t_free_u):
                            P.wait("gpsimd", t_free_g)
                            P.instr("gpsimd",
                                    lambda E, d=wgb[buf_i][:].rearrange("p (k n) -> p k n", n=512),
                                    s=wg_v[:, :, ig * 512:(ig + 1) * 512]:
                                    E.dma_start(out=d, in_=s))
                            P.mark("gpsimd", S["dWF"], 16)
                            P.wait("gpsimd", t_free_u)
                            P.instr("gpsimd",
                                    lambda E, d=wub[buf_i][:].rearrange("p (k n) -> p k n", n=512),
                                    s=wu_v[:, :, ig * 512:(ig + 1) * 512]:
                                    E.dma_start(out=d, in_=s))
                            t_wgb_ready[buf_i] = P.mark("gpsimd", S["dWF"], 16)
                            return t_wgb_ready[buf_i]

                        t_wg_free = [None, None]
                        load_wgrp(0, 0, None, None)
                        t_wd_free = None
                        t_wd = None
                        t_msk_prev = None
                        t_hs_prev = None
                        t_hs_free = None
                        t_hsT_free = None
                        t_acc_done = [None] * 8
                        t_psum_free = [t_PB_free, t_PA_free]
                        t_g_free = None
                        t_down_done = None
                        NU = NIG * 8

                        def emitA(u):
                            nonlocal t_wd, t_wd_free
                            ig, ts = u // 8, u % 8
                            bi = ig % 2
                            par = u % 2
                            pt = P_B if par == 0 else P_A
                            P.wait("tensor", t_wgb_ready[bi])
                            P.wait("tensor", t_xnfm)
                            P.wait("tensor", t_psum_free[par])
                            for k in range(NK):
                                P.instr("tensor",
                                        lambda E, o=pt[:, 0:512],
                                        l=xnfm[:, k * 1024 + ts * 128:k * 1024 + (ts + 1) * 128],
                                        r=wgb[bi][:, k * 512:(k + 1) * 512],
                                        st=(k == 0), sp=(k == NK - 1):
                                        E.matmul(o, l, r, start=st, stop=sp))
                                P.instr("tensor",
                                        lambda E, o=pt[:, 512:1024],
                                        l=xnfm[:, k * 1024 + ts * 128:k * 1024 + (ts + 1) * 128],
                                        r=wub[bi][:, k * 512:(k + 1) * 512],
                                        st=(k == 0), sp=(k == NK - 1):
                                        E.matmul(o, l, r, start=st, stop=sp))
                            return P.mark("tensor", S["sP"]), pt

                        def emitRest(u, t_gu, pt):
                            nonlocal t_msk_prev, t_hs_prev, t_hs_free, t_hsT_free
                            nonlocal t_psum_free, t_g_free, t_down_done, t_wd_free
                            ig, ts = u // 8, u % 8
                            par = u % 2
                            P.wait("scalar", t_gu)
                            P.wait("scalar", t_g_free)
                            P.instr("scalar",
                                    lambda E: E.activation(g_b[:], pt[:, 0:512], AF.Silu))
                            t_si = P.mark("scalar", S["sS"])
                            P.wait("vector", t_si)
                            P.instr("vector",
                                    lambda E: E.tensor_tensor(h_b[:], g_b[:], pt[:, 512:1024],
                                                              ALU.mult))
                            t_h = P.mark("vector", S["sV"])
                            t_psum_free[par] = t_h
                            t_g_free = t_h
                            P.wait("scalar", t_h)
                            P.wait("scalar", t_msk_prev)
                            P.instr("scalar",
                                    lambda E: E.activation(a_b[:], h_b[:], AF.Square))
                            t_a = P.mark("scalar", S["sS"])
                            P.wait("vector", t_a)
                            P.instr("vector",
                                    lambda E: E.tensor_tensor(tA[:], a_g[:, :, 0], a_g[:, :, 1], ALU.max))
                            P.instr("vector",
                                    lambda E: E.tensor_tensor(tB[:], a_g[:, :, 2], a_g[:, :, 3], ALU.max))
                            P.instr("vector",
                                    lambda E: E.tensor_tensor(tC2[:], a_g[:, :, 0], a_g[:, :, 1], ALU.min))
                            P.instr("vector",
                                    lambda E: E.tensor_tensor(tD2[:], a_g[:, :, 2], a_g[:, :, 3], ALU.min))
                            P.instr("vector",
                                    lambda E: E.tensor_tensor(tA[:], tA[:], tB[:], ALU.min))
                            P.instr("vector",
                                    lambda E: E.tensor_tensor(tC2[:], tC2[:], tD2[:], ALU.max))
                            P.instr("vector",
                                    lambda E: E.tensor_tensor(sec[:], tA[:], tC2[:], ALU.max))
                            for j4 in range(4):
                                P.instr("vector",
                                        lambda E, j4=j4:
                                        E.tensor_tensor(m_g[:, :, j4], a_g[:, :, j4], sec[:],
                                                        ALU.is_ge))
                            t_msk = P.mark("vector", S["sV"])
                            t_msk_prev = t_msk
                            P.wait("vector", t_hs_free)
                            P.instr("vector",
                                    lambda E: E.tensor_tensor(hs_b[:], h_b[:], msk[:], ALU.mult))
                            t_hs = P.mark("vector", S["sV"])
                            t_hs_prev = t_hs
                            if DBG_DUMP_D and ig == 0 and ts == 0:
                                P.wait("vector", t_gu)
                                P.instr("vector", lambda E: E.tensor_copy(xtf[0][:, 0:1024], pt[:]))
                                t_dgu = P.mark("vector", S["sV"])
                                for nm, src in [("g", g_b), ("h", h_b), ("a", a_b),
                                                ("sec", sec), ("msk", msk), ("hs", hs_b)]:
                                    P.wait("sync", t_hs)
                                    P.wait("sync", t_dgu)
                                    P.instr("sync",
                                            lambda E, d=dbgd[nm], s2=src:
                                            E.dma_start(out=d, in_=s2[:]))
                                    P.mark("sync", S["dSO"], 16)
                                P.instr("sync", lambda E: E.dma_start(out=dbgd["gu"], in_=xtf[0][:, 0:1024]))
                                t_dgu2 = P.mark("sync", S["dSO"], 16)
                                P.wait("sync", t_dgu2)
                            t_hsT = transpose_group(hs_b, 0, 4, hsT, 0, ig % 2, t_hs)
                            t_hs_free = t_hsT
                            if DBG_DUMP_D and ig == 0 and ts == 0:
                                P.wait("sync", t_hsT)
                                P.instr("sync", lambda E: E.dma_start(out=dbgd["hsT"], in_=hsT[:]))
                                P.mark("sync", S["dSO"], 16)
                            t_ac = None
                            for qt in range(4):
                                bk = qt % 2
                                P.wait("tensor", t_hsT)
                                P.wait("tensor", t_wd)
                                P.wait("tensor", t_PC_free_l[bk])
                                for c in range(4):
                                    P.instr("tensor",
                                            lambda E, o=P_C[:, bk * 512:(bk + 1) * 512],
                                            l=hsT[:, c * 128:(c + 1) * 128],
                                            r=wdb[:, c * 2048 + qt * 512:
                                                  c * 2048 + (qt + 1) * 512],
                                            st=(c == 0), sp=(c == 3):
                                            E.matmul(o, l, r, start=st, stop=sp))
                                t_dn = P.mark("tensor", S["sP"])
                                P.wait("vector", t_dn)
                                if ig == 0:
                                    P.instr("vector",
                                            lambda E, d=accb[:, ts * D + qt * 512:
                                                           ts * D + (qt + 1) * 512],
                                            sp2=P_C[:, bk * 512:(bk + 1) * 512]:
                                            E.tensor_copy(d, sp2))
                                else:
                                    P.instr("vector",
                                            lambda E, d=accb[:, ts * D + qt * 512:
                                                           ts * D + (qt + 1) * 512],
                                            sp2=P_C[:, bk * 512:(bk + 1) * 512]:
                                            E.tensor_tensor(d, d, sp2, ALU.add))
                                t_ac = P.mark("vector", S["sV"])
                                t_PC_free_l[bk] = t_ac
                            if ig == NIG - 1:
                                t_acc_done[ts] = t_ac
                            t_down_done = (S["sP"], S["sP"].v)

                        t_PC_free_l = [t_PC_free, t_PC_free]

                        t_xtf_free = [None]
                        t_ob_free = [None]

                        def emit_final(ts):
                            tg = half * 8 + ts
                            sl = 0
                            P.wait("sync", t_xn_stored[ts])
                            P.wait("sync", t_xtf_free[sl])
                            P.instr("sync",
                                    lambda E, d=xtf[sl], s=xn_scr[ts, :, :]:
                                    E.dma_start(out=d[:], in_=s))
                            t_xnl = P.mark("sync", S["dXN"], 16)
                            P.wait("vector", t_xnl)
                            P.wait("vector", t_acc_done[ts])
                            P.wait("vector", t_ob_free[0])
                            ob = outb[0]
                            P.instr("vector",
                                    lambda E, d=ob, a=accb[:, ts * D:(ts + 1) * D], x2=xtf[sl]:
                                    E.scalar_tensor_tensor(d[:], a, 1.0, x2[:], ALU.mult, ALU.add,
                                                           accum_out=r2sm[:]))
                            t_r2 = P.mark("vector", S["sV"])
                            P.wait("scalar", t_r2)
                            P.instr("scalar",
                                    lambda E, s=ob, d=accb[:, ts * D:(ts + 1) * D]:
                                    E.activation(d, s[:], AF.Square, accum_out=q2sm[:]))
                            t_q2 = P.mark("scalar", S["sS"])
                            P.wait("vector", t_q2)
                            P.instr("vector",
                                    lambda E: E.tensor_scalar(l21[:], r2sm[:], 1.0 / D, None, ALU.mult))
                            P.instr("vector",
                                    lambda E: E.tensor_scalar(l22[:], q2sm[:], 1.0 / D, None, ALU.mult))
                            vsync()
                            P.instr("vector",
                                    lambda E: E.tensor_tensor(l23[:], l21[:], l21[:], ALU.mult))
                            vsync()
                            P.instr("vector",
                                    lambda E: E.tensor_tensor(l22[:], l22[:], l23[:], ALU.subtract))
                            t_v2 = P.mark("vector", S["sV"])
                            P.wait("scalar", t_v2)
                            P.instr("scalar",
                                    lambda E: E.activation(l22[:], l22[:], AF.Sqrt,
                                                           bias=c_eps[:], scale=1.0))
                            t_s2 = P.mark("scalar", S["sS"])
                            P.wait("vector", t_s2)
                            P.instr("vector", lambda E: E.reciprocal(l23[:], l22[:]))
                            vsync()
                            P.instr("vector",
                                    lambda E, d=xtf[sl], s=ob: E.tensor_scalar(d[:], s[:], l21[:],
                                                                               None, ALU.subtract))
                            P.instr("vector",
                                    lambda E, d=ob, s=xtf[sl]: E.tensor_scalar(d[:], s[:], l23[:],
                                                                               None, ALU.mult))
                            t_o = P.mark("vector", S["sV"])
                            t_xtf_free[sl] = t_o
                            P.wait("sync", t_o)
                            P.instr("sync",
                                    lambda E, d=y_out[seq, tg * 128:(tg + 1) * 128, :], s=ob:
                                    E.dma_start(out=d, in_=s[:]))
                            t_ob_free[0] = P.mark("sync", S["dOUT"], 16)
                            nonlocal_n_out[0] += 1

                        prev = None
                        for u in range(NU):
                            ig, ts = u // 8, u % 8
                            cur = emitA(u)
                            if (u == 0) or (ts == 1 and ig >= 1):
                                P.wait("gpsimd", t_wd_free)
                                P.instr("gpsimd",
                                        lambda E, d=wdb[:].rearrange("p (c n) -> p c n", n=2048),
                                        s=wd_v[:, 4 * ig:4 * ig + 4, :]:
                                        E.dma_start(out=d, in_=s))
                                t_wd = P.mark("gpsimd", S["dWF"], 16)
                                if ig + 1 < NIG:
                                    load_wgrp(ig + 1, (ig + 1) % 2,
                                              t_wg_free[(ig + 1) % 2], t_wg_free[(ig + 1) % 2])
                            if prev is not None:
                                emitRest(*prev)
                                if ts == 0:
                                    t_wg_free[(ig + 1) % 2] = prev[1]
                                    t_wd_free = t_down_done
                                if prev[0] // 8 == NIG - 1:
                                    emit_final(prev[0] % 8)
                            prev = (u, cur[0], cur[1])
                        emitRest(*prev)
                        emit_final(prev[0] % 8)
                        t_PC_free = [t_PC_free_l[0], t_PC_free_l[1]]
                        t_PA_free = t_psum_free[1]
                        t_PB_free = t_psum_free[0]

                        if DBG_DUMP_D and half == 0:
                            P.wait("sync", (S["sV"], S["sV"].v))
                            P.instr("sync", lambda E: E.dma_start(out=dbgd["acc"], in_=accb[:]))
                            P.mark("sync", S["dSO"], 16)
                            P.instr("sync", lambda E: E.dma_start(out=dbgd["xnfm"], in_=xnfm[:]))
                            P.mark("sync", S["dSO"], 16)
                        t_xn_consumed = (S["dXN"], S["dXN"].v)
                        P.flush(block)

                t_oscr_consumed = (S["dOS"], S["dOS"].v)

        n_out = nonlocal_n_out[0]
        for nm in ("dOUT", "dSO", "dSX"):
            if S[nm].v:
                P.wait("sync", (S[nm], S[nm].v))
        for nm in ("sP", "sV", "sS", "sG"):
            if S[nm].v:
                P.wait("sync", (S[nm], S[nm].v))
        P.flush(block)

    return nc


_NC_CACHE = None


def _get_nc():
    global _NC_CACHE
    if _NC_CACHE is None:
        _NC_CACHE = build_nc()
    return _NC_CACHE


def kernel(L0_attn_x, L0_attn_Wq, L0_attn_Wk, L0_attn_Wv, L0_attn_Wo,
           L0_ffn_w_gate, L0_ffn_w_up, L0_ffn_w_down):
    x = np.ascontiguousarray(np.asarray(L0_attn_x, dtype=np.float32))
    wq = np.ascontiguousarray(np.asarray(L0_attn_Wq, dtype=np.float32))
    wk = np.ascontiguousarray(np.asarray(L0_attn_Wk, dtype=np.float32))
    wv = np.ascontiguousarray(np.asarray(L0_attn_Wv, dtype=np.float32))
    wo = np.ascontiguousarray(np.asarray(L0_attn_Wo, dtype=np.float32))
    wg = np.ascontiguousarray(np.asarray(L0_ffn_w_gate, dtype=np.float32))
    wu = np.ascontiguousarray(np.asarray(L0_ffn_w_up, dtype=np.float32))
    wd = np.ascontiguousarray(np.asarray(L0_ffn_w_down, dtype=np.float32))

    c_id = np.eye(128, dtype=ml_dtypes.bfloat16)
    c_tril = (np.tril(np.ones((128, 128), dtype=np.float32)) * SCALE).astype(np.float32)
    c_eps = np.full((128, 1), EPS, dtype=np.float32)
    c_cnt2 = np.tile(np.array([float(T - (q + 1) * 128) for q in range(NQ)],
                              dtype=np.float32)[None, :], (128, 1))

    nc = _get_nc()
    in_maps = []
    for c in range(NCORES):
        in_maps.append({
            "x": x[c * BLOC:(c + 1) * BLOC],
            "wq": wq, "wk": wk, "wv": wv, "wo": wo,
            "wg": wg, "wu": wu, "wd": wd,
            "c_id": c_id, "c_tril": c_tril, "c_eps": c_eps, "c_cnt2": c_cnt2,
        })
    res = run_bass_kernel_spmd(nc, in_maps, core_ids=list(range(NCORES)))
    out = np.concatenate([r["y"] for r in res.results], axis=0)
    return out.astype(np.float32)


# version 16
# speedup vs baseline: 1.1239x; 1.0102x over previous
import contextlib
import numpy as np
import ml_dtypes
import concourse.bass as bass
import concourse.mybir as mybir
from concourse.bass_utils import run_bass_kernel_spmd

F32 = mybir.dt.float32
BF16 = mybir.dt.bfloat16
AF = mybir.ActivationFunctionType
ALU = mybir.AluOpType
AX = mybir.AxisListType

NCORES = 8
BLOC = 2
T = 2048
D = 2048
H = 16
II = 8192
NK = D // 128
NQ = T // 128
NC_I = II // 128
NIG = 16
SCALE = float(1.0 / np.sqrt(float(T)))
EPS = 1e-5
QOFF = [128 * (q * (q + 1) // 2) for q in range(NQ + 1)]
ATT_W = QOFF[NQ]

DBG_SEQS = None
DBG_HEADS = H
DBG_PHASES = "TABCD"
DBG_DUMP = False
DBG_DUMP_AB = False
DBG_DUMP_D = False


class Sem:
    __slots__ = ("h", "v")

    def __init__(self, h):
        self.h = h
        self.v = 0


class Prog:
    ENGINES = ("tensor", "vector", "scalar", "gpsimd", "sync")

    def __init__(self):
        self.q = {e: [] for e in self.ENGINES}
        self.lastwait = {e: {} for e in self.ENGINES}
        self.last = {e: None for e in self.ENGINES}

    def instr(self, eng, fn):
        ent = {"fn": fn, "inc": None, "amt": 0}
        self.q[eng].append(ent)
        self.last[eng] = ent

    def mark(self, eng, sem, amt=1):
        ent = self.last[eng]
        assert ent is not None and ent["inc"] is None
        sem.v += amt
        ent["inc"] = sem
        ent["amt"] = amt
        return (sem, sem.v)

    def wait(self, eng, tkt):
        if tkt is None:
            return
        if isinstance(tkt, (list, tuple)) and tkt and isinstance(tkt[0], (list, tuple, type(None))):
            for t in tkt:
                self.wait(eng, t)
            return
        sem, v = tkt
        key = id(sem)
        if self.lastwait[eng].get(key, -1) >= v:
            return
        self.lastwait[eng][key] = v
        ent = {"fn": None, "inc": None, "amt": 0, "wait": (sem.h, v)}
        self.q[eng].append(ent)
        self.last[eng] = None

    def flush(self, block):
        for eng in self.ENGINES:
            ents = self.q[eng]
            if not ents:
                continue
            dec = getattr(block, eng)

            def section(E, ents=ents):
                for ent in ents:
                    if "wait" in ent:
                        s, v = ent["wait"]
                        E.wait_ge(s, v)
                    elif ent["inc"] is not None:
                        ent["fn"](E).then_inc(ent["inc"].h, ent["amt"])
                    else:
                        ent["fn"](E)

            dec(section)
        self.q = {e: [] for e in self.ENGINES}
        self.last = {e: None for e in self.ENGINES}


def build_nc():
    nc = bass.Bass("TRN2", target_bir_lowering=False)

    x_in = nc.dram_tensor("x", [BLOC, T, D], F32, kind="ExternalInput").ap()
    wq_in = nc.dram_tensor("wq", [D, D], F32, kind="ExternalInput").ap()
    wk_in = nc.dram_tensor("wk", [D, D], F32, kind="ExternalInput").ap()
    wv_in = nc.dram_tensor("wv", [D, D], F32, kind="ExternalInput").ap()
    wo_in = nc.dram_tensor("wo", [D, D], F32, kind="ExternalInput").ap()
    wg_in = nc.dram_tensor("wg", [D, II], F32, kind="ExternalInput").ap()
    wu_in = nc.dram_tensor("wu", [D, II], F32, kind="ExternalInput").ap()
    wd_in = nc.dram_tensor("wd", [II, D], F32, kind="ExternalInput").ap()
    c_id_in = nc.dram_tensor("c_id", [128, 128], BF16, kind="ExternalInput").ap()
    c_tril_in = nc.dram_tensor("c_tril", [128, 128], F32, kind="ExternalInput").ap()
    c_eps_in = nc.dram_tensor("c_eps", [128, 1], F32, kind="ExternalInput").ap()
    c_cnt2_in = nc.dram_tensor("c_cnt2", [128, 16], F32, kind="ExternalInput").ap()
    y_out = nc.dram_tensor("y", [BLOC, T, D], F32, kind="ExternalOutput").ap()
    scr_kind = "ExternalOutput" if DBG_DUMP else "Internal"
    o_scr = nc.dram_tensor("o_scr", [H, 128, T], BF16, kind=scr_kind).ap()
    wg16 = nc.dram_tensor("wg16", [D, II], BF16, kind="Internal").ap()
    wu16 = nc.dram_tensor("wu16", [D, II], BF16, kind="Internal").ap()
    wd16 = nc.dram_tensor("wd16", [II, D], BF16, kind="Internal").ap()
    wo16 = nc.dram_tensor("wo16", [D, D], BF16, kind="Internal").ap()
    xn_scr = nc.dram_tensor("xn_scr", [8, 128, D], F32, kind=scr_kind).ap()
    if DBG_DUMP_D:
        dbgd = {nm: nc.dram_tensor(f"dbgd_{nm}", shp, dt, kind="ExternalOutput").ap()
                for nm, shp, dt in [
                    ("xnfm", [128, NK * 8 * 128], BF16), ("acc", [128, 8 * D], F32),
                    ("g", [128, 512], F32), ("h", [128, 512], F32),
                    ("a", [128, 512], F32), ("sec", [128, 128], F32),
                    ("msk", [128, 512], BF16), ("hs", [128, 512], BF16),
                    ("hsT", [128, 512], BF16), ("gu", [128, 1024], F32)]}
    if DBG_DUMP_AB:
        dbg_outs = {nm: nc.dram_tensor(f"dbg_{nm}", shp, dt, kind="ExternalOutput").ap()
                    for nm, shp, dt in [
                        ("qfm", [128, T], BF16), ("kfm", [128, T], BF16),
                        ("vfm", [128, T], BF16), ("vtm", [128, NK * 128], BF16),
                        ("sufft", [1, NQ * 128], BF16), ("alpha", [1, NQ * 128], BF16),
                        ("attn", [128, T], BF16), ("attnT", [128, ATT_W], BF16),
                        ("ebuf", [128, T], F32), ("xfm", [128, NK * T], BF16)]}

    wq_v = wq_in.rearrange("(k p) (h j) -> p k h j", p=128, j=128)
    wk_v = wk_in.rearrange("(k p) (h j) -> p k h j", p=128, j=128)
    wv_v = wv_in.rearrange("(k p) (h j) -> p k h j", p=128, j=128)
    wo_v = wo16.rearrange("(k p) n -> p k n", p=128)
    wg_v = wg16.rearrange("(k p) n -> p k n", p=128)
    wu_v = wu16.rearrange("(k p) n -> p k n", p=128)
    wd_v = wd16.rearrange("(c p) n -> p c n", p=128)
    o_scr_v = o_scr.rearrange("h p t -> p h t")

    es0 = contextlib.ExitStack()
    with es0:
        names = ["dC", "dX", "dW", "dWO", "dOS", "dWF", "dXN",
                 "dSO", "dSX", "dOUT", "dCA", "sP", "sV", "sS", "sG"]
        S = {n: Sem(es0.enter_context(nc.semaphore(n))) for n in names}

        c_id = es0.enter_context(nc.sbuf_tensor("c_id_s", [128, 128], BF16))
        c_tril = es0.enter_context(nc.sbuf_tensor("c_tril_s", [128, 128], F32))
        c_eps = es0.enter_context(nc.sbuf_tensor("c_eps_s", [128, 1], F32))
        c_cnt2 = es0.enter_context(nc.sbuf_tensor("c_cnt2_s", [128, 16], F32))

        P_A = es0.enter_context(nc.psum_tensor("P_A", [128, 1024], F32))
        P_B = es0.enter_context(nc.psum_tensor("P_B", [128, 1024], F32))
        P_C = es0.enter_context(nc.psum_tensor("P_C", [128, 1024], F32))
        P_T = es0.enter_context(nc.psum_tensor("P_T", [128, 2048], BF16))

        block = es0.enter_context(nc.Block())
        P = Prog()

        P.instr("sync", lambda E: E.dma_start(out=c_id[:], in_=c_id_in))
        P.mark("sync", S["dC"], 16)
        P.instr("sync", lambda E: E.dma_start(out=c_tril[:], in_=c_tril_in))
        P.mark("sync", S["dC"], 16)
        P.instr("sync", lambda E: E.dma_start(out=c_eps[:], in_=c_eps_in))
        P.mark("sync", S["dC"], 16)
        P.instr("sync", lambda E: E.dma_start(out=c_cnt2[:], in_=c_cnt2_in))
        t_consts = P.mark("sync", S["dC"], 16)

        t_PA_free = None
        t_PB_free = None
        t_PC_free = None
        t_PT_free = [None, None]
        t_oscr_stored = None
        t_oscr_consumed = None
        t_xn_stored = [None] * 8
        t_xn_consumed = None

        def transpose_group(src_sb, src_off, n_blk, dst_sb, dst_off, ps, t_src, extra_waits=()):
            nonlocal t_PT_free
            P.wait("tensor", t_src)
            for w in extra_waits:
                P.wait("tensor", w)
            P.wait("tensor", t_PT_free[ps])
            P.wait("tensor", t_consts)
            for j in range(n_blk):
                P.instr("tensor",
                        lambda E, d=P_T[:, ps * 1024 + j * 128:ps * 1024 + (j + 1) * 128],
                        s=src_sb[:, src_off + j * 128:src_off + (j + 1) * 128]:
                        E.transpose(d, s, c_id[:]))
            t_tr = P.mark("tensor", S["sP"])
            P.wait("vector", t_tr)
            P.instr("vector",
                    lambda E, d=dst_sb[:, dst_off:dst_off + n_blk * 128],
                    s=P_T[:, ps * 1024:ps * 1024 + n_blk * 128]:
                    E.tensor_copy(d, s))
            t_cp = P.mark("vector", S["sV"])
            t_PT_free[ps] = t_cp
            return t_cp

        def transpose_group_strided(src_sb, blocks, dst_view, ps, t_src):
            nonlocal t_PT_free
            P.wait("tensor", t_src)
            P.wait("tensor", t_PT_free[ps])
            P.wait("tensor", t_consts)
            for j, off in enumerate(blocks):
                P.instr("tensor",
                        lambda E, d=P_T[:, ps * 1024 + j * 128:ps * 1024 + (j + 1) * 128],
                        s=src_sb[:, off:off + 128]:
                        E.transpose(d, s, c_id[:]))
            t_tr = P.mark("tensor", S["sP"])
            P.wait("vector", t_tr)
            nb = len(blocks)
            P.instr("vector",
                    lambda E, d=dst_view,
                    s=P_T[:, ps * 1024:ps * 1024 + nb * 128].rearrange("p (k t) -> p k t", t=128):
                    E.tensor_copy(d, s))
            t_cp = P.mark("vector", S["sV"])
            t_PT_free[ps] = t_cp
            return t_cp

        def vsync():
            t = P.mark("vector", S["sV"])
            P.wait("vector", t)
            return t

        def barrier_all():
            for eng in Prog.ENGINES:
                for sem in S.values():
                    if sem.v:
                        P.wait(eng, (sem, sem.v))

        n_out = 0
        nonlocal_n_out = [0]

        cast_jobs = []
        for src, dst, nrows, rchunk in ((wo_in, wo16, D, 256), (wd_in, wd16, II, 1024),
                                        (wg_in, wg16, D, 256), (wu_in, wu16, D, 256)):
            for r0 in range(0, nrows, rchunk):
                cast_jobs.append((src, dst, r0, r0 + rchunk))

        def emit_casts(njobs):
            t = None
            for _ in range(njobs):
                if not cast_jobs:
                    break
                src, dst, r0, r1 = cast_jobs.pop(0)
                P.instr("gpsimd",
                        lambda E, s2=src[r0:r1, :], d=dst[r0:r1, :]: E.dma_start(out=d, in_=s2))
                t = P.mark("gpsimd", S["dCA"], 16)
            return t

        t_cast_all = None
        seqs = DBG_SEQS if DBG_SEQS is not None else list(range(BLOC))
        for seq in seqs:
            barrier_all()
            with contextlib.ExitStack() as esAB:
                sb = lambda name, shape, dt: esAB.enter_context(
                    nc.sbuf_tensor(f"{name}_{seq}", shape, dt))
                xt = [sb(f"xt{i}", [128, D], F32) for i in range(2)]
                xb = [sb(f"xb{i}", [128, D], BF16) for i in range(2)]
                xfm = sb("xfm", [128, NK * T], BF16)
                wsl = {w: sb(f"w{w}0", [128, NK * 128], BF16) for w in "qkv"}
                qfm = sb("qfm", [128, T], BF16)
                kfm = sb("kfm", [128, T], BF16)
                vfm = sb("vfm", [128, T], BF16)
                vtm = sb("vtm", [128, NK * 128], BF16)
                cs_c = sb("cs", [128, 16], F32)
                suf_c = sb("suf", [128, 16], F32)
                suf_b = sb("sufb", [128, 16], BF16)
                sufft = sb("sufft", [1, NQ * 128], BF16)
                alpha = sb("alpha", [1, NQ * 128], BF16)
                e_pack = sb("epack", [128, ATT_W], BF16)
                zdm = sb("zdm", [128, 256], F32)
                rs_all = sb("rsall", [128, 96], F32)
                rsum_all = sb("rsumall", [128, 16], F32)
                rden_all = sb("rdenall", [128, 16], F32)
                rdb_all = sb("rdball", [128, 16], BF16)
                attnT = sb("attnT", [128, ATT_W], BF16)
                ofm = sb("ofm", [128, T], BF16)

                t_xt_free = [None, None]
                t_xb_free = [None, None]
                t_ofm_free = None
                t_epack_free = None

                for w, wvv in (("q", wq_v), ("k", wk_v), ("v", wv_v)):
                    P.instr("gpsimd",
                            lambda E, d=wsl[w], s=wvv[:, :, 0, :]:
                            E.dma_start(out=d[:].rearrange("p (k j) -> p k j", j=128), in_=s))
                    P.mark("gpsimd", S["dW"], 16)

                xfm_r = xfm[:].rearrange("p (k t) -> p k t", t=T)
                t_xfm = None
                for i in range(NQ):
                    sl = i % 2
                    P.wait("sync", t_xt_free[sl])
                    P.instr("sync",
                            lambda E, d=xt[sl], s=x_in[seq, i * 128:(i + 1) * 128, :]:
                            E.dma_start(out=d[:], in_=s))
                    t_ld = P.mark("sync", S["dX"], 16)
                    P.wait("vector", t_ld)
                    P.wait("vector", t_xb_free[sl])
                    P.instr("vector", lambda E, d=xb[sl], s=xt[sl]: E.tensor_copy(d[:], s[:]))
                    t_cast = P.mark("vector", S["sV"])
                    t_xt_free[sl] = t_cast
                    for g in range(4):
                        t_xfm = transpose_group_strided(
                            xb[sl], [(4 * g + j) * 128 for j in range(4)],
                            xfm_r[:, 4 * g:4 * g + 4, i * 128:(i + 1) * 128],
                            g % 2, t_cast)
                    t_xb_free[sl] = t_xfm

                t_cs_prev = None
                t_head = {}

                def fn_proj(h):
                    nonlocal t_PA_free, t_PC_free, t_cs_prev
                    t_w_ready = (S["dW"], 16 * 3 * (h + 1))

                    def proj(dst_sb, wbuf, pt, t_pfree, extra_evac_wait=None):
                        t_evac = None
                        for half in range(2):
                            P.wait("tensor", t_w_ready)
                            P.wait("tensor", t_xfm)
                            P.wait("tensor", t_pfree if half == 0 else t_evac)
                            for k in range(NK):
                                for s2 in range(2):
                                    P.instr("tensor",
                                            lambda E, o=pt[:, s2 * 512:(s2 + 1) * 512],
                                            l=wbuf[:, k * 128:(k + 1) * 128],
                                            r=xfm[:, k * T + half * 1024 + s2 * 512:
                                                  k * T + half * 1024 + (s2 + 1) * 512],
                                            st=(k == 0), sp=(k == NK - 1):
                                            E.matmul(o, l, r, start=st, stop=sp))
                            t_mm = P.mark("tensor", S["sP"])
                            P.wait("scalar", t_mm)
                            if extra_evac_wait is not None:
                                P.wait("scalar", extra_evac_wait)
                            P.instr("scalar",
                                    lambda E, d=dst_sb[:, half * 1024:(half + 1) * 1024], s=pt:
                                    E.copy(out=d, in_=s[:]))
                            t_evac = P.mark("scalar", S["sS"])
                        return t_evac

                    t_q = proj(qfm, wsl["q"], P_C, t_PC_free)
                    t_k = proj(kfm, wsl["k"], P_A, t_PA_free)
                    t_v = proj(vfm, wsl["v"], P_C, t_q, extra_evac_wait=t_cs_prev)
                    t_PA_free = t_k
                    t_PC_free = t_v
                    t_head[h] = {"q": t_q, "k": t_k, "v": t_v}
                    if h + 1 < DBG_HEADS:
                        for w, wvv, tk in (("q", wq_v, t_q), ("k", wk_v, t_k), ("v", wv_v, t_v)):
                            P.wait("gpsimd", tk)
                            P.instr("gpsimd",
                                    lambda E, d=wsl[w], s=wvv[:, :, h + 1, :]:
                                    E.dma_start(out=d[:].rearrange("p (k j) -> p k j", j=128), in_=s))
                            P.mark("gpsimd", S["dW"], 16)

                def fn_vprep(h):
                    nonlocal t_cs_prev
                    t_v = t_head[h]["v"]
                    t_vtm = None
                    for g in range(4):
                        t_vtm = transpose_group(vfm, g * 512, 4, vtm, g * 512, g % 2, t_v)
                    t_head[h]["vtm"] = t_vtm
                    P.wait("vector", t_v)
                    t_cs = None
                    for j in range(NK):
                        P.instr("vector",
                                lambda E, d=cs_c[:, j:j + 1], s=vfm[:, j * 128:(j + 1) * 128]:
                                E.reduce_sum(d, s, axis=AX.X))
                        t_cs = P.mark("vector", S["sV"])
                    P.wait("vector", t_cs)
                    for q in range(NQ - 1):
                        P.instr("vector",
                                lambda E, d=suf_c[:, q:q + 1], a=cs_c[:, q + 1:16]:
                                E.reduce_sum(d, a, axis=AX.X))
                    vsync()
                    P.instr("vector", lambda E: E.tensor_copy(suf_b[:], suf_c[:]))
                    t_suf = P.mark("vector", S["sV"])
                    t_cs_prev = t_suf
                    for q in range(NQ - 1):
                        ps = q % 2
                        P.wait("tensor", t_suf)
                        P.wait("tensor", t_PT_free[ps])
                        P.instr("tensor",
                                lambda E, d=P_T[0:1, ps * 1024 + 512:ps * 1024 + 640],
                                s=suf_b[:, q:q + 1]:
                                E.transpose(d, s, c_id[:]))
                        t_tr = P.mark("tensor", S["sP"])
                        P.wait("vector", t_tr)
                        P.instr("vector",
                                lambda E, d=sufft[0:1, q * 128:(q + 1) * 128],
                                s=P_T[0:1, ps * 1024 + 512:ps * 1024 + 640]:
                                E.tensor_copy(d, s))
                        t_PT_free[ps] = P.mark("vector", S["sV"])

                def fn_S1(h):
                    nonlocal t_PA_free
                    t_q = t_head[h]["q"]; t_k = t_head[h]["k"]
                    t_PA_slot = [t_PA_free, t_PA_free]
                    t_zde_q = [None] * NQ
                    t_exps_all = []
                    for q in range(NQ):
                        par = q % 2
                        seg_bounds = []
                        c0 = 0
                        while c0 < q * 128:
                            c1 = min(c0 + 512, q * 128)
                            seg_bounds.append((c0, c1))
                            c0 = c1
                        used = 0
                        nseg = len(seg_bounds)
                        for si in range(nseg + 1):
                            ps = si % 2
                            P.wait("tensor", t_q)
                            P.wait("tensor", t_k)
                            P.wait("tensor", t_PA_slot[ps])
                            if si < nseg:
                                c0, c1 = seg_bounds[si]
                                P.instr("tensor",
                                        lambda E, o=P_A[:, ps * 512:ps * 512 + (c1 - c0)],
                                        l=qfm[:, q * 128:(q + 1) * 128], r=kfm[:, c0:c1]:
                                        E.matmul(o, l, r, start=True, stop=True))
                                t_z = P.mark("tensor", S["sP"])
                                P.wait("scalar", t_z)
                                P.wait("scalar", t_epack_free)
                                P.instr("scalar",
                                        lambda E, d=e_pack[:, QOFF[q] + c0:QOFF[q] + c1],
                                        s=P_A[:, ps * 512:ps * 512 + (c1 - c0)],
                                        a=rs_all[:, q * 6 + used:q * 6 + used + 1]:
                                        E.activation(d, s, AF.Exp, scale=SCALE, accum_out=a))
                                t_e = P.mark("scalar", S["sS"])
                                t_PA_slot[ps] = t_e
                            else:
                                P.instr("tensor",
                                        lambda E, o=P_A[:, ps * 512:ps * 512 + 128],
                                        l=qfm[:, q * 128:(q + 1) * 128],
                                        r=kfm[:, q * 128:(q + 1) * 128]:
                                        E.matmul(o, l, r, start=True, stop=True))
                                t_z = P.mark("tensor", S["sP"])
                                P.wait("vector", t_z)
                                P.wait("vector", t_consts)
                                if q >= 2:
                                    P.wait("vector", t_zde_q[q - 2])
                                P.instr("vector",
                                        lambda E, d=zdm[:, par * 128:(par + 1) * 128],
                                        s=P_A[:, ps * 512:ps * 512 + 128]:
                                        E.tensor_tensor(d, s, c_tril[:], ALU.mult))
                                t_zd = P.mark("vector", S["sV"])
                                t_PA_slot[ps] = t_zd
                                P.wait("scalar", t_zd)
                                P.wait("scalar", t_epack_free)
                                P.instr("scalar",
                                        lambda E, d=e_pack[:, QOFF[q] + q * 128:QOFF[q] + (q + 1) * 128],
                                        sz=zdm[:, par * 128:(par + 1) * 128],
                                        a=rs_all[:, q * 6 + used:q * 6 + used + 1]:
                                        E.activation(d, sz, AF.Exp, scale=1.0, accum_out=a))
                                t_e = P.mark("scalar", S["sS"])
                                t_zde_q[q] = t_e
                            used += 1
                        t_exps_all.append((t_e, used))
                    t_PA_free = [t_PA_slot[0], t_PA_slot[1]]
                    t_head[h]["exps"] = t_exps_all

                def fn_S2S3(h):
                    nonlocal t_epack_free
                    t_exps_all = t_head[h]["exps"]
                    t_alpha_done = [None] * NQ
                    t_attnT_done = [None] * NQ
                    for q in range(NQ):
                        P.wait("vector", t_exps_all[q][0])
                    for q in range(NQ):
                        P.instr("vector",
                                lambda E, q=q, u=t_exps_all[q][1]:
                                E.reduce_sum(rsum_all[:, q:q + 1],
                                             rs_all[:, q * 6:q * 6 + u], axis=AX.X))
                    vsync()
                    P.instr("vector",
                            lambda E: E.tensor_tensor(rsum_all[:], rsum_all[:], c_cnt2[:], ALU.add))
                    vsync()
                    P.instr("vector", lambda E: E.reciprocal(rden_all[:], rsum_all[:]))
                    vsync()
                    P.instr("vector", lambda E: E.tensor_copy(rdb_all[:], rden_all[:]))
                    t_rdb = P.mark("vector", S["sV"])
                    for q in range(NQ):
                        psr = q % 2
                        P.wait("tensor", t_rdb)
                        P.wait("tensor", t_PT_free[psr])
                        P.instr("tensor",
                                lambda E, d=P_T[0:1, psr * 1024 + 512:psr * 1024 + 640],
                                s=rdb_all[:, q:q + 1]:
                                E.transpose(d, s, c_id[:]))
                        t_tra = P.mark("tensor", S["sP"])
                        P.wait("vector", t_tra)
                        P.instr("vector",
                                lambda E, d=alpha[0:1, q * 128:(q + 1) * 128],
                                s=P_T[0:1, psr * 1024 + 512:psr * 1024 + 640]:
                                E.tensor_copy(d, s))
                        t_PT_free[psr] = P.mark("vector", S["sV"])
                        t_alpha_done[q] = t_PT_free[psr]
                    for q in range(NQ):
                        P.wait("vector", t_rdb)
                        P.instr("vector",
                                lambda E, q=q, w=(q + 1) * 128:
                                E.tensor_scalar(e_pack[:, QOFF[q]:QOFF[q] + w],
                                                e_pack[:, QOFF[q]:QOFF[q] + w],
                                                rden_all[:, q:q + 1], None, ALU.mult))
                        t_nrm = P.mark("vector", S["sV"])
                        t_last = None
                        for g in range((q + 1 + 3) // 4):
                            nj = min(4, q + 1 - 4 * g)
                            t_last = transpose_group(e_pack, QOFF[q] + g * 512, nj, attnT,
                                                     QOFF[q] + g * 512, g % 2, t_nrm)
                        t_attnT_done[q] = t_last
                    t_epack_free = (S["sV"], S["sV"].v)
                    t_head[h]["alpha"] = t_alpha_done
                    t_head[h]["attnT"] = t_attnT_done

                def fn_AV(h):
                    nonlocal t_PB_free, t_ofm_free
                    t_vtm = t_head[h]["vtm"]
                    t_alpha_done = t_head[h]["alpha"]
                    t_attnT_done = t_head[h]["attnT"]
                    t_evO = [None, None]
                    for th in range(2):
                        qb = th * 8
                        P.wait("tensor", t_vtm)
                        for q in range(qb, qb + 8):
                            P.wait("tensor", t_attnT_done[q])
                            P.wait("tensor", t_alpha_done[q])
                        P.wait("tensor", t_PB_free)
                        for j in range(qb + 8):
                            for q in range(max(j, qb), qb + 8):
                                P.instr("tensor",
                                        lambda E, o=P_B[:, (q - qb) * 128:(q - qb + 1) * 128],
                                        l=vtm[:, j * 128:(j + 1) * 128],
                                        r=attnT[:, QOFF[q] + j * 128:QOFF[q] + (j + 1) * 128],
                                        st=(j == 0 and (q - qb) % 4 == 0),
                                        sp=(j == NQ - 1 and q == NQ - 1):
                                        E.matmul(o, l, r, start=st, stop=sp))
                        for q in range(qb, qb + 8):
                            if q == NQ - 1:
                                continue
                            P.instr("tensor",
                                    lambda E, o=P_B[:, (q - qb) * 128:(q - qb + 1) * 128],
                                    l=sufft[0:1, q * 128:(q + 1) * 128],
                                    r=alpha[0:1, q * 128:(q + 1) * 128]:
                                    E.matmul(o, l, r, start=False, stop=True))
                        t_av = P.mark("tensor", S["sP"])
                        P.wait("scalar", t_av)
                        P.instr("scalar",
                                lambda E, d=ofm[:, th * 1024:(th + 1) * 1024]:
                                E.copy(out=d, in_=P_B[:]))
                        t_evO[th] = P.mark("scalar", S["sS"])
                        t_PB_free = t_evO[th]
                    P.wait("sync", t_evO[0])
                    P.wait("sync", t_evO[1])
                    P.wait("sync", t_ofm_free)
                    P.wait("sync", t_oscr_consumed)
                    P.instr("sync",
                            lambda E, d=o_scr[h, :, :], s=ofm: E.dma_start(out=d, in_=s[:]))
                    t_ofm_free = P.mark("sync", S["dSO"], 16)

                if "A" in DBG_PHASES and DBG_HEADS > 0:
                    fn_proj(0)
                    emit_casts(2)
                    fn_vprep(0)
                    for h in range(DBG_HEADS):
                        fn_S1(h)
                        if h + 1 < DBG_HEADS:
                            fn_proj(h + 1)
                        emit_casts(2)
                        fn_S2S3(h)
                        fn_AV(h)
                        if h + 1 < DBG_HEADS:
                            fn_vprep(h + 1)
                    emit_casts(len(cast_jobs))
                    t_cast_all = (S["dCA"], S["dCA"].v)

                t_oscr_stored = (S["dSO"], S["dSO"].v)
                P.flush(block)

            if "C" not in DBG_PHASES:
                continue
            with contextlib.ExitStack() as esCD:
                sb2 = lambda name, shape, dt: esCD.enter_context(
                    nc.sbuf_tensor(f"{name}_{seq}", shape, dt))
                xnfm = sb2("xnfm", [128, NK * 8 * 128], BF16)
                accb = sb2("acc", [128, 8 * D], F32)

                for half in range(2):
                    barrier_all()
                    with contextlib.ExitStack() as esC:
                        sb3 = lambda name, shape, dt: esC.enter_context(
                            nc.sbuf_tensor(f"{name}_{seq}_{half}", shape, dt))
                        wo_sb = sb3("wo_sb", [128, NK * 2048], BF16)
                        osl = sb3("osl", [128, NK * 128], BF16)
                        xct = [sb3(f"xct{i}", [128, D], F32) for i in range(2)]
                        r_b = sb3("r_b", [128, D], F32)
                        xn_b = sb3("xn_b", [128, D], F32)
                        xnbb = sb3("xnbb", [128, D], BF16)
                        rsm = sb3("rsm", [128, 2], F32)
                        sqm = sb3("sqm", [128, 2], F32)
                        ln1 = sb3("ln1", [128, 1], F32)
                        ln2 = sb3("ln2", [128, 1], F32)
                        ln3 = sb3("ln3", [128, 1], F32)

                        P.wait("gpsimd", t_oscr_stored)
                        P.wait("gpsimd", t_cast_all)
                        wo_r = wo_sb[:].rearrange("p (k n) -> p k n", n=2048)
                        P.instr("gpsimd",
                                lambda E: E.dma_start(out=wo_r[:, :, 0:1024], in_=wo_v[:, :, 0:1024]))
                        t_wo0 = P.mark("gpsimd", S["dWO"], 16)
                        P.instr("gpsimd",
                                lambda E: E.dma_start(out=wo_r[:, :, 1024:2048], in_=wo_v[:, :, 1024:2048]))
                        t_wo1 = P.mark("gpsimd", S["dWO"], 16)
                        t_wo_h = [t_wo0, t_wo1]

                        t_osl_free = None
                        t_xct_free = [None, None]
                        t_sq_prev = None
                        for ts in range(8):
                            tg = half * 8 + ts
                            P.wait("sync", t_oscr_stored)
                            P.wait("sync", t_osl_free)
                            P.instr("sync",
                                    lambda E, d=osl[:].rearrange("p (k j) -> p k j", j=128),
                                    s=o_scr_v[:, :, tg * 128:(tg + 1) * 128]:
                                    E.dma_start(out=d, in_=s))
                            t_os = P.mark("sync", S["dOS"], 16)
                            sl = ts % 2
                            P.wait("sync", t_xct_free[sl])
                            P.instr("sync",
                                    lambda E, d=xct[sl], s=x_in[seq, tg * 128:(tg + 1) * 128, :]:
                                    E.dma_start(out=d[:], in_=s))
                            t_xl = P.mark("sync", S["dX"], 16)
                            t_r = [None, None]
                            for ddh in range(2):
                                pt = P_A if ddh == 0 else P_C
                                P.wait("tensor", t_wo_h[ddh])
                                P.wait("tensor", t_os)
                                P.wait("tensor", t_PA_free if ddh == 0 else t_PC_free)
                                for k in range(NK):
                                    for s2 in range(2):
                                        P.instr("tensor",
                                                lambda E, o=pt[:, s2 * 512:(s2 + 1) * 512],
                                                l=osl[:, k * 128:(k + 1) * 128],
                                                r=wo_sb[:, k * 2048 + ddh * 1024 + s2 * 512:
                                                      k * 2048 + ddh * 1024 + (s2 + 1) * 512],
                                                st=(k == 0), sp=(k == NK - 1):
                                                E.matmul(o, l, r, start=st, stop=sp))
                                t_mm = P.mark("tensor", S["sP"])
                                P.wait("vector", t_mm)
                                P.wait("vector", t_xl)
                                P.wait("vector", t_sq_prev)
                                P.instr("vector",
                                        lambda E, d=r_b[:, ddh * 1024:(ddh + 1) * 1024],
                                        x2=xct[sl][:, ddh * 1024:(ddh + 1) * 1024],
                                        a=rsm[:, ddh:ddh + 1], sp2=pt:
                                        E.scalar_tensor_tensor(d, sp2[:], 1.0, x2,
                                                               ALU.mult, ALU.add, accum_out=a))
                                t_r[ddh] = P.mark("vector", S["sV"])
                                if ddh == 0:
                                    t_PA_free = t_r[0]
                                else:
                                    t_PC_free = t_r[1]
                            t_osl_free = t_r[1]
                            t_xct_free[sl] = t_r[1]
                            for ddh in range(2):
                                P.wait("scalar", t_r[ddh])
                                P.instr("scalar",
                                        lambda E, d=xn_b[:, ddh * 1024:(ddh + 1) * 1024],
                                        s=r_b[:, ddh * 1024:(ddh + 1) * 1024],
                                        a=sqm[:, ddh:ddh + 1]:
                                        E.activation(d, s, AF.Square, accum_out=a))
                            t_sq = P.mark("scalar", S["sS"])
                            t_sq_prev = t_sq
                            P.wait("vector", t_sq)
                            P.instr("vector",
                                    lambda E: E.tensor_tensor(ln1[:], rsm[:, 0:1], rsm[:, 1:2], ALU.add))
                            vsync()
                            P.instr("vector",
                                    lambda E: E.tensor_scalar(ln1[:], ln1[:], 1.0 / D, None, ALU.mult))
                            P.instr("vector",
                                    lambda E: E.tensor_tensor(ln2[:], sqm[:, 0:1], sqm[:, 1:2], ALU.add))
                            vsync()
                            P.instr("vector",
                                    lambda E: E.tensor_scalar(ln2[:], ln2[:], 1.0 / D, None, ALU.mult))
                            vsync()
                            P.instr("vector",
                                    lambda E: E.tensor_tensor(ln3[:], ln1[:], ln1[:], ALU.mult))
                            vsync()
                            P.instr("vector",
                                    lambda E: E.tensor_tensor(ln2[:], ln2[:], ln3[:], ALU.subtract))
                            t_var = P.mark("vector", S["sV"])
                            P.wait("scalar", t_var)
                            P.instr("scalar",
                                    lambda E: E.activation(ln2[:], ln2[:], AF.Sqrt,
                                                           bias=c_eps[:], scale=1.0))
                            t_sqr = P.mark("scalar", S["sS"])
                            P.wait("vector", t_sqr)
                            P.instr("vector", lambda E: E.reciprocal(ln3[:], ln2[:]))
                            vsync()
                            P.instr("vector",
                                    lambda E: E.tensor_scalar(xn_b[:], r_b[:], ln1[:], None,
                                                              ALU.subtract))
                            P.instr("vector",
                                    lambda E: E.tensor_scalar(r_b[:], xn_b[:], ln3[:], None,
                                                              ALU.mult))
                            P.instr("vector", lambda E: E.tensor_copy(xnbb[:], r_b[:]))
                            t_xnb = P.mark("vector", S["sV"])
                            P.wait("sync", t_xnb)
                            P.wait("sync", t_xn_consumed)
                            P.instr("sync",
                                    lambda E, d=xn_scr[ts, :, :]: E.dma_start(out=d, in_=r_b[:]))
                            t_xn_stored[ts] = P.mark("sync", S["dSX"], 16)
                            xnfm_r = xnfm[:].rearrange("p (k t) -> p k t", t=8 * 128)
                            for g in range(4):
                                transpose_group_strided(
                                    xnbb, [(4 * g + j) * 128 for j in range(4)],
                                    xnfm_r[:, 4 * g:4 * g + 4, ts * 128:(ts + 1) * 128],
                                    g % 2, t_xnb)
                        t_xnfm = t_PT_free[1] if t_PT_free[1] else t_PT_free[0]
                        t_xnfm = (S["sV"], S["sV"].v)
                        P.flush(block)

                    if "D" not in DBG_PHASES:
                        continue
                    barrier_all()
                    with contextlib.ExitStack() as esD:
                        sb4 = lambda name, shape, dt: esD.enter_context(
                            nc.sbuf_tensor(f"{name}_{seq}_{half}", shape, dt))
                        wgb = [sb4(f"wgb{i}", [128, NK * 512], BF16) for i in range(2)]
                        wub = [sb4(f"wub{i}", [128, NK * 512], BF16) for i in range(2)]
                        wdb = sb4("wdb", [128, 4 * 2048], BF16)
                        g_b = sb4("g_b", [128, 512], F32)
                        h_b = sb4("h_b", [128, 512], F32)
                        a_b = sb4("a_b", [128, 512], F32)
                        tA = sb4("tA", [128, 128], F32)
                        tB = sb4("tB", [128, 128], F32)
                        tC2 = sb4("tC2", [128, 128], F32)
                        tD2 = sb4("tD2", [128, 128], F32)
                        sec = sb4("sec", [128, 128], F32)
                        msk = sb4("msk", [128, 512], BF16)
                        hs_b = sb4("hs_b", [128, 512], BF16)
                        hsT = sb4("hsT", [128, 512], BF16)
                        xtf = [sb4("xtf0", [128, D], F32)]
                        outb = [sb4("outb0", [128, D], F32)]
                        r2sm = sb4("r2sm", [128, 1], F32)
                        q2sm = sb4("q2sm", [128, 1], F32)
                        l21 = sb4("l21", [128, 1], F32)
                        l22 = sb4("l22", [128, 1], F32)
                        l23 = sb4("l23", [128, 1], F32)

                        a_g = a_b[:].rearrange("p (g four) -> p g four", four=4)
                        h_g = h_b[:].rearrange("p (g four) -> p g four", four=4)
                        m_g = msk[:].rearrange("p (g four) -> p g four", four=4)

                        t_wgb_ready = [None, None]

                        def load_wgrp(ig, buf_i, t_free_g, t_free_u):
                            P.wait("gpsimd", t_free_g)
                            P.instr("gpsimd",
                                    lambda E, d=wgb[buf_i][:].rearrange("p (k n) -> p k n", n=512),
                                    s=wg_v[:, :, ig * 512:(ig + 1) * 512]:
                                    E.dma_start(out=d, in_=s))
                            P.mark("gpsimd", S["dWF"], 16)
                            P.wait("gpsimd", t_free_u)
                            P.instr("gpsimd",
                                    lambda E, d=wub[buf_i][:].rearrange("p (k n) -> p k n", n=512),
                                    s=wu_v[:, :, ig * 512:(ig + 1) * 512]:
                                    E.dma_start(out=d, in_=s))
                            t_wgb_ready[buf_i] = P.mark("gpsimd", S["dWF"], 16)
                            return t_wgb_ready[buf_i]

                        t_wg_free = [None, None]
                        load_wgrp(0, 0, None, None)
                        t_wd_free = None
                        t_wd = None
                        t_msk_prev = None
                        t_hs_prev = None
                        t_hs_free = None
                        t_hsT_free = None
                        t_acc_done = [None] * 8
                        t_psum_free = [t_PB_free, t_PA_free]
                        t_g_free = None
                        t_down_done = None
                        NU = NIG * 8

                        def emitA(u):
                            nonlocal t_wd, t_wd_free
                            ig, ts = u // 8, u % 8
                            bi = ig % 2
                            par = u % 2
                            pt = P_B if par == 0 else P_A
                            P.wait("tensor", t_wgb_ready[bi])
                            P.wait("tensor", t_xnfm)
                            P.wait("tensor", t_psum_free[par])
                            for k in range(NK):
                                P.instr("tensor",
                                        lambda E, o=pt[:, 0:512],
                                        l=xnfm[:, k * 1024 + ts * 128:k * 1024 + (ts + 1) * 128],
                                        r=wgb[bi][:, k * 512:(k + 1) * 512],
                                        st=(k == 0), sp=(k == NK - 1):
                                        E.matmul(o, l, r, start=st, stop=sp))
                                P.instr("tensor",
                                        lambda E, o=pt[:, 512:1024],
                                        l=xnfm[:, k * 1024 + ts * 128:k * 1024 + (ts + 1) * 128],
                                        r=wub[bi][:, k * 512:(k + 1) * 512],
                                        st=(k == 0), sp=(k == NK - 1):
                                        E.matmul(o, l, r, start=st, stop=sp))
                            return P.mark("tensor", S["sP"]), pt

                        def emitRest(u, t_gu, pt):
                            nonlocal t_msk_prev, t_hs_prev, t_hs_free, t_hsT_free
                            nonlocal t_psum_free, t_g_free, t_down_done, t_wd_free
                            ig, ts = u // 8, u % 8
                            par = u % 2
                            P.wait("scalar", t_gu)
                            P.wait("scalar", t_g_free)
                            P.instr("scalar",
                                    lambda E: E.activation(g_b[:], pt[:, 0:512], AF.Silu))
                            t_si = P.mark("scalar", S["sS"])
                            P.wait("vector", t_si)
                            P.instr("vector",
                                    lambda E: E.tensor_tensor(h_b[:], g_b[:], pt[:, 512:1024],
                                                              ALU.mult))
                            t_h = P.mark("vector", S["sV"])
                            t_psum_free[par] = t_h
                            t_g_free = t_h
                            P.wait("scalar", t_h)
                            P.wait("scalar", t_msk_prev)
                            P.instr("scalar",
                                    lambda E: E.activation(a_b[:], h_b[:], AF.Square))
                            t_a = P.mark("scalar", S["sS"])
                            P.wait("vector", t_a)
                            P.instr("vector",
                                    lambda E: E.tensor_tensor(tA[:], a_g[:, :, 0], a_g[:, :, 1], ALU.max))
                            P.instr("vector",
                                    lambda E: E.tensor_tensor(tB[:], a_g[:, :, 2], a_g[:, :, 3], ALU.max))
                            P.instr("vector",
                                    lambda E: E.tensor_tensor(tC2[:], a_g[:, :, 0], a_g[:, :, 1], ALU.min))
                            P.instr("vector",
                                    lambda E: E.tensor_tensor(tD2[:], a_g[:, :, 2], a_g[:, :, 3], ALU.min))
                            P.instr("vector",
                                    lambda E: E.tensor_tensor(tA[:], tA[:], tB[:], ALU.min))
                            P.instr("vector",
                                    lambda E: E.tensor_tensor(tC2[:], tC2[:], tD2[:], ALU.max))
                            P.instr("vector",
                                    lambda E: E.tensor_tensor(sec[:], tA[:], tC2[:], ALU.max))
                            for j4 in range(4):
                                P.instr("vector",
                                        lambda E, j4=j4:
                                        E.tensor_tensor(m_g[:, :, j4], a_g[:, :, j4], sec[:],
                                                        ALU.is_ge))
                            t_msk = P.mark("vector", S["sV"])
                            t_msk_prev = t_msk
                            P.wait("vector", t_hs_free)
                            P.instr("vector",
                                    lambda E: E.tensor_tensor(hs_b[:], h_b[:], msk[:], ALU.mult))
                            t_hs = P.mark("vector", S["sV"])
                            t_hs_prev = t_hs
                            if DBG_DUMP_D and ig == 0 and ts == 0:
                                P.wait("vector", t_gu)
                                P.instr("vector", lambda E: E.tensor_copy(xtf[0][:, 0:1024], pt[:]))
                                t_dgu = P.mark("vector", S["sV"])
                                for nm, src in [("g", g_b), ("h", h_b), ("a", a_b),
                                                ("sec", sec), ("msk", msk), ("hs", hs_b)]:
                                    P.wait("sync", t_hs)
                                    P.wait("sync", t_dgu)
                                    P.instr("sync",
                                            lambda E, d=dbgd[nm], s2=src:
                                            E.dma_start(out=d, in_=s2[:]))
                                    P.mark("sync", S["dSO"], 16)
                                P.instr("sync", lambda E: E.dma_start(out=dbgd["gu"], in_=xtf[0][:, 0:1024]))
                                t_dgu2 = P.mark("sync", S["dSO"], 16)
                                P.wait("sync", t_dgu2)
                            t_hsT = transpose_group(hs_b, 0, 4, hsT, 0, ig % 2, t_hs)
                            t_hs_free = t_hsT
                            if DBG_DUMP_D and ig == 0 and ts == 0:
                                P.wait("sync", t_hsT)
                                P.instr("sync", lambda E: E.dma_start(out=dbgd["hsT"], in_=hsT[:]))
                                P.mark("sync", S["dSO"], 16)
                            t_ac = None
                            for qt in range(4):
                                bk = qt % 2
                                P.wait("tensor", t_hsT)
                                P.wait("tensor", t_wd)
                                P.wait("tensor", t_PC_free_l[bk])
                                for c in range(4):
                                    P.instr("tensor",
                                            lambda E, o=P_C[:, bk * 512:(bk + 1) * 512],
                                            l=hsT[:, c * 128:(c + 1) * 128],
                                            r=wdb[:, c * 2048 + qt * 512:
                                                  c * 2048 + (qt + 1) * 512],
                                            st=(c == 0), sp=(c == 3):
                                            E.matmul(o, l, r, start=st, stop=sp))
                                t_dn = P.mark("tensor", S["sP"])
                                P.wait("vector", t_dn)
                                if ig == 0:
                                    P.instr("vector",
                                            lambda E, d=accb[:, ts * D + qt * 512:
                                                           ts * D + (qt + 1) * 512],
                                            sp2=P_C[:, bk * 512:(bk + 1) * 512]:
                                            E.tensor_copy(d, sp2))
                                else:
                                    P.instr("vector",
                                            lambda E, d=accb[:, ts * D + qt * 512:
                                                           ts * D + (qt + 1) * 512],
                                            sp2=P_C[:, bk * 512:(bk + 1) * 512]:
                                            E.tensor_tensor(d, d, sp2, ALU.add))
                                t_ac = P.mark("vector", S["sV"])
                                t_PC_free_l[bk] = t_ac
                            if ig == NIG - 1:
                                t_acc_done[ts] = t_ac
                            t_down_done = t_dn

                        t_PC_free_l = [t_PC_free, t_PC_free]

                        t_xtf_free = [None]
                        t_ob_free = [None]

                        def emit_final(ts):
                            tg = half * 8 + ts
                            sl = 0
                            P.wait("sync", t_xn_stored[ts])
                            P.wait("sync", t_xtf_free[sl])
                            P.instr("sync",
                                    lambda E, d=xtf[sl], s=xn_scr[ts, :, :]:
                                    E.dma_start(out=d[:], in_=s))
                            t_xnl = P.mark("sync", S["dXN"], 16)
                            P.wait("vector", t_xnl)
                            P.wait("vector", t_acc_done[ts])
                            P.wait("vector", t_ob_free[0])
                            ob = outb[0]
                            P.instr("vector",
                                    lambda E, d=ob, a=accb[:, ts * D:(ts + 1) * D], x2=xtf[sl]:
                                    E.scalar_tensor_tensor(d[:], a, 1.0, x2[:], ALU.mult, ALU.add,
                                                           accum_out=r2sm[:]))
                            t_r2 = P.mark("vector", S["sV"])
                            P.wait("scalar", t_r2)
                            P.instr("scalar",
                                    lambda E, s=ob, d=accb[:, ts * D:(ts + 1) * D]:
                                    E.activation(d, s[:], AF.Square, accum_out=q2sm[:]))
                            t_q2 = P.mark("scalar", S["sS"])
                            P.wait("vector", t_q2)
                            P.instr("vector",
                                    lambda E: E.tensor_scalar(l21[:], r2sm[:], 1.0 / D, None, ALU.mult))
                            P.instr("vector",
                                    lambda E: E.tensor_scalar(l22[:], q2sm[:], 1.0 / D, None, ALU.mult))
                            vsync()
                            P.instr("vector",
                                    lambda E: E.tensor_tensor(l23[:], l21[:], l21[:], ALU.mult))
                            vsync()
                            P.instr("vector",
                                    lambda E: E.tensor_tensor(l22[:], l22[:], l23[:], ALU.subtract))
                            t_v2 = P.mark("vector", S["sV"])
                            P.wait("scalar", t_v2)
                            P.instr("scalar",
                                    lambda E: E.activation(l22[:], l22[:], AF.Sqrt,
                                                           bias=c_eps[:], scale=1.0))
                            t_s2 = P.mark("scalar", S["sS"])
                            P.wait("vector", t_s2)
                            P.instr("vector", lambda E: E.reciprocal(l23[:], l22[:]))
                            vsync()
                            P.instr("vector",
                                    lambda E, d=xtf[sl], s=ob: E.tensor_scalar(d[:], s[:], l21[:],
                                                                               None, ALU.subtract))
                            P.instr("vector",
                                    lambda E, d=ob, s=xtf[sl]: E.tensor_scalar(d[:], s[:], l23[:],
                                                                               None, ALU.mult))
                            t_o = P.mark("vector", S["sV"])
                            t_xtf_free[sl] = t_o
                            P.wait("sync", t_o)
                            P.instr("sync",
                                    lambda E, d=y_out[seq, tg * 128:(tg + 1) * 128, :], s=ob:
                                    E.dma_start(out=d, in_=s[:]))
                            t_ob_free[0] = P.mark("sync", S["dOUT"], 16)
                            nonlocal_n_out[0] += 1

                        prev = None
                        for u in range(NU):
                            ig, ts = u // 8, u % 8
                            cur = emitA(u)
                            if (u == 0) or (ts == 1 and ig >= 1):
                                P.wait("gpsimd", t_wd_free)
                                P.instr("gpsimd",
                                        lambda E, d=wdb[:].rearrange("p (c n) -> p c n", n=2048),
                                        s=wd_v[:, 4 * ig:4 * ig + 4, :]:
                                        E.dma_start(out=d, in_=s))
                                t_wd = P.mark("gpsimd", S["dWF"], 16)
                                if ig + 1 < NIG:
                                    load_wgrp(ig + 1, (ig + 1) % 2,
                                              t_wg_free[(ig + 1) % 2], t_wg_free[(ig + 1) % 2])
                            if prev is not None:
                                emitRest(*prev)
                                if ts == 0:
                                    t_wg_free[(ig + 1) % 2] = prev[1]
                                    t_wd_free = t_down_done
                                if prev[0] // 8 == NIG - 1:
                                    emit_final(prev[0] % 8)
                            prev = (u, cur[0], cur[1])
                        emitRest(*prev)
                        emit_final(prev[0] % 8)
                        t_PC_free = [t_PC_free_l[0], t_PC_free_l[1]]
                        t_PA_free = t_psum_free[1]
                        t_PB_free = t_psum_free[0]

                        if DBG_DUMP_D and half == 0:
                            P.wait("sync", (S["sV"], S["sV"].v))
                            P.instr("sync", lambda E: E.dma_start(out=dbgd["acc"], in_=accb[:]))
                            P.mark("sync", S["dSO"], 16)
                            P.instr("sync", lambda E: E.dma_start(out=dbgd["xnfm"], in_=xnfm[:]))
                            P.mark("sync", S["dSO"], 16)
                        t_xn_consumed = (S["dXN"], S["dXN"].v)
                        P.flush(block)

                t_oscr_consumed = (S["dOS"], S["dOS"].v)

        n_out = nonlocal_n_out[0]
        for nm in ("dOUT", "dSO", "dSX"):
            if S[nm].v:
                P.wait("sync", (S[nm], S[nm].v))
        for nm in ("sP", "sV", "sS", "sG"):
            if S[nm].v:
                P.wait("sync", (S[nm], S[nm].v))
        P.flush(block)

    return nc


_NC_CACHE = None


def _get_nc():
    global _NC_CACHE
    if _NC_CACHE is None:
        _NC_CACHE = build_nc()
    return _NC_CACHE


def kernel(L0_attn_x, L0_attn_Wq, L0_attn_Wk, L0_attn_Wv, L0_attn_Wo,
           L0_ffn_w_gate, L0_ffn_w_up, L0_ffn_w_down):
    x = np.ascontiguousarray(np.asarray(L0_attn_x, dtype=np.float32))
    wq = np.ascontiguousarray(np.asarray(L0_attn_Wq, dtype=np.float32))
    wk = np.ascontiguousarray(np.asarray(L0_attn_Wk, dtype=np.float32))
    wv = np.ascontiguousarray(np.asarray(L0_attn_Wv, dtype=np.float32))
    wo = np.ascontiguousarray(np.asarray(L0_attn_Wo, dtype=np.float32))
    wg = np.ascontiguousarray(np.asarray(L0_ffn_w_gate, dtype=np.float32))
    wu = np.ascontiguousarray(np.asarray(L0_ffn_w_up, dtype=np.float32))
    wd = np.ascontiguousarray(np.asarray(L0_ffn_w_down, dtype=np.float32))

    c_id = np.eye(128, dtype=ml_dtypes.bfloat16)
    c_tril = (np.tril(np.ones((128, 128), dtype=np.float32)) * SCALE).astype(np.float32)
    c_eps = np.full((128, 1), EPS, dtype=np.float32)
    c_cnt2 = np.tile(np.array([float(T - (q + 1) * 128) for q in range(NQ)],
                              dtype=np.float32)[None, :], (128, 1))

    nc = _get_nc()
    in_maps = []
    for c in range(NCORES):
        in_maps.append({
            "x": x[c * BLOC:(c + 1) * BLOC],
            "wq": wq, "wk": wk, "wv": wv, "wo": wo,
            "wg": wg, "wu": wu, "wd": wd,
            "c_id": c_id, "c_tril": c_tril, "c_eps": c_eps, "c_cnt2": c_cnt2,
        })
    res = run_bass_kernel_spmd(nc, in_maps, core_ids=list(range(NCORES)))
    out = np.concatenate([r["y"] for r in res.results], axis=0)
    return out.astype(np.float32)


# version 17
# speedup vs baseline: 1.1378x; 1.0124x over previous
import contextlib
import numpy as np
import ml_dtypes
import concourse.bass as bass
import concourse.mybir as mybir
from concourse.bass_utils import run_bass_kernel_spmd

F32 = mybir.dt.float32
BF16 = mybir.dt.bfloat16
AF = mybir.ActivationFunctionType
ALU = mybir.AluOpType
AX = mybir.AxisListType

NCORES = 8
BLOC = 2
T = 2048
D = 2048
H = 16
II = 8192
NK = D // 128
NQ = T // 128
NC_I = II // 128
NIG = 16
SCALE = float(1.0 / np.sqrt(float(T)))
EPS = 1e-5
QOFF = [128 * (q * (q + 1) // 2) for q in range(NQ + 1)]
ATT_W = QOFF[NQ]

DBG_SEQS = None
DBG_HEADS = H
DBG_PHASES = "TABCD"
DBG_DUMP = False
DBG_DUMP_AB = False
DBG_DUMP_D = False


class Sem:
    __slots__ = ("h", "v")

    def __init__(self, h):
        self.h = h
        self.v = 0


class Prog:
    ENGINES = ("tensor", "vector", "scalar", "gpsimd", "sync")

    def __init__(self):
        self.q = {e: [] for e in self.ENGINES}
        self.lastwait = {e: {} for e in self.ENGINES}
        self.last = {e: None for e in self.ENGINES}

    def instr(self, eng, fn):
        ent = {"fn": fn, "inc": None, "amt": 0}
        self.q[eng].append(ent)
        self.last[eng] = ent

    def mark(self, eng, sem, amt=1):
        ent = self.last[eng]
        assert ent is not None and ent["inc"] is None
        sem.v += amt
        ent["inc"] = sem
        ent["amt"] = amt
        return (sem, sem.v)

    def wait(self, eng, tkt):
        if tkt is None:
            return
        if isinstance(tkt, (list, tuple)) and tkt and isinstance(tkt[0], (list, tuple, type(None))):
            for t in tkt:
                self.wait(eng, t)
            return
        sem, v = tkt
        key = id(sem)
        if self.lastwait[eng].get(key, -1) >= v:
            return
        self.lastwait[eng][key] = v
        ent = {"fn": None, "inc": None, "amt": 0, "wait": (sem.h, v)}
        self.q[eng].append(ent)
        self.last[eng] = None

    def flush(self, block):
        for eng in self.ENGINES:
            ents = self.q[eng]
            if not ents:
                continue
            dec = getattr(block, eng)

            def section(E, ents=ents):
                for ent in ents:
                    if "wait" in ent:
                        s, v = ent["wait"]
                        E.wait_ge(s, v)
                    elif ent["inc"] is not None:
                        ent["fn"](E).then_inc(ent["inc"].h, ent["amt"])
                    else:
                        ent["fn"](E)

            dec(section)
        self.q = {e: [] for e in self.ENGINES}
        self.last = {e: None for e in self.ENGINES}


def build_nc():
    nc = bass.Bass("TRN2", target_bir_lowering=False)

    x_in = nc.dram_tensor("x", [BLOC, T, D], F32, kind="ExternalInput").ap()
    wq_in = nc.dram_tensor("wq", [D, D], F32, kind="ExternalInput").ap()
    wk_in = nc.dram_tensor("wk", [D, D], F32, kind="ExternalInput").ap()
    wv_in = nc.dram_tensor("wv", [D, D], F32, kind="ExternalInput").ap()
    wo_in = nc.dram_tensor("wo", [D, D], F32, kind="ExternalInput").ap()
    wg_in = nc.dram_tensor("wg", [D, II], F32, kind="ExternalInput").ap()
    wu_in = nc.dram_tensor("wu", [D, II], F32, kind="ExternalInput").ap()
    wd_in = nc.dram_tensor("wd", [II, D], F32, kind="ExternalInput").ap()
    c_id_in = nc.dram_tensor("c_id", [128, 128], BF16, kind="ExternalInput").ap()
    c_tril_in = nc.dram_tensor("c_tril", [128, 128], F32, kind="ExternalInput").ap()
    c_eps_in = nc.dram_tensor("c_eps", [128, 1], F32, kind="ExternalInput").ap()
    c_cnt2_in = nc.dram_tensor("c_cnt2", [128, 16], F32, kind="ExternalInput").ap()
    y_out = nc.dram_tensor("y", [BLOC, T, D], F32, kind="ExternalOutput").ap()
    scr_kind = "ExternalOutput" if DBG_DUMP else "Internal"
    o_scr = nc.dram_tensor("o_scr", [H, 128, T], BF16, kind=scr_kind).ap()
    wg16 = nc.dram_tensor("wg16", [D, II], BF16, kind="Internal").ap()
    wu16 = nc.dram_tensor("wu16", [D, II], BF16, kind="Internal").ap()
    wd16 = nc.dram_tensor("wd16", [II, D], BF16, kind="Internal").ap()
    wo16 = nc.dram_tensor("wo16", [D, D], BF16, kind="Internal").ap()
    xn_scr = nc.dram_tensor("xn_scr", [8, 128, D], F32, kind=scr_kind).ap()
    if DBG_DUMP_D:
        dbgd = {nm: nc.dram_tensor(f"dbgd_{nm}", shp, dt, kind="ExternalOutput").ap()
                for nm, shp, dt in [
                    ("xnfm", [128, NK * 8 * 128], BF16), ("acc", [128, 8 * D], F32),
                    ("g", [128, 512], F32), ("h", [128, 512], F32),
                    ("a", [128, 512], F32), ("sec", [128, 128], F32),
                    ("msk", [128, 512], BF16), ("hs", [128, 512], BF16),
                    ("hsT", [128, 512], BF16), ("gu", [128, 1024], F32)]}
    if DBG_DUMP_AB:
        dbg_outs = {nm: nc.dram_tensor(f"dbg_{nm}", shp, dt, kind="ExternalOutput").ap()
                    for nm, shp, dt in [
                        ("qfm", [128, T], BF16), ("kfm", [128, T], BF16),
                        ("vfm", [128, T], BF16), ("vtm", [128, NK * 128], BF16),
                        ("sufft", [1, NQ * 128], BF16), ("alpha", [1, NQ * 128], BF16),
                        ("attn", [128, T], BF16), ("attnT", [128, ATT_W], BF16),
                        ("ebuf", [128, T], F32), ("xfm", [128, NK * T], BF16)]}

    wq_v = wq_in.rearrange("(k p) (h j) -> p k h j", p=128, j=128)
    wk_v = wk_in.rearrange("(k p) (h j) -> p k h j", p=128, j=128)
    wv_v = wv_in.rearrange("(k p) (h j) -> p k h j", p=128, j=128)
    wo_v = wo16.rearrange("(k p) n -> p k n", p=128)
    wg_v = wg16.rearrange("(k p) n -> p k n", p=128)
    wu_v = wu16.rearrange("(k p) n -> p k n", p=128)
    wd_v = wd16.rearrange("(c p) n -> p c n", p=128)
    o_scr_v = o_scr.rearrange("h p t -> p h t")

    es0 = contextlib.ExitStack()
    with es0:
        names = ["dC", "dX", "dX1", "dW", "dWO", "dWO1", "dOS", "dWF", "dWD", "dXN",
                 "dSO", "dSX", "dOUT", "dCA", "sP", "sV", "sS", "sG"]
        S = {n: Sem(es0.enter_context(nc.semaphore(n))) for n in names}

        c_id = es0.enter_context(nc.sbuf_tensor("c_id_s", [128, 128], BF16))
        c_tril = es0.enter_context(nc.sbuf_tensor("c_tril_s", [128, 128], F32))
        c_eps = es0.enter_context(nc.sbuf_tensor("c_eps_s", [128, 1], F32))
        c_cnt2 = es0.enter_context(nc.sbuf_tensor("c_cnt2_s", [128, 16], F32))

        P_A = es0.enter_context(nc.psum_tensor("P_A", [128, 1024], F32))
        P_B = es0.enter_context(nc.psum_tensor("P_B", [128, 1024], F32))
        P_C = es0.enter_context(nc.psum_tensor("P_C", [128, 1024], F32))
        P_T = es0.enter_context(nc.psum_tensor("P_T", [128, 2048], BF16))

        block = es0.enter_context(nc.Block())
        P = Prog()

        P.instr("sync", lambda E: E.dma_start(out=c_id[:], in_=c_id_in))
        P.mark("sync", S["dC"], 16)
        P.instr("sync", lambda E: E.dma_start(out=c_tril[:], in_=c_tril_in))
        P.mark("sync", S["dC"], 16)
        P.instr("sync", lambda E: E.dma_start(out=c_eps[:], in_=c_eps_in))
        P.mark("sync", S["dC"], 16)
        P.instr("sync", lambda E: E.dma_start(out=c_cnt2[:], in_=c_cnt2_in))
        t_consts = P.mark("sync", S["dC"], 16)

        t_PA_free = None
        t_PB_free = None
        t_PC_free = None
        t_PT_free = [None, None]
        t_oscr_stored = None
        t_oscr_consumed = None
        t_xn_stored = [None] * 8
        t_xn_consumed = None

        def transpose_group(src_sb, src_off, n_blk, dst_sb, dst_off, ps, t_src, extra_waits=()):
            nonlocal t_PT_free
            P.wait("tensor", t_src)
            for w in extra_waits:
                P.wait("tensor", w)
            P.wait("tensor", t_PT_free[ps])
            P.wait("tensor", t_consts)
            for j in range(n_blk):
                P.instr("tensor",
                        lambda E, d=P_T[:, ps * 1024 + j * 128:ps * 1024 + (j + 1) * 128],
                        s=src_sb[:, src_off + j * 128:src_off + (j + 1) * 128]:
                        E.transpose(d, s, c_id[:]))
            t_tr = P.mark("tensor", S["sP"])
            P.wait("vector", t_tr)
            P.instr("vector",
                    lambda E, d=dst_sb[:, dst_off:dst_off + n_blk * 128],
                    s=P_T[:, ps * 1024:ps * 1024 + n_blk * 128]:
                    E.tensor_copy(d, s))
            t_cp = P.mark("vector", S["sV"])
            t_PT_free[ps] = t_cp
            return t_cp

        def transpose_group_strided(src_sb, blocks, dst_view, ps, t_src):
            nonlocal t_PT_free
            P.wait("tensor", t_src)
            P.wait("tensor", t_PT_free[ps])
            P.wait("tensor", t_consts)
            for j, off in enumerate(blocks):
                P.instr("tensor",
                        lambda E, d=P_T[:, ps * 1024 + j * 128:ps * 1024 + (j + 1) * 128],
                        s=src_sb[:, off:off + 128]:
                        E.transpose(d, s, c_id[:]))
            t_tr = P.mark("tensor", S["sP"])
            P.wait("vector", t_tr)
            nb = len(blocks)
            P.instr("vector",
                    lambda E, d=dst_view,
                    s=P_T[:, ps * 1024:ps * 1024 + nb * 128].rearrange("p (k t) -> p k t", t=128):
                    E.tensor_copy(d, s))
            t_cp = P.mark("vector", S["sV"])
            t_PT_free[ps] = t_cp
            return t_cp

        def vsync():
            t = P.mark("vector", S["sV"])
            P.wait("vector", t)
            return t

        def barrier_all():
            for eng in Prog.ENGINES:
                for sem in S.values():
                    if sem.v:
                        P.wait(eng, (sem, sem.v))

        n_out = 0
        nonlocal_n_out = [0]

        cast_jobs = []
        for src, dst, nrows, rchunk in ((wo_in, wo16, D, 256), (wd_in, wd16, II, 1024),
                                        (wg_in, wg16, D, 256), (wu_in, wu16, D, 256)):
            for r0 in range(0, nrows, rchunk):
                cast_jobs.append((src, dst, r0, r0 + rchunk))

        def emit_casts(njobs):
            t = None
            for _ in range(njobs):
                if not cast_jobs:
                    break
                src, dst, r0, r1 = cast_jobs.pop(0)
                P.instr("gpsimd",
                        lambda E, s2=src[r0:r1, :], d=dst[r0:r1, :]: E.dma_start(out=d, in_=s2))
                t = P.mark("gpsimd", S["dCA"], 16)
            return t

        t_cast_all = None
        seqs = DBG_SEQS if DBG_SEQS is not None else list(range(BLOC))
        for seq in seqs:
            barrier_all()
            with contextlib.ExitStack() as esAB:
                sb = lambda name, shape, dt: esAB.enter_context(
                    nc.sbuf_tensor(f"{name}_{seq}", shape, dt))
                xt = [sb(f"xt{i}", [128, D], F32) for i in range(2)]
                xb = [sb(f"xb{i}", [128, D], BF16) for i in range(2)]
                xfm = sb("xfm", [128, NK * T], BF16)
                wsl = {w: sb(f"w{w}0", [128, NK * 128], BF16) for w in "qkv"}
                qfm = sb("qfm", [128, T], BF16)
                kfm = sb("kfm", [128, T], BF16)
                vfm = sb("vfm", [128, T], BF16)
                vtm = sb("vtm", [128, NK * 128], BF16)
                cs_c = sb("cs", [128, 16], F32)
                suf_c = sb("suf", [128, 16], F32)
                suf_b = sb("sufb", [128, 16], BF16)
                sufft = sb("sufft", [1, NQ * 128], BF16)
                alpha = sb("alpha", [1, NQ * 128], BF16)
                e_pack = sb("epack", [128, ATT_W], BF16)
                zdm = sb("zdm", [128, 256], F32)
                rs_all = sb("rsall", [128, 96], F32)
                rsum_all = sb("rsumall", [128, 16], F32)
                rden_all = sb("rdenall", [128, 16], F32)
                rdb_all = sb("rdball", [128, 16], BF16)
                attnT = sb("attnT", [128, ATT_W], BF16)
                ofm = sb("ofm", [128, T], BF16)

                t_xt_free = [None, None]
                t_xb_free = [None, None]
                t_ofm_free = None
                t_epack_free = None

                for w, wvv in (("q", wq_v), ("k", wk_v), ("v", wv_v)):
                    P.instr("gpsimd",
                            lambda E, d=wsl[w], s=wvv[:, :, 0, :]:
                            E.dma_start(out=d[:].rearrange("p (k j) -> p k j", j=128), in_=s))
                    P.mark("gpsimd", S["dW"], 16)

                xfm_r = xfm[:].rearrange("p (k t) -> p k t", t=T)
                t_xfm = None
                for i in range(NQ):
                    sl = i % 2
                    P.wait("sync", t_xt_free[sl])
                    P.instr("sync",
                            lambda E, d=xt[sl], s=x_in[seq, i * 128:(i + 1) * 128, :]:
                            E.dma_start(out=d[:], in_=s))
                    t_ld = P.mark("sync", S["dX"] if sl == 0 else S["dX1"], 16)
                    P.wait("vector", t_ld)
                    P.wait("vector", t_xb_free[sl])
                    P.instr("vector", lambda E, d=xb[sl], s=xt[sl]: E.tensor_copy(d[:], s[:]))
                    t_cast = P.mark("vector", S["sV"])
                    t_xt_free[sl] = t_cast
                    for g in range(4):
                        t_xfm = transpose_group_strided(
                            xb[sl], [(4 * g + j) * 128 for j in range(4)],
                            xfm_r[:, 4 * g:4 * g + 4, i * 128:(i + 1) * 128],
                            g % 2, t_cast)
                    t_xb_free[sl] = t_xfm

                t_cs_prev = None
                t_head = {}

                def fn_proj(h):
                    nonlocal t_PA_free, t_PC_free, t_cs_prev
                    t_w_ready = (S["dW"], 16 * 3 * (h + 1))

                    def proj(dst_sb, wbuf, pt, t_pfree, extra_evac_wait=None):
                        t_evac = None
                        for half in range(2):
                            P.wait("tensor", t_w_ready)
                            P.wait("tensor", t_xfm)
                            P.wait("tensor", t_pfree if half == 0 else t_evac)
                            for k in range(NK):
                                for s2 in range(2):
                                    P.instr("tensor",
                                            lambda E, o=pt[:, s2 * 512:(s2 + 1) * 512],
                                            l=wbuf[:, k * 128:(k + 1) * 128],
                                            r=xfm[:, k * T + half * 1024 + s2 * 512:
                                                  k * T + half * 1024 + (s2 + 1) * 512],
                                            st=(k == 0), sp=(k == NK - 1):
                                            E.matmul(o, l, r, start=st, stop=sp))
                            t_mm = P.mark("tensor", S["sP"])
                            P.wait("scalar", t_mm)
                            if extra_evac_wait is not None:
                                P.wait("scalar", extra_evac_wait)
                            P.instr("scalar",
                                    lambda E, d=dst_sb[:, half * 1024:(half + 1) * 1024], s=pt:
                                    E.copy(out=d, in_=s[:]))
                            t_evac = P.mark("scalar", S["sS"])
                        return t_evac

                    t_q = proj(qfm, wsl["q"], P_C, t_PC_free)
                    t_k = proj(kfm, wsl["k"], P_A, t_PA_free)
                    t_v = proj(vfm, wsl["v"], P_C, t_q, extra_evac_wait=t_cs_prev)
                    t_PA_free = t_k
                    t_PC_free = t_v
                    t_head[h] = {"q": t_q, "k": t_k, "v": t_v}
                    if h + 1 < DBG_HEADS:
                        for w, wvv, tk in (("q", wq_v, t_q), ("k", wk_v, t_k), ("v", wv_v, t_v)):
                            P.wait("gpsimd", tk)
                            P.instr("gpsimd",
                                    lambda E, d=wsl[w], s=wvv[:, :, h + 1, :]:
                                    E.dma_start(out=d[:].rearrange("p (k j) -> p k j", j=128), in_=s))
                            P.mark("gpsimd", S["dW"], 16)

                def fn_vprep(h):
                    nonlocal t_cs_prev
                    t_v = t_head[h]["v"]
                    t_vtm = None
                    for g in range(4):
                        t_vtm = transpose_group(vfm, g * 512, 4, vtm, g * 512, g % 2, t_v)
                    t_head[h]["vtm"] = t_vtm
                    P.wait("vector", t_v)
                    t_cs = None
                    for j in range(NK):
                        P.instr("vector",
                                lambda E, d=cs_c[:, j:j + 1], s=vfm[:, j * 128:(j + 1) * 128]:
                                E.reduce_sum(d, s, axis=AX.X))
                        t_cs = P.mark("vector", S["sV"])
                    P.wait("vector", t_cs)
                    for q in range(NQ - 1):
                        P.instr("vector",
                                lambda E, d=suf_c[:, q:q + 1], a=cs_c[:, q + 1:16]:
                                E.reduce_sum(d, a, axis=AX.X))
                    vsync()
                    P.instr("vector", lambda E: E.tensor_copy(suf_b[:], suf_c[:]))
                    t_suf = P.mark("vector", S["sV"])
                    t_cs_prev = t_suf
                    for q in range(NQ - 1):
                        ps = q % 2
                        P.wait("tensor", t_suf)
                        P.wait("tensor", t_PT_free[ps])
                        P.instr("tensor",
                                lambda E, d=P_T[0:1, ps * 1024 + 512:ps * 1024 + 640],
                                s=suf_b[:, q:q + 1]:
                                E.transpose(d, s, c_id[:]))
                        t_tr = P.mark("tensor", S["sP"])
                        P.wait("vector", t_tr)
                        P.instr("vector",
                                lambda E, d=sufft[0:1, q * 128:(q + 1) * 128],
                                s=P_T[0:1, ps * 1024 + 512:ps * 1024 + 640]:
                                E.tensor_copy(d, s))
                        t_PT_free[ps] = P.mark("vector", S["sV"])

                def fn_S1(h):
                    nonlocal t_PA_free
                    t_q = t_head[h]["q"]; t_k = t_head[h]["k"]
                    t_PA_slot = [t_PA_free, t_PA_free]
                    t_zde_q = [None] * NQ
                    t_exps_all = []
                    for q in range(NQ):
                        par = q % 2
                        seg_bounds = []
                        c0 = 0
                        while c0 < q * 128:
                            c1 = min(c0 + 512, q * 128)
                            seg_bounds.append((c0, c1))
                            c0 = c1
                        used = 0
                        nseg = len(seg_bounds)
                        for si in range(nseg + 1):
                            ps = si % 2
                            P.wait("tensor", t_q)
                            P.wait("tensor", t_k)
                            P.wait("tensor", t_PA_slot[ps])
                            if si < nseg:
                                c0, c1 = seg_bounds[si]
                                P.instr("tensor",
                                        lambda E, o=P_A[:, ps * 512:ps * 512 + (c1 - c0)],
                                        l=qfm[:, q * 128:(q + 1) * 128], r=kfm[:, c0:c1]:
                                        E.matmul(o, l, r, start=True, stop=True))
                                t_z = P.mark("tensor", S["sP"])
                                P.wait("scalar", t_z)
                                P.wait("scalar", t_epack_free)
                                P.instr("scalar",
                                        lambda E, d=e_pack[:, QOFF[q] + c0:QOFF[q] + c1],
                                        s=P_A[:, ps * 512:ps * 512 + (c1 - c0)],
                                        a=rs_all[:, q * 6 + used:q * 6 + used + 1]:
                                        E.activation(d, s, AF.Exp, scale=SCALE, accum_out=a))
                                t_e = P.mark("scalar", S["sS"])
                                t_PA_slot[ps] = t_e
                            else:
                                P.instr("tensor",
                                        lambda E, o=P_A[:, ps * 512:ps * 512 + 128],
                                        l=qfm[:, q * 128:(q + 1) * 128],
                                        r=kfm[:, q * 128:(q + 1) * 128]:
                                        E.matmul(o, l, r, start=True, stop=True))
                                t_z = P.mark("tensor", S["sP"])
                                P.wait("vector", t_z)
                                P.wait("vector", t_consts)
                                if q >= 2:
                                    P.wait("vector", t_zde_q[q - 2])
                                P.instr("vector",
                                        lambda E, d=zdm[:, par * 128:(par + 1) * 128],
                                        s=P_A[:, ps * 512:ps * 512 + 128]:
                                        E.tensor_tensor(d, s, c_tril[:], ALU.mult))
                                t_zd = P.mark("vector", S["sV"])
                                t_PA_slot[ps] = t_zd
                                P.wait("scalar", t_zd)
                                P.wait("scalar", t_epack_free)
                                P.instr("scalar",
                                        lambda E, d=e_pack[:, QOFF[q] + q * 128:QOFF[q] + (q + 1) * 128],
                                        sz=zdm[:, par * 128:(par + 1) * 128],
                                        a=rs_all[:, q * 6 + used:q * 6 + used + 1]:
                                        E.activation(d, sz, AF.Exp, scale=1.0, accum_out=a))
                                t_e = P.mark("scalar", S["sS"])
                                t_zde_q[q] = t_e
                            used += 1
                        t_exps_all.append((t_e, used))
                    t_PA_free = [t_PA_slot[0], t_PA_slot[1]]
                    t_head[h]["exps"] = t_exps_all

                def fn_S2S3(h):
                    nonlocal t_epack_free
                    t_exps_all = t_head[h]["exps"]
                    t_alpha_done = [None] * NQ
                    t_attnT_done = [None] * NQ
                    for q in range(NQ):
                        P.wait("vector", t_exps_all[q][0])
                    for q in range(NQ):
                        P.instr("vector",
                                lambda E, q=q, u=t_exps_all[q][1]:
                                E.reduce_sum(rsum_all[:, q:q + 1],
                                             rs_all[:, q * 6:q * 6 + u], axis=AX.X))
                    vsync()
                    P.instr("vector",
                            lambda E: E.tensor_tensor(rsum_all[:], rsum_all[:], c_cnt2[:], ALU.add))
                    vsync()
                    P.instr("vector", lambda E: E.reciprocal(rden_all[:], rsum_all[:]))
                    vsync()
                    P.instr("vector", lambda E: E.tensor_copy(rdb_all[:], rden_all[:]))
                    t_rdb = P.mark("vector", S["sV"])
                    for q in range(NQ):
                        psr = q % 2
                        P.wait("tensor", t_rdb)
                        P.wait("tensor", t_PT_free[psr])
                        P.instr("tensor",
                                lambda E, d=P_T[0:1, psr * 1024 + 512:psr * 1024 + 640],
                                s=rdb_all[:, q:q + 1]:
                                E.transpose(d, s, c_id[:]))
                        t_tra = P.mark("tensor", S["sP"])
                        P.wait("vector", t_tra)
                        P.instr("vector",
                                lambda E, d=alpha[0:1, q * 128:(q + 1) * 128],
                                s=P_T[0:1, psr * 1024 + 512:psr * 1024 + 640]:
                                E.tensor_copy(d, s))
                        t_PT_free[psr] = P.mark("vector", S["sV"])
                        t_alpha_done[q] = t_PT_free[psr]
                    for q in range(NQ):
                        P.wait("vector", t_rdb)
                        P.instr("vector",
                                lambda E, q=q, w=(q + 1) * 128:
                                E.tensor_scalar(e_pack[:, QOFF[q]:QOFF[q] + w],
                                                e_pack[:, QOFF[q]:QOFF[q] + w],
                                                rden_all[:, q:q + 1], None, ALU.mult))
                        t_nrm = P.mark("vector", S["sV"])
                        t_last = None
                        for g in range((q + 1 + 3) // 4):
                            nj = min(4, q + 1 - 4 * g)
                            t_last = transpose_group(e_pack, QOFF[q] + g * 512, nj, attnT,
                                                     QOFF[q] + g * 512, g % 2, t_nrm)
                        t_attnT_done[q] = t_last
                    t_epack_free = (S["sV"], S["sV"].v)
                    t_head[h]["alpha"] = t_alpha_done
                    t_head[h]["attnT"] = t_attnT_done

                def fn_AV(h):
                    nonlocal t_PB_free, t_ofm_free
                    t_vtm = t_head[h]["vtm"]
                    t_alpha_done = t_head[h]["alpha"]
                    t_attnT_done = t_head[h]["attnT"]
                    t_evO = [None, None]
                    for th in range(2):
                        qb = th * 8
                        P.wait("tensor", t_vtm)
                        for q in range(qb, qb + 8):
                            P.wait("tensor", t_attnT_done[q])
                            P.wait("tensor", t_alpha_done[q])
                        P.wait("tensor", t_PB_free)
                        for j in range(qb + 8):
                            for q in range(max(j, qb), qb + 8):
                                P.instr("tensor",
                                        lambda E, o=P_B[:, (q - qb) * 128:(q - qb + 1) * 128],
                                        l=vtm[:, j * 128:(j + 1) * 128],
                                        r=attnT[:, QOFF[q] + j * 128:QOFF[q] + (j + 1) * 128],
                                        st=(j == 0 and (q - qb) % 4 == 0),
                                        sp=(j == NQ - 1 and q == NQ - 1):
                                        E.matmul(o, l, r, start=st, stop=sp))
                        for q in range(qb, qb + 8):
                            if q == NQ - 1:
                                continue
                            P.instr("tensor",
                                    lambda E, o=P_B[:, (q - qb) * 128:(q - qb + 1) * 128],
                                    l=sufft[0:1, q * 128:(q + 1) * 128],
                                    r=alpha[0:1, q * 128:(q + 1) * 128]:
                                    E.matmul(o, l, r, start=False, stop=True))
                        t_av = P.mark("tensor", S["sP"])
                        P.wait("scalar", t_av)
                        P.instr("scalar",
                                lambda E, d=ofm[:, th * 1024:(th + 1) * 1024]:
                                E.copy(out=d, in_=P_B[:]))
                        t_evO[th] = P.mark("scalar", S["sS"])
                        t_PB_free = t_evO[th]
                    P.wait("sync", t_evO[0])
                    P.wait("sync", t_evO[1])
                    P.wait("sync", t_ofm_free)
                    P.wait("sync", t_oscr_consumed)
                    P.instr("sync",
                            lambda E, d=o_scr[h, :, :], s=ofm: E.dma_start(out=d, in_=s[:]))
                    t_ofm_free = P.mark("sync", S["dSO"], 16)

                if "A" in DBG_PHASES and DBG_HEADS > 0:
                    fn_proj(0)
                    emit_casts(2)
                    fn_vprep(0)
                    for h in range(DBG_HEADS):
                        fn_S1(h)
                        if h + 1 < DBG_HEADS:
                            fn_proj(h + 1)
                        emit_casts(2)
                        fn_S2S3(h)
                        fn_AV(h)
                        if h + 1 < DBG_HEADS:
                            fn_vprep(h + 1)
                    emit_casts(len(cast_jobs))
                    t_cast_all = (S["dCA"], S["dCA"].v)

                t_oscr_stored = (S["dSO"], S["dSO"].v)
                P.flush(block)

            if "C" not in DBG_PHASES:
                continue
            with contextlib.ExitStack() as esCD:
                sb2 = lambda name, shape, dt: esCD.enter_context(
                    nc.sbuf_tensor(f"{name}_{seq}", shape, dt))
                xnfm = sb2("xnfm", [128, NK * 8 * 128], BF16)
                accb = sb2("acc", [128, 8 * D], F32)

                for half in range(2):
                    barrier_all()
                    with contextlib.ExitStack() as esC:
                        sb3 = lambda name, shape, dt: esC.enter_context(
                            nc.sbuf_tensor(f"{name}_{seq}_{half}", shape, dt))
                        wo_sb = sb3("wo_sb", [128, NK * 2048], BF16)
                        osl = sb3("osl", [128, NK * 128], BF16)
                        xct = [sb3(f"xct{i}", [128, D], F32) for i in range(2)]
                        r_b = sb3("r_b", [128, D], F32)
                        xn_b = sb3("xn_b", [128, D], F32)
                        xnbb = sb3("xnbb", [128, D], BF16)
                        rsm = sb3("rsm", [128, 2], F32)
                        sqm = sb3("sqm", [128, 2], F32)
                        ln1 = sb3("ln1", [128, 1], F32)
                        ln2 = sb3("ln2", [128, 1], F32)
                        ln3 = sb3("ln3", [128, 1], F32)

                        P.wait("gpsimd", t_oscr_stored)
                        P.wait("gpsimd", t_cast_all)
                        wo_r = wo_sb[:].rearrange("p (k n) -> p k n", n=2048)
                        P.instr("gpsimd",
                                lambda E: E.dma_start(out=wo_r[:, :, 0:1024], in_=wo_v[:, :, 0:1024]))
                        t_wo0 = P.mark("gpsimd", S["dWO"], 16)
                        P.instr("gpsimd",
                                lambda E: E.dma_start(out=wo_r[:, :, 1024:2048], in_=wo_v[:, :, 1024:2048]))
                        t_wo1 = P.mark("gpsimd", S["dWO1"], 16)
                        t_wo_h = [t_wo0, t_wo1]

                        t_osl_free = None
                        t_xct_free = [None, None]
                        t_sq_prev = None
                        for ts in range(8):
                            tg = half * 8 + ts
                            P.wait("sync", t_oscr_stored)
                            P.wait("sync", t_osl_free)
                            P.instr("sync",
                                    lambda E, d=osl[:].rearrange("p (k j) -> p k j", j=128),
                                    s=o_scr_v[:, :, tg * 128:(tg + 1) * 128]:
                                    E.dma_start(out=d, in_=s))
                            t_os = P.mark("sync", S["dOS"], 16)
                            sl = ts % 2
                            P.wait("sync", t_xct_free[sl])
                            P.instr("sync",
                                    lambda E, d=xct[sl], s=x_in[seq, tg * 128:(tg + 1) * 128, :]:
                                    E.dma_start(out=d[:], in_=s))
                            t_xl = P.mark("sync", S["dX"] if sl == 0 else S["dX1"], 16)
                            t_r = [None, None]
                            for ddh in range(2):
                                pt = P_A if ddh == 0 else P_C
                                P.wait("tensor", t_wo_h[ddh])
                                P.wait("tensor", t_os)
                                P.wait("tensor", t_PA_free if ddh == 0 else t_PC_free)
                                for k in range(NK):
                                    for s2 in range(2):
                                        P.instr("tensor",
                                                lambda E, o=pt[:, s2 * 512:(s2 + 1) * 512],
                                                l=osl[:, k * 128:(k + 1) * 128],
                                                r=wo_sb[:, k * 2048 + ddh * 1024 + s2 * 512:
                                                      k * 2048 + ddh * 1024 + (s2 + 1) * 512],
                                                st=(k == 0), sp=(k == NK - 1):
                                                E.matmul(o, l, r, start=st, stop=sp))
                                t_mm = P.mark("tensor", S["sP"])
                                P.wait("vector", t_mm)
                                P.wait("vector", t_xl)
                                P.wait("vector", t_sq_prev)
                                P.instr("vector",
                                        lambda E, d=r_b[:, ddh * 1024:(ddh + 1) * 1024],
                                        x2=xct[sl][:, ddh * 1024:(ddh + 1) * 1024],
                                        a=rsm[:, ddh:ddh + 1], sp2=pt:
                                        E.scalar_tensor_tensor(d, sp2[:], 1.0, x2,
                                                               ALU.mult, ALU.add, accum_out=a))
                                t_r[ddh] = P.mark("vector", S["sV"])
                                if ddh == 0:
                                    t_PA_free = t_r[0]
                                else:
                                    t_PC_free = t_r[1]
                            t_osl_free = t_r[1]
                            t_xct_free[sl] = t_r[1]
                            for ddh in range(2):
                                P.wait("scalar", t_r[ddh])
                                P.instr("scalar",
                                        lambda E, d=xn_b[:, ddh * 1024:(ddh + 1) * 1024],
                                        s=r_b[:, ddh * 1024:(ddh + 1) * 1024],
                                        a=sqm[:, ddh:ddh + 1]:
                                        E.activation(d, s, AF.Square, accum_out=a))
                            t_sq = P.mark("scalar", S["sS"])
                            t_sq_prev = t_sq
                            P.wait("vector", t_sq)
                            P.instr("vector",
                                    lambda E: E.tensor_tensor(ln1[:], rsm[:, 0:1], rsm[:, 1:2], ALU.add))
                            vsync()
                            P.instr("vector",
                                    lambda E: E.tensor_scalar(ln1[:], ln1[:], 1.0 / D, None, ALU.mult))
                            P.instr("vector",
                                    lambda E: E.tensor_tensor(ln2[:], sqm[:, 0:1], sqm[:, 1:2], ALU.add))
                            vsync()
                            P.instr("vector",
                                    lambda E: E.tensor_scalar(ln2[:], ln2[:], 1.0 / D, None, ALU.mult))
                            vsync()
                            P.instr("vector",
                                    lambda E: E.tensor_tensor(ln3[:], ln1[:], ln1[:], ALU.mult))
                            vsync()
                            P.instr("vector",
                                    lambda E: E.tensor_tensor(ln2[:], ln2[:], ln3[:], ALU.subtract))
                            t_var = P.mark("vector", S["sV"])
                            P.wait("scalar", t_var)
                            P.instr("scalar",
                                    lambda E: E.activation(ln2[:], ln2[:], AF.Sqrt,
                                                           bias=c_eps[:], scale=1.0))
                            t_sqr = P.mark("scalar", S["sS"])
                            P.wait("vector", t_sqr)
                            P.instr("vector", lambda E: E.reciprocal(ln3[:], ln2[:]))
                            vsync()
                            P.instr("vector",
                                    lambda E: E.tensor_scalar(xn_b[:], r_b[:], ln1[:], None,
                                                              ALU.subtract))
                            P.instr("vector",
                                    lambda E: E.tensor_scalar(r_b[:], xn_b[:], ln3[:], None,
                                                              ALU.mult))
                            P.instr("vector", lambda E: E.tensor_copy(xnbb[:], r_b[:]))
                            t_xnb = P.mark("vector", S["sV"])
                            P.wait("sync", t_xnb)
                            P.wait("sync", t_xn_consumed)
                            P.instr("sync",
                                    lambda E, d=xn_scr[ts, :, :]: E.dma_start(out=d, in_=r_b[:]))
                            t_xn_stored[ts] = P.mark("sync", S["dSX"], 16)
                            xnfm_r = xnfm[:].rearrange("p (k t) -> p k t", t=8 * 128)
                            for g in range(4):
                                transpose_group_strided(
                                    xnbb, [(4 * g + j) * 128 for j in range(4)],
                                    xnfm_r[:, 4 * g:4 * g + 4, ts * 128:(ts + 1) * 128],
                                    g % 2, t_xnb)
                        t_xnfm = t_PT_free[1] if t_PT_free[1] else t_PT_free[0]
                        t_xnfm = (S["sV"], S["sV"].v)
                        P.flush(block)

                    if "D" not in DBG_PHASES:
                        continue
                    barrier_all()
                    with contextlib.ExitStack() as esD:
                        sb4 = lambda name, shape, dt: esD.enter_context(
                            nc.sbuf_tensor(f"{name}_{seq}_{half}", shape, dt))
                        wgb = [sb4(f"wgb{i}", [128, NK * 512], BF16) for i in range(2)]
                        wub = [sb4(f"wub{i}", [128, NK * 512], BF16) for i in range(2)]
                        wdb = sb4("wdb", [128, 4 * 2048], BF16)
                        g_b = sb4("g_b", [128, 512], F32)
                        h_b = sb4("h_b", [128, 512], F32)
                        a_b = sb4("a_b", [128, 512], F32)
                        tA = sb4("tA", [128, 128], F32)
                        tB = sb4("tB", [128, 128], F32)
                        tC2 = sb4("tC2", [128, 128], F32)
                        tD2 = sb4("tD2", [128, 128], F32)
                        sec = sb4("sec", [128, 128], F32)
                        msk = sb4("msk", [128, 512], BF16)
                        hs_b = sb4("hs_b", [128, 512], BF16)
                        hsT = sb4("hsT", [128, 512], BF16)
                        xtf = [sb4("xtf0", [128, D], F32)]
                        outb = [sb4("outb0", [128, D], F32)]
                        r2sm = sb4("r2sm", [128, 1], F32)
                        q2sm = sb4("q2sm", [128, 1], F32)
                        l21 = sb4("l21", [128, 1], F32)
                        l22 = sb4("l22", [128, 1], F32)
                        l23 = sb4("l23", [128, 1], F32)

                        a_g = a_b[:].rearrange("p (g four) -> p g four", four=4)
                        h_g = h_b[:].rearrange("p (g four) -> p g four", four=4)
                        m_g = msk[:].rearrange("p (g four) -> p g four", four=4)

                        t_wgb_ready = [None, None]

                        def load_wgrp(ig, buf_i, t_free_g, t_free_u):
                            P.wait("gpsimd", t_free_g)
                            P.instr("gpsimd",
                                    lambda E, d=wgb[buf_i][:].rearrange("p (k n) -> p k n", n=512),
                                    s=wg_v[:, :, ig * 512:(ig + 1) * 512]:
                                    E.dma_start(out=d, in_=s))
                            P.mark("gpsimd", S["dWF"], 16)
                            P.wait("gpsimd", t_free_u)
                            P.instr("gpsimd",
                                    lambda E, d=wub[buf_i][:].rearrange("p (k n) -> p k n", n=512),
                                    s=wu_v[:, :, ig * 512:(ig + 1) * 512]:
                                    E.dma_start(out=d, in_=s))
                            t_wgb_ready[buf_i] = P.mark("gpsimd", S["dWF"], 16)
                            return t_wgb_ready[buf_i]

                        t_wg_free = [None, None]
                        load_wgrp(0, 0, None, None)
                        t_wd_free = None
                        t_wd = None
                        t_msk_prev = None
                        t_hs_prev = None
                        t_hs_free = None
                        t_hsT_free = None
                        t_acc_done = [None] * 8
                        t_psum_free = [t_PB_free, t_PA_free]
                        t_g_free = None
                        t_down_done = None
                        NU = NIG * 8

                        def emitA(u):
                            nonlocal t_wd, t_wd_free
                            ig, ts = u // 8, u % 8
                            bi = ig % 2
                            par = u % 2
                            pt = P_B if par == 0 else P_A
                            P.wait("tensor", t_wgb_ready[bi])
                            P.wait("tensor", t_xnfm)
                            P.wait("tensor", t_psum_free[par])
                            for k in range(NK):
                                P.instr("tensor",
                                        lambda E, o=pt[:, 0:512],
                                        l=xnfm[:, k * 1024 + ts * 128:k * 1024 + (ts + 1) * 128],
                                        r=wgb[bi][:, k * 512:(k + 1) * 512],
                                        st=(k == 0), sp=(k == NK - 1):
                                        E.matmul(o, l, r, start=st, stop=sp))
                                P.instr("tensor",
                                        lambda E, o=pt[:, 512:1024],
                                        l=xnfm[:, k * 1024 + ts * 128:k * 1024 + (ts + 1) * 128],
                                        r=wub[bi][:, k * 512:(k + 1) * 512],
                                        st=(k == 0), sp=(k == NK - 1):
                                        E.matmul(o, l, r, start=st, stop=sp))
                            return P.mark("tensor", S["sP"]), pt

                        def emitRest(u, t_gu, pt):
                            nonlocal t_msk_prev, t_hs_prev, t_hs_free, t_hsT_free
                            nonlocal t_psum_free, t_g_free, t_down_done, t_wd_free
                            ig, ts = u // 8, u % 8
                            par = u % 2
                            P.wait("scalar", t_gu)
                            P.wait("scalar", t_g_free)
                            P.instr("scalar",
                                    lambda E: E.activation(g_b[:], pt[:, 0:512], AF.Silu))
                            t_si = P.mark("scalar", S["sS"])
                            P.wait("vector", t_si)
                            P.instr("vector",
                                    lambda E: E.tensor_tensor(h_b[:], g_b[:], pt[:, 512:1024],
                                                              ALU.mult))
                            t_h = P.mark("vector", S["sV"])
                            t_psum_free[par] = t_h
                            t_g_free = t_h
                            P.wait("scalar", t_h)
                            P.wait("scalar", t_msk_prev)
                            P.instr("scalar",
                                    lambda E: E.activation(a_b[:], h_b[:], AF.Square))
                            t_a = P.mark("scalar", S["sS"])
                            P.wait("vector", t_a)
                            P.instr("vector",
                                    lambda E: E.tensor_tensor(tA[:], a_g[:, :, 0], a_g[:, :, 1], ALU.max))
                            P.instr("vector",
                                    lambda E: E.tensor_tensor(tB[:], a_g[:, :, 2], a_g[:, :, 3], ALU.max))
                            P.instr("vector",
                                    lambda E: E.tensor_tensor(tC2[:], a_g[:, :, 0], a_g[:, :, 1], ALU.min))
                            P.instr("vector",
                                    lambda E: E.tensor_tensor(tD2[:], a_g[:, :, 2], a_g[:, :, 3], ALU.min))
                            P.instr("vector",
                                    lambda E: E.tensor_tensor(tA[:], tA[:], tB[:], ALU.min))
                            P.instr("vector",
                                    lambda E: E.tensor_tensor(tC2[:], tC2[:], tD2[:], ALU.max))
                            P.instr("vector",
                                    lambda E: E.tensor_tensor(sec[:], tA[:], tC2[:], ALU.max))
                            for j4 in range(4):
                                P.instr("vector",
                                        lambda E, j4=j4:
                                        E.tensor_tensor(m_g[:, :, j4], a_g[:, :, j4], sec[:],
                                                        ALU.is_ge))
                            t_msk = P.mark("vector", S["sV"])
                            t_msk_prev = t_msk
                            P.wait("vector", t_hs_free)
                            P.instr("vector",
                                    lambda E: E.tensor_tensor(hs_b[:], h_b[:], msk[:], ALU.mult))
                            t_hs = P.mark("vector", S["sV"])
                            t_hs_prev = t_hs
                            if DBG_DUMP_D and ig == 0 and ts == 0:
                                P.wait("vector", t_gu)
                                P.instr("vector", lambda E: E.tensor_copy(xtf[0][:, 0:1024], pt[:]))
                                t_dgu = P.mark("vector", S["sV"])
                                for nm, src in [("g", g_b), ("h", h_b), ("a", a_b),
                                                ("sec", sec), ("msk", msk), ("hs", hs_b)]:
                                    P.wait("sync", t_hs)
                                    P.wait("sync", t_dgu)
                                    P.instr("sync",
                                            lambda E, d=dbgd[nm], s2=src:
                                            E.dma_start(out=d, in_=s2[:]))
                                    P.mark("sync", S["dSO"], 16)
                                P.instr("sync", lambda E: E.dma_start(out=dbgd["gu"], in_=xtf[0][:, 0:1024]))
                                t_dgu2 = P.mark("sync", S["dSO"], 16)
                                P.wait("sync", t_dgu2)
                            t_hsT = transpose_group(hs_b, 0, 4, hsT, 0, ig % 2, t_hs)
                            t_hs_free = t_hsT
                            if DBG_DUMP_D and ig == 0 and ts == 0:
                                P.wait("sync", t_hsT)
                                P.instr("sync", lambda E: E.dma_start(out=dbgd["hsT"], in_=hsT[:]))
                                P.mark("sync", S["dSO"], 16)
                            t_ac = None
                            for qt in range(4):
                                bk = qt % 2
                                P.wait("tensor", t_hsT)
                                P.wait("tensor", t_wd)
                                P.wait("tensor", t_PC_free_l[bk])
                                for c in range(4):
                                    P.instr("tensor",
                                            lambda E, o=P_C[:, bk * 512:(bk + 1) * 512],
                                            l=hsT[:, c * 128:(c + 1) * 128],
                                            r=wdb[:, c * 2048 + qt * 512:
                                                  c * 2048 + (qt + 1) * 512],
                                            st=(c == 0), sp=(c == 3):
                                            E.matmul(o, l, r, start=st, stop=sp))
                                t_dn = P.mark("tensor", S["sP"])
                                P.wait("vector", t_dn)
                                if ig == 0:
                                    P.instr("vector",
                                            lambda E, d=accb[:, ts * D + qt * 512:
                                                           ts * D + (qt + 1) * 512],
                                            sp2=P_C[:, bk * 512:(bk + 1) * 512]:
                                            E.tensor_copy(d, sp2))
                                else:
                                    P.instr("vector",
                                            lambda E, d=accb[:, ts * D + qt * 512:
                                                           ts * D + (qt + 1) * 512],
                                            sp2=P_C[:, bk * 512:(bk + 1) * 512]:
                                            E.tensor_tensor(d, d, sp2, ALU.add))
                                t_ac = P.mark("vector", S["sV"])
                                t_PC_free_l[bk] = t_ac
                            if ig == NIG - 1:
                                t_acc_done[ts] = t_ac
                            t_down_done = t_dn

                        t_PC_free_l = [t_PC_free, t_PC_free]

                        t_xtf_free = [None]
                        t_ob_free = [None]

                        def emit_final(ts):
                            tg = half * 8 + ts
                            sl = 0
                            P.wait("sync", (S["dSX"], S["dSX"].v))
                            P.wait("sync", t_xtf_free[sl])
                            P.instr("sync",
                                    lambda E, d=xtf[sl], s=xn_scr[ts, :, :]:
                                    E.dma_start(out=d[:], in_=s))
                            t_xnl = P.mark("sync", S["dXN"], 16)
                            P.wait("vector", t_xnl)
                            P.wait("vector", t_acc_done[ts])
                            P.wait("vector", t_ob_free[0])
                            ob = outb[0]
                            P.instr("vector",
                                    lambda E, d=ob, a=accb[:, ts * D:(ts + 1) * D], x2=xtf[sl]:
                                    E.scalar_tensor_tensor(d[:], a, 1.0, x2[:], ALU.mult, ALU.add,
                                                           accum_out=r2sm[:]))
                            t_r2 = P.mark("vector", S["sV"])
                            P.wait("scalar", t_r2)
                            P.instr("scalar",
                                    lambda E, s=ob, d=accb[:, ts * D:(ts + 1) * D]:
                                    E.activation(d, s[:], AF.Square, accum_out=q2sm[:]))
                            t_q2 = P.mark("scalar", S["sS"])
                            P.wait("vector", t_q2)
                            P.instr("vector",
                                    lambda E: E.tensor_scalar(l21[:], r2sm[:], 1.0 / D, None, ALU.mult))
                            P.instr("vector",
                                    lambda E: E.tensor_scalar(l22[:], q2sm[:], 1.0 / D, None, ALU.mult))
                            vsync()
                            P.instr("vector",
                                    lambda E: E.tensor_tensor(l23[:], l21[:], l21[:], ALU.mult))
                            vsync()
                            P.instr("vector",
                                    lambda E: E.tensor_tensor(l22[:], l22[:], l23[:], ALU.subtract))
                            t_v2 = P.mark("vector", S["sV"])
                            P.wait("scalar", t_v2)
                            P.instr("scalar",
                                    lambda E: E.activation(l22[:], l22[:], AF.Sqrt,
                                                           bias=c_eps[:], scale=1.0))
                            t_s2 = P.mark("scalar", S["sS"])
                            P.wait("vector", t_s2)
                            P.instr("vector", lambda E: E.reciprocal(l23[:], l22[:]))
                            vsync()
                            P.instr("vector",
                                    lambda E, d=xtf[sl], s=ob: E.tensor_scalar(d[:], s[:], l21[:],
                                                                               None, ALU.subtract))
                            P.instr("vector",
                                    lambda E, d=ob, s=xtf[sl]: E.tensor_scalar(d[:], s[:], l23[:],
                                                                               None, ALU.mult))
                            t_o = P.mark("vector", S["sV"])
                            t_xtf_free[sl] = t_o
                            P.wait("sync", t_o)
                            P.instr("sync",
                                    lambda E, d=y_out[seq, tg * 128:(tg + 1) * 128, :], s=ob:
                                    E.dma_start(out=d, in_=s[:]))
                            t_ob_free[0] = P.mark("sync", S["dOUT"], 16)
                            nonlocal_n_out[0] += 1

                        prev = None
                        for u in range(NU):
                            ig, ts = u // 8, u % 8
                            cur = emitA(u)
                            if (u == 0) or (ts == 1 and ig >= 1):
                                P.wait("gpsimd", t_wd_free)
                                P.instr("gpsimd",
                                        lambda E, d=wdb[:].rearrange("p (c n) -> p c n", n=2048),
                                        s=wd_v[:, 4 * ig:4 * ig + 4, :]:
                                        E.dma_start(out=d, in_=s))
                                t_wd = P.mark("gpsimd", S["dWD"], 16)
                                if ig + 1 < NIG:
                                    load_wgrp(ig + 1, (ig + 1) % 2,
                                              t_wg_free[(ig + 1) % 2], t_wg_free[(ig + 1) % 2])
                            if prev is not None:
                                emitRest(*prev)
                                if ts == 0:
                                    t_wg_free[(ig + 1) % 2] = prev[1]
                                    t_wd_free = t_down_done
                                if prev[0] // 8 == NIG - 1:
                                    emit_final(prev[0] % 8)
                            prev = (u, cur[0], cur[1])
                        emitRest(*prev)
                        emit_final(prev[0] % 8)
                        t_PC_free = [t_PC_free_l[0], t_PC_free_l[1]]
                        t_PA_free = t_psum_free[1]
                        t_PB_free = t_psum_free[0]

                        if DBG_DUMP_D and half == 0:
                            P.wait("sync", (S["sV"], S["sV"].v))
                            P.instr("sync", lambda E: E.dma_start(out=dbgd["acc"], in_=accb[:]))
                            P.mark("sync", S["dSO"], 16)
                            P.instr("sync", lambda E: E.dma_start(out=dbgd["xnfm"], in_=xnfm[:]))
                            P.mark("sync", S["dSO"], 16)
                        t_xn_consumed = (S["dXN"], S["dXN"].v)
                        P.flush(block)

                t_oscr_consumed = (S["dOS"], S["dOS"].v)

        n_out = nonlocal_n_out[0]
        for nm in ("dOUT", "dSO", "dSX"):
            if S[nm].v:
                P.wait("sync", (S[nm], S[nm].v))
        for nm in ("sP", "sV", "sS", "sG"):
            if S[nm].v:
                P.wait("sync", (S[nm], S[nm].v))
        P.flush(block)

    return nc


_NC_CACHE = None


def _get_nc():
    global _NC_CACHE
    if _NC_CACHE is None:
        _NC_CACHE = build_nc()
    return _NC_CACHE


def kernel(L0_attn_x, L0_attn_Wq, L0_attn_Wk, L0_attn_Wv, L0_attn_Wo,
           L0_ffn_w_gate, L0_ffn_w_up, L0_ffn_w_down):
    x = np.ascontiguousarray(np.asarray(L0_attn_x, dtype=np.float32))
    wq = np.ascontiguousarray(np.asarray(L0_attn_Wq, dtype=np.float32))
    wk = np.ascontiguousarray(np.asarray(L0_attn_Wk, dtype=np.float32))
    wv = np.ascontiguousarray(np.asarray(L0_attn_Wv, dtype=np.float32))
    wo = np.ascontiguousarray(np.asarray(L0_attn_Wo, dtype=np.float32))
    wg = np.ascontiguousarray(np.asarray(L0_ffn_w_gate, dtype=np.float32))
    wu = np.ascontiguousarray(np.asarray(L0_ffn_w_up, dtype=np.float32))
    wd = np.ascontiguousarray(np.asarray(L0_ffn_w_down, dtype=np.float32))

    c_id = np.eye(128, dtype=ml_dtypes.bfloat16)
    c_tril = (np.tril(np.ones((128, 128), dtype=np.float32)) * SCALE).astype(np.float32)
    c_eps = np.full((128, 1), EPS, dtype=np.float32)
    c_cnt2 = np.tile(np.array([float(T - (q + 1) * 128) for q in range(NQ)],
                              dtype=np.float32)[None, :], (128, 1))

    nc = _get_nc()
    in_maps = []
    for c in range(NCORES):
        in_maps.append({
            "x": x[c * BLOC:(c + 1) * BLOC],
            "wq": wq, "wk": wk, "wv": wv, "wo": wo,
            "wg": wg, "wu": wu, "wd": wd,
            "c_id": c_id, "c_tril": c_tril, "c_eps": c_eps, "c_cnt2": c_cnt2,
        })
    res = run_bass_kernel_spmd(nc, in_maps, core_ids=list(range(NCORES)))
    out = np.concatenate([r["y"] for r in res.results], axis=0)
    return out.astype(np.float32)


# version 18 (final)
# speedup vs baseline: 1.1507x; 1.0114x over previous
import contextlib
import numpy as np
import ml_dtypes
import concourse.bass as bass
import concourse.mybir as mybir
from concourse.bass_utils import run_bass_kernel_spmd

F32 = mybir.dt.float32
BF16 = mybir.dt.bfloat16
AF = mybir.ActivationFunctionType
ALU = mybir.AluOpType
AX = mybir.AxisListType

NCORES = 8
BLOC = 2
T = 2048
D = 2048
H = 16
II = 8192
NK = D // 128
NQ = T // 128
NC_I = II // 128
NIG = 16
SCALE = float(1.0 / np.sqrt(float(T)))
EPS = 1e-5
QOFF = [128 * (q * (q + 1) // 2) for q in range(NQ + 1)]
ATT_W = QOFF[NQ]

DBG_SEQS = None
DBG_HEADS = H
DBG_PHASES = "TABCD"
DBG_DUMP = False
DBG_DUMP_AB = False
DBG_DUMP_D = False


class Sem:
    __slots__ = ("h", "v")

    def __init__(self, h):
        self.h = h
        self.v = 0


class Prog:
    ENGINES = ("tensor", "vector", "scalar", "gpsimd", "sync")

    def __init__(self):
        self.q = {e: [] for e in self.ENGINES}
        self.lastwait = {e: {} for e in self.ENGINES}
        self.last = {e: None for e in self.ENGINES}

    def instr(self, eng, fn):
        ent = {"fn": fn, "inc": None, "amt": 0}
        self.q[eng].append(ent)
        self.last[eng] = ent

    def mark(self, eng, sem, amt=1):
        ent = self.last[eng]
        assert ent is not None and ent["inc"] is None
        sem.v += amt
        ent["inc"] = sem
        ent["amt"] = amt
        return (sem, sem.v)

    def wait(self, eng, tkt):
        if tkt is None:
            return
        if isinstance(tkt, (list, tuple)) and tkt and isinstance(tkt[0], (list, tuple, type(None))):
            for t in tkt:
                self.wait(eng, t)
            return
        sem, v = tkt
        key = id(sem)
        if self.lastwait[eng].get(key, -1) >= v:
            return
        self.lastwait[eng][key] = v
        ent = {"fn": None, "inc": None, "amt": 0, "wait": (sem.h, v)}
        self.q[eng].append(ent)
        self.last[eng] = None

    def flush(self, block):
        for eng in self.ENGINES:
            ents = self.q[eng]
            if not ents:
                continue
            dec = getattr(block, eng)

            def section(E, ents=ents):
                for ent in ents:
                    if "wait" in ent:
                        s, v = ent["wait"]
                        E.wait_ge(s, v)
                    elif ent["inc"] is not None:
                        ent["fn"](E).then_inc(ent["inc"].h, ent["amt"])
                    else:
                        ent["fn"](E)

            dec(section)
        self.q = {e: [] for e in self.ENGINES}
        self.last = {e: None for e in self.ENGINES}


def build_nc():
    nc = bass.Bass("TRN2", target_bir_lowering=False)

    x_in = nc.dram_tensor("x", [BLOC, T, D], F32, kind="ExternalInput").ap()
    wq_in = nc.dram_tensor("wq", [D, D], F32, kind="ExternalInput").ap()
    wk_in = nc.dram_tensor("wk", [D, D], F32, kind="ExternalInput").ap()
    wv_in = nc.dram_tensor("wv", [D, D], F32, kind="ExternalInput").ap()
    wo_in = nc.dram_tensor("wo", [D, D], F32, kind="ExternalInput").ap()
    wg_in = nc.dram_tensor("wg", [D, II], F32, kind="ExternalInput").ap()
    wu_in = nc.dram_tensor("wu", [D, II], F32, kind="ExternalInput").ap()
    wd_in = nc.dram_tensor("wd", [II, D], F32, kind="ExternalInput").ap()
    c_id_in = nc.dram_tensor("c_id", [128, 128], BF16, kind="ExternalInput").ap()
    c_tril_in = nc.dram_tensor("c_tril", [128, 128], F32, kind="ExternalInput").ap()
    c_eps_in = nc.dram_tensor("c_eps", [128, 1], F32, kind="ExternalInput").ap()
    c_cnt2_in = nc.dram_tensor("c_cnt2", [128, 16], F32, kind="ExternalInput").ap()
    y_out = nc.dram_tensor("y", [BLOC, T, D], F32, kind="ExternalOutput").ap()
    scr_kind = "ExternalOutput" if DBG_DUMP else "Internal"
    o_scr = nc.dram_tensor("o_scr", [H, 128, T], BF16, kind=scr_kind).ap()
    wg16 = nc.dram_tensor("wg16", [D, II], BF16, kind="Internal").ap()
    wu16 = nc.dram_tensor("wu16", [D, II], BF16, kind="Internal").ap()
    wd16 = nc.dram_tensor("wd16", [II, D], BF16, kind="Internal").ap()
    wo16 = nc.dram_tensor("wo16", [D, D], BF16, kind="Internal").ap()
    xn_scr = nc.dram_tensor("xn_scr", [8, 128, D], F32, kind=scr_kind).ap()
    if DBG_DUMP_D:
        dbgd = {nm: nc.dram_tensor(f"dbgd_{nm}", shp, dt, kind="ExternalOutput").ap()
                for nm, shp, dt in [
                    ("xnfm", [128, NK * 8 * 128], BF16), ("acc", [128, 8 * D], F32),
                    ("g", [128, 512], F32), ("h", [128, 512], F32),
                    ("a", [128, 512], F32), ("sec", [128, 128], F32),
                    ("msk", [128, 512], BF16), ("hs", [128, 512], BF16),
                    ("hsT", [128, 512], BF16), ("gu", [128, 1024], F32)]}
    if DBG_DUMP_AB:
        dbg_outs = {nm: nc.dram_tensor(f"dbg_{nm}", shp, dt, kind="ExternalOutput").ap()
                    for nm, shp, dt in [
                        ("qfm", [128, T], BF16), ("kfm", [128, T], BF16),
                        ("vfm", [128, T], BF16), ("vtm", [128, NK * 128], BF16),
                        ("sufft", [1, NQ * 128], BF16), ("alpha", [1, NQ * 128], BF16),
                        ("attn", [128, T], BF16), ("attnT", [128, ATT_W], BF16),
                        ("ebuf", [128, T], F32), ("xfm", [128, NK * T], BF16)]}

    wq_v = wq_in.rearrange("(k p) (h j) -> p k h j", p=128, j=128)
    wk_v = wk_in.rearrange("(k p) (h j) -> p k h j", p=128, j=128)
    wv_v = wv_in.rearrange("(k p) (h j) -> p k h j", p=128, j=128)
    wo_v = wo16.rearrange("(k p) n -> p k n", p=128)
    wg_v = wg16.rearrange("(k p) n -> p k n", p=128)
    wu_v = wu16.rearrange("(k p) n -> p k n", p=128)
    wd_v = wd16.rearrange("(c p) n -> p c n", p=128)
    o_scr_v = o_scr.rearrange("h p t -> p h t")

    es0 = contextlib.ExitStack()
    with es0:
        names = ["dC", "dX", "dX1", "dW", "dWO", "dWO1", "dOS", "dWF", "dWD", "dXN",
                 "dSO", "dSX", "dOUT", "dCA", "sP", "sV", "sS", "sG"]
        S = {n: Sem(es0.enter_context(nc.semaphore(n))) for n in names}

        c_id = es0.enter_context(nc.sbuf_tensor("c_id_s", [128, 128], BF16))
        c_tril = es0.enter_context(nc.sbuf_tensor("c_tril_s", [128, 128], F32))
        c_eps = es0.enter_context(nc.sbuf_tensor("c_eps_s", [128, 1], F32))
        c_cnt2 = es0.enter_context(nc.sbuf_tensor("c_cnt2_s", [128, 16], F32))

        P_A = es0.enter_context(nc.psum_tensor("P_A", [128, 1024], F32))
        P_B = es0.enter_context(nc.psum_tensor("P_B", [128, 1024], F32))
        P_C = es0.enter_context(nc.psum_tensor("P_C", [128, 1024], F32))
        P_T = es0.enter_context(nc.psum_tensor("P_T", [128, 2048], BF16))

        block = es0.enter_context(nc.Block())
        P = Prog()

        P.instr("sync", lambda E: E.dma_start(out=c_id[:], in_=c_id_in))
        P.mark("sync", S["dC"], 16)
        P.instr("sync", lambda E: E.dma_start(out=c_tril[:], in_=c_tril_in))
        P.mark("sync", S["dC"], 16)
        P.instr("sync", lambda E: E.dma_start(out=c_eps[:], in_=c_eps_in))
        P.mark("sync", S["dC"], 16)
        P.instr("sync", lambda E: E.dma_start(out=c_cnt2[:], in_=c_cnt2_in))
        t_consts = P.mark("sync", S["dC"], 16)

        t_PA_free = None
        t_PB_free = None
        t_PC_free = None
        t_PT_free = [None, None]
        t_oscr_stored = None
        t_oscr_consumed = None
        t_xn_stored = [None] * 8
        t_xn_consumed = None

        def transpose_group(src_sb, src_off, n_blk, dst_sb, dst_off, ps, t_src, extra_waits=()):
            nonlocal t_PT_free
            P.wait("tensor", t_src)
            for w in extra_waits:
                P.wait("tensor", w)
            P.wait("tensor", t_PT_free[ps])
            P.wait("tensor", t_consts)
            for j in range(n_blk):
                P.instr("tensor",
                        lambda E, d=P_T[:, ps * 1024 + j * 128:ps * 1024 + (j + 1) * 128],
                        s=src_sb[:, src_off + j * 128:src_off + (j + 1) * 128]:
                        E.transpose(d, s, c_id[:]))
            t_tr = P.mark("tensor", S["sP"])
            P.wait("vector", t_tr)
            P.instr("vector",
                    lambda E, d=dst_sb[:, dst_off:dst_off + n_blk * 128],
                    s=P_T[:, ps * 1024:ps * 1024 + n_blk * 128]:
                    E.tensor_copy(d, s))
            t_cp = P.mark("vector", S["sV"])
            t_PT_free[ps] = t_cp
            return t_cp

        def transpose_group_strided(src_sb, blocks, dst_view, ps, t_src):
            nonlocal t_PT_free
            P.wait("tensor", t_src)
            P.wait("tensor", t_PT_free[ps])
            P.wait("tensor", t_consts)
            for j, off in enumerate(blocks):
                P.instr("tensor",
                        lambda E, d=P_T[:, ps * 1024 + j * 128:ps * 1024 + (j + 1) * 128],
                        s=src_sb[:, off:off + 128]:
                        E.transpose(d, s, c_id[:]))
            t_tr = P.mark("tensor", S["sP"])
            P.wait("vector", t_tr)
            nb = len(blocks)
            P.instr("vector",
                    lambda E, d=dst_view,
                    s=P_T[:, ps * 1024:ps * 1024 + nb * 128].rearrange("p (k t) -> p k t", t=128):
                    E.tensor_copy(d, s))
            t_cp = P.mark("vector", S["sV"])
            t_PT_free[ps] = t_cp
            return t_cp

        def vsync():
            t = P.mark("vector", S["sV"])
            P.wait("vector", t)
            return t

        def barrier_all():
            for eng in Prog.ENGINES:
                for sem in S.values():
                    if sem.v:
                        P.wait(eng, (sem, sem.v))

        n_out = 0
        nonlocal_n_out = [0]

        cast_jobs = []
        for src, dst, nrows, rchunk in ((wo_in, wo16, D, 256), (wd_in, wd16, II, 1024),
                                        (wg_in, wg16, D, 256), (wu_in, wu16, D, 256)):
            for r0 in range(0, nrows, rchunk):
                cast_jobs.append((src, dst, r0, r0 + rchunk))

        def emit_casts(njobs):
            t = None
            for _ in range(njobs):
                if not cast_jobs:
                    break
                src, dst, r0, r1 = cast_jobs.pop(0)
                P.instr("gpsimd",
                        lambda E, s2=src[r0:r1, :], d=dst[r0:r1, :]: E.dma_start(out=d, in_=s2))
                t = P.mark("gpsimd", S["dCA"], 16)
            return t

        t_cast_all = None
        seqs = DBG_SEQS if DBG_SEQS is not None else list(range(BLOC))
        for seq in seqs:
            barrier_all()
            with contextlib.ExitStack() as esAB:
                sb = lambda name, shape, dt: esAB.enter_context(
                    nc.sbuf_tensor(f"{name}_{seq}", shape, dt))
                xt = [sb(f"xt{i}", [128, D], F32) for i in range(2)]
                xb = [sb(f"xb{i}", [128, D], BF16) for i in range(2)]
                xfm = sb("xfm", [128, NK * T], BF16)
                wsl = {w: sb(f"w{w}0", [128, NK * 128], BF16) for w in "qkv"}
                qfm = sb("qfm", [128, T], BF16)
                kfm = sb("kfm", [128, T], BF16)
                vfm = sb("vfm", [128, T], BF16)
                vtm = sb("vtm", [128, NK * 128], BF16)
                cs_c = sb("cs", [128, 16], F32)
                suf_c = sb("suf", [128, 16], F32)
                suf_b = sb("sufb", [128, 16], BF16)
                sufft = sb("sufft", [1, NQ * 128], BF16)
                alpha = sb("alpha", [1, NQ * 128], BF16)
                e_pack = sb("epack", [128, ATT_W], BF16)
                zdm = sb("zdm", [128, 256], F32)
                rs_all = sb("rsall", [128, 96], F32)
                rsum_all = sb("rsumall", [128, 16], F32)
                rden_all = sb("rdenall", [128, 16], F32)
                rdb_all = sb("rdball", [128, 16], BF16)
                attnT = sb("attnT", [128, ATT_W], BF16)
                ofm = sb("ofm", [128, T], BF16)

                t_xt_free = [None, None]
                t_xb_free = [None, None]
                t_ofm_free = None
                t_epack_free = None

                for w, wvv in (("q", wq_v), ("k", wk_v), ("v", wv_v)):
                    P.instr("gpsimd",
                            lambda E, d=wsl[w], s=wvv[:, :, 0, :]:
                            E.dma_start(out=d[:].rearrange("p (k j) -> p k j", j=128), in_=s))
                    P.mark("gpsimd", S["dW"], 16)

                xfm_r = xfm[:].rearrange("p (k t) -> p k t", t=T)
                t_xfm = None
                for i in range(NQ):
                    sl = i % 2
                    P.wait("sync", t_xt_free[sl])
                    P.instr("sync",
                            lambda E, d=xt[sl], s=x_in[seq, i * 128:(i + 1) * 128, :]:
                            E.dma_start(out=d[:], in_=s))
                    t_ld = P.mark("sync", S["dX"] if sl == 0 else S["dX1"], 16)
                    P.wait("vector", t_ld)
                    P.wait("vector", t_xb_free[sl])
                    P.instr("vector", lambda E, d=xb[sl], s=xt[sl]: E.tensor_copy(d[:], s[:]))
                    t_cast = P.mark("vector", S["sV"])
                    t_xt_free[sl] = t_cast
                    for g in range(4):
                        t_xfm = transpose_group_strided(
                            xb[sl], [(4 * g + j) * 128 for j in range(4)],
                            xfm_r[:, 4 * g:4 * g + 4, i * 128:(i + 1) * 128],
                            g % 2, t_cast)
                    t_xb_free[sl] = t_xfm

                t_cs_prev = None
                t_head = {}

                def fn_proj(h):
                    nonlocal t_PA_free, t_PC_free, t_cs_prev
                    t_w_ready = (S["dW"], 16 * 3 * (h + 1))

                    def proj(dst_sb, wbuf, pt, t_pfree, extra_evac_wait=None):
                        t_evac = None
                        for half in range(2):
                            P.wait("tensor", t_w_ready)
                            P.wait("tensor", t_xfm)
                            P.wait("tensor", t_pfree if half == 0 else t_evac)
                            for k in range(NK):
                                for s2 in range(2):
                                    P.instr("tensor",
                                            lambda E, o=pt[:, s2 * 512:(s2 + 1) * 512],
                                            l=wbuf[:, k * 128:(k + 1) * 128],
                                            r=xfm[:, k * T + half * 1024 + s2 * 512:
                                                  k * T + half * 1024 + (s2 + 1) * 512],
                                            st=(k == 0), sp=(k == NK - 1):
                                            E.matmul(o, l, r, start=st, stop=sp))
                            t_mm = P.mark("tensor", S["sP"])
                            P.wait("scalar", t_mm)
                            if extra_evac_wait is not None:
                                P.wait("scalar", extra_evac_wait)
                            P.instr("scalar",
                                    lambda E, d=dst_sb[:, half * 1024:(half + 1) * 1024], s=pt:
                                    E.copy(out=d, in_=s[:]))
                            t_evac = P.mark("scalar", S["sS"])
                        return t_evac

                    t_q = proj(qfm, wsl["q"], P_C, t_PC_free)
                    t_k = proj(kfm, wsl["k"], P_A, t_PA_free)
                    t_v = proj(vfm, wsl["v"], P_C, t_q, extra_evac_wait=t_cs_prev)
                    t_PA_free = t_k
                    t_PC_free = t_v
                    t_head[h] = {"q": t_q, "k": t_k, "v": t_v}
                    if h + 1 < DBG_HEADS:
                        for w, wvv, tk in (("q", wq_v, t_q), ("k", wk_v, t_k), ("v", wv_v, t_v)):
                            P.wait("gpsimd", tk)
                            P.instr("gpsimd",
                                    lambda E, d=wsl[w], s=wvv[:, :, h + 1, :]:
                                    E.dma_start(out=d[:].rearrange("p (k j) -> p k j", j=128), in_=s))
                            P.mark("gpsimd", S["dW"], 16)

                def fn_vprep(h):
                    nonlocal t_cs_prev
                    t_v = t_head[h]["v"]
                    t_vtm = None
                    for g in range(4):
                        t_vtm = transpose_group(vfm, g * 512, 4, vtm, g * 512, g % 2, t_v)
                    t_head[h]["vtm"] = t_vtm
                    P.wait("vector", t_v)
                    t_cs = None
                    for j in range(NK):
                        P.instr("vector",
                                lambda E, d=cs_c[:, j:j + 1], s=vfm[:, j * 128:(j + 1) * 128]:
                                E.reduce_sum(d, s, axis=AX.X))
                        t_cs = P.mark("vector", S["sV"])
                    P.wait("vector", t_cs)
                    for q in range(NQ - 1):
                        P.instr("vector",
                                lambda E, d=suf_c[:, q:q + 1], a=cs_c[:, q + 1:16]:
                                E.reduce_sum(d, a, axis=AX.X))
                    vsync()
                    P.instr("vector",
                            lambda E: E.tensor_copy(suf_b[:, 0:15], suf_c[:, 0:15]))
                    t_suf = P.mark("vector", S["sV"])
                    t_cs_prev = t_suf
                    for q in range(NQ - 1):
                        ps = q % 2
                        P.wait("tensor", t_suf)
                        P.wait("tensor", t_PT_free[ps])
                        P.instr("tensor",
                                lambda E, d=P_T[0:1, ps * 1024 + 512:ps * 1024 + 640],
                                s=suf_b[:, q:q + 1]:
                                E.transpose(d, s, c_id[:]))
                        t_tr = P.mark("tensor", S["sP"])
                        P.wait("vector", t_tr)
                        P.instr("vector",
                                lambda E, d=sufft[0:1, q * 128:(q + 1) * 128],
                                s=P_T[0:1, ps * 1024 + 512:ps * 1024 + 640]:
                                E.tensor_copy(d, s))
                        t_PT_free[ps] = P.mark("vector", S["sV"])

                def fn_S1(h):
                    nonlocal t_PA_free
                    t_q = t_head[h]["q"]; t_k = t_head[h]["k"]
                    t_PA_slot = [t_PA_free, t_PA_free]
                    t_zde_q = [None] * NQ
                    t_exps_all = []
                    for q in range(NQ):
                        par = q % 2
                        seg_bounds = []
                        c0 = 0
                        while c0 < q * 128:
                            c1 = min(c0 + 512, q * 128)
                            seg_bounds.append((c0, c1))
                            c0 = c1
                        used = 0
                        nseg = len(seg_bounds)
                        for si in range(nseg + 1):
                            ps = si % 2
                            P.wait("tensor", t_q)
                            P.wait("tensor", t_k)
                            P.wait("tensor", t_PA_slot[ps])
                            if si < nseg:
                                c0, c1 = seg_bounds[si]
                                P.instr("tensor",
                                        lambda E, o=P_A[:, ps * 512:ps * 512 + (c1 - c0)],
                                        l=qfm[:, q * 128:(q + 1) * 128], r=kfm[:, c0:c1]:
                                        E.matmul(o, l, r, start=True, stop=True))
                                t_z = P.mark("tensor", S["sP"])
                                P.wait("scalar", t_z)
                                P.wait("scalar", t_epack_free)
                                P.instr("scalar",
                                        lambda E, d=e_pack[:, QOFF[q] + c0:QOFF[q] + c1],
                                        s=P_A[:, ps * 512:ps * 512 + (c1 - c0)],
                                        a=rs_all[:, q * 6 + used:q * 6 + used + 1]:
                                        E.activation(d, s, AF.Exp, scale=SCALE, accum_out=a))
                                t_e = P.mark("scalar", S["sS"])
                                t_PA_slot[ps] = t_e
                            else:
                                P.instr("tensor",
                                        lambda E, o=P_A[:, ps * 512:ps * 512 + 128],
                                        l=qfm[:, q * 128:(q + 1) * 128],
                                        r=kfm[:, q * 128:(q + 1) * 128]:
                                        E.matmul(o, l, r, start=True, stop=True))
                                t_z = P.mark("tensor", S["sP"])
                                P.wait("vector", t_z)
                                P.wait("vector", t_consts)
                                if q >= 2:
                                    P.wait("vector", t_zde_q[q - 2])
                                P.instr("vector",
                                        lambda E, d=zdm[:, par * 128:(par + 1) * 128],
                                        s=P_A[:, ps * 512:ps * 512 + 128]:
                                        E.tensor_tensor(d, s, c_tril[:], ALU.mult))
                                t_zd = P.mark("vector", S["sV"])
                                t_PA_slot[ps] = t_zd
                                P.wait("scalar", t_zd)
                                P.wait("scalar", t_epack_free)
                                P.instr("scalar",
                                        lambda E, d=e_pack[:, QOFF[q] + q * 128:QOFF[q] + (q + 1) * 128],
                                        sz=zdm[:, par * 128:(par + 1) * 128],
                                        a=rs_all[:, q * 6 + used:q * 6 + used + 1]:
                                        E.activation(d, sz, AF.Exp, scale=1.0, accum_out=a))
                                t_e = P.mark("scalar", S["sS"])
                                t_zde_q[q] = t_e
                            used += 1
                        t_exps_all.append((t_e, used))
                    t_PA_free = [t_PA_slot[0], t_PA_slot[1]]
                    t_head[h]["exps"] = t_exps_all

                def fn_S2S3(h):
                    nonlocal t_epack_free
                    t_exps_all = t_head[h]["exps"]
                    t_alpha_done = [None] * NQ
                    t_attnT_done = [None] * NQ
                    for q in range(NQ):
                        P.wait("vector", t_exps_all[q][0])
                    for q in range(NQ):
                        P.instr("vector",
                                lambda E, q=q, u=t_exps_all[q][1]:
                                E.reduce_sum(rsum_all[:, q:q + 1],
                                             rs_all[:, q * 6:q * 6 + u], axis=AX.X))
                    vsync()
                    P.instr("vector",
                            lambda E: E.tensor_tensor(rsum_all[:], rsum_all[:], c_cnt2[:], ALU.add))
                    vsync()
                    P.instr("vector", lambda E: E.reciprocal(rden_all[:], rsum_all[:]))
                    vsync()
                    P.instr("vector", lambda E: E.tensor_copy(rdb_all[:], rden_all[:]))
                    t_rdb = P.mark("vector", S["sV"])
                    for q in range(NQ):
                        psr = q % 2
                        P.wait("tensor", t_rdb)
                        P.wait("tensor", t_PT_free[psr])
                        P.instr("tensor",
                                lambda E, d=P_T[0:1, psr * 1024 + 512:psr * 1024 + 640],
                                s=rdb_all[:, q:q + 1]:
                                E.transpose(d, s, c_id[:]))
                        t_tra = P.mark("tensor", S["sP"])
                        P.wait("vector", t_tra)
                        P.instr("vector",
                                lambda E, d=alpha[0:1, q * 128:(q + 1) * 128],
                                s=P_T[0:1, psr * 1024 + 512:psr * 1024 + 640]:
                                E.tensor_copy(d, s))
                        t_PT_free[psr] = P.mark("vector", S["sV"])
                        t_alpha_done[q] = t_PT_free[psr]
                    for q in range(NQ):
                        P.wait("vector", t_rdb)
                        P.instr("vector",
                                lambda E, q=q, w=(q + 1) * 128:
                                E.tensor_scalar(e_pack[:, QOFF[q]:QOFF[q] + w],
                                                e_pack[:, QOFF[q]:QOFF[q] + w],
                                                rden_all[:, q:q + 1], None, ALU.mult))
                        t_nrm = P.mark("vector", S["sV"])
                        t_last = None
                        for g in range((q + 1 + 3) // 4):
                            nj = min(4, q + 1 - 4 * g)
                            t_last = transpose_group(e_pack, QOFF[q] + g * 512, nj, attnT,
                                                     QOFF[q] + g * 512, g % 2, t_nrm)
                        t_attnT_done[q] = t_last
                    t_epack_free = (S["sV"], S["sV"].v)
                    t_head[h]["alpha"] = t_alpha_done
                    t_head[h]["attnT"] = t_attnT_done

                def fn_AV(h):
                    nonlocal t_PB_free, t_ofm_free
                    t_vtm = t_head[h]["vtm"]
                    t_alpha_done = t_head[h]["alpha"]
                    t_attnT_done = t_head[h]["attnT"]
                    t_evO = [None, None]
                    for th in range(2):
                        qb = th * 8
                        P.wait("tensor", t_vtm)
                        for q in range(qb, qb + 8):
                            P.wait("tensor", t_attnT_done[q])
                            P.wait("tensor", t_alpha_done[q])
                        P.wait("tensor", t_PB_free)
                        for j in range(qb + 8):
                            for q in range(max(j, qb), qb + 8):
                                P.instr("tensor",
                                        lambda E, o=P_B[:, (q - qb) * 128:(q - qb + 1) * 128],
                                        l=vtm[:, j * 128:(j + 1) * 128],
                                        r=attnT[:, QOFF[q] + j * 128:QOFF[q] + (j + 1) * 128],
                                        st=(j == 0 and (q - qb) % 4 == 0),
                                        sp=(j == NQ - 1 and q == NQ - 1):
                                        E.matmul(o, l, r, start=st, stop=sp))
                        for q in range(qb, qb + 8):
                            if q == NQ - 1:
                                continue
                            P.instr("tensor",
                                    lambda E, o=P_B[:, (q - qb) * 128:(q - qb + 1) * 128],
                                    l=sufft[0:1, q * 128:(q + 1) * 128],
                                    r=alpha[0:1, q * 128:(q + 1) * 128]:
                                    E.matmul(o, l, r, start=False, stop=True))
                        t_av = P.mark("tensor", S["sP"])
                        P.wait("scalar", t_av)
                        P.instr("scalar",
                                lambda E, d=ofm[:, th * 1024:(th + 1) * 1024]:
                                E.copy(out=d, in_=P_B[:]))
                        t_evO[th] = P.mark("scalar", S["sS"])
                        t_PB_free = t_evO[th]
                    P.wait("sync", t_evO[0])
                    P.wait("sync", t_evO[1])
                    P.wait("sync", t_ofm_free)
                    P.wait("sync", t_oscr_consumed)
                    P.instr("sync",
                            lambda E, d=o_scr[h, :, :], s=ofm: E.dma_start(out=d, in_=s[:]))
                    t_ofm_free = P.mark("sync", S["dSO"], 16)

                if "A" in DBG_PHASES and DBG_HEADS > 0:
                    fn_proj(0)
                    emit_casts(2)
                    fn_vprep(0)
                    for h in range(DBG_HEADS):
                        fn_S1(h)
                        if h + 1 < DBG_HEADS:
                            fn_proj(h + 1)
                        emit_casts(2)
                        fn_S2S3(h)
                        fn_AV(h)
                        if h + 1 < DBG_HEADS:
                            fn_vprep(h + 1)
                    emit_casts(len(cast_jobs))
                    t_cast_all = (S["dCA"], S["dCA"].v)

                t_oscr_stored = (S["dSO"], S["dSO"].v)
                P.flush(block)

            if "C" not in DBG_PHASES:
                continue
            with contextlib.ExitStack() as esCD:
                sb2 = lambda name, shape, dt: esCD.enter_context(
                    nc.sbuf_tensor(f"{name}_{seq}", shape, dt))
                xnfm = sb2("xnfm", [128, NK * 8 * 128], BF16)
                accb = sb2("acc", [128, 8 * D], F32)

                for half in range(2):
                    barrier_all()
                    with contextlib.ExitStack() as esC:
                        sb3 = lambda name, shape, dt: esC.enter_context(
                            nc.sbuf_tensor(f"{name}_{seq}_{half}", shape, dt))
                        wo_sb = sb3("wo_sb", [128, NK * 2048], BF16)
                        osl = sb3("osl", [128, NK * 128], BF16)
                        xct = [sb3(f"xct{i}", [128, D], F32) for i in range(2)]
                        r_b = sb3("r_b", [128, D], F32)
                        xn_b = sb3("xn_b", [128, D], F32)
                        xnbb = sb3("xnbb", [128, D], BF16)
                        rsm = sb3("rsm", [128, 2], F32)
                        sqm = sb3("sqm", [128, 2], F32)
                        ln1 = sb3("ln1", [128, 1], F32)
                        ln2 = sb3("ln2", [128, 1], F32)
                        ln3 = sb3("ln3", [128, 1], F32)

                        P.wait("gpsimd", t_oscr_stored)
                        P.wait("gpsimd", t_cast_all)
                        wo_r = wo_sb[:].rearrange("p (k n) -> p k n", n=2048)
                        P.instr("gpsimd",
                                lambda E: E.dma_start(out=wo_r[:, :, 0:1024], in_=wo_v[:, :, 0:1024]))
                        t_wo0 = P.mark("gpsimd", S["dWO"], 16)
                        P.instr("gpsimd",
                                lambda E: E.dma_start(out=wo_r[:, :, 1024:2048], in_=wo_v[:, :, 1024:2048]))
                        t_wo1 = P.mark("gpsimd", S["dWO1"], 16)
                        t_wo_h = [t_wo0, t_wo1]

                        t_osl_free = None
                        t_xct_free = [None, None]
                        t_sq_prev = None
                        for ts in range(8):
                            tg = half * 8 + ts
                            P.wait("sync", t_oscr_stored)
                            P.wait("sync", t_osl_free)
                            P.instr("sync",
                                    lambda E, d=osl[:].rearrange("p (k j) -> p k j", j=128),
                                    s=o_scr_v[:, :, tg * 128:(tg + 1) * 128]:
                                    E.dma_start(out=d, in_=s))
                            t_os = P.mark("sync", S["dOS"], 16)
                            sl = ts % 2
                            P.wait("sync", t_xct_free[sl])
                            P.instr("sync",
                                    lambda E, d=xct[sl], s=x_in[seq, tg * 128:(tg + 1) * 128, :]:
                                    E.dma_start(out=d[:], in_=s))
                            t_xl = P.mark("sync", S["dX"] if sl == 0 else S["dX1"], 16)
                            t_r = [None, None]
                            for ddh in range(2):
                                pt = P_A if ddh == 0 else P_C
                                P.wait("tensor", t_wo_h[ddh])
                                P.wait("tensor", t_os)
                                P.wait("tensor", t_PA_free if ddh == 0 else t_PC_free)
                                for k in range(NK):
                                    for s2 in range(2):
                                        P.instr("tensor",
                                                lambda E, o=pt[:, s2 * 512:(s2 + 1) * 512],
                                                l=osl[:, k * 128:(k + 1) * 128],
                                                r=wo_sb[:, k * 2048 + ddh * 1024 + s2 * 512:
                                                      k * 2048 + ddh * 1024 + (s2 + 1) * 512],
                                                st=(k == 0), sp=(k == NK - 1):
                                                E.matmul(o, l, r, start=st, stop=sp))
                                t_mm = P.mark("tensor", S["sP"])
                                P.wait("vector", t_mm)
                                P.wait("vector", t_xl)
                                P.wait("vector", t_sq_prev)
                                P.instr("vector",
                                        lambda E, d=r_b[:, ddh * 1024:(ddh + 1) * 1024],
                                        x2=xct[sl][:, ddh * 1024:(ddh + 1) * 1024],
                                        a=rsm[:, ddh:ddh + 1], sp2=pt:
                                        E.scalar_tensor_tensor(d, sp2[:], 1.0, x2,
                                                               ALU.mult, ALU.add, accum_out=a))
                                t_r[ddh] = P.mark("vector", S["sV"])
                                if ddh == 0:
                                    t_PA_free = t_r[0]
                                else:
                                    t_PC_free = t_r[1]
                            t_osl_free = t_r[1]
                            t_xct_free[sl] = t_r[1]
                            for ddh in range(2):
                                P.wait("scalar", t_r[ddh])
                                P.instr("scalar",
                                        lambda E, d=xn_b[:, ddh * 1024:(ddh + 1) * 1024],
                                        s=r_b[:, ddh * 1024:(ddh + 1) * 1024],
                                        a=sqm[:, ddh:ddh + 1]:
                                        E.activation(d, s, AF.Square, accum_out=a))
                            t_sq = P.mark("scalar", S["sS"])
                            t_sq_prev = t_sq
                            P.wait("vector", t_sq)
                            P.instr("vector",
                                    lambda E: E.tensor_tensor(ln1[:], rsm[:, 0:1], rsm[:, 1:2], ALU.add))
                            vsync()
                            P.instr("vector",
                                    lambda E: E.tensor_scalar(ln1[:], ln1[:], 1.0 / D, None, ALU.mult))
                            P.instr("vector",
                                    lambda E: E.tensor_tensor(ln2[:], sqm[:, 0:1], sqm[:, 1:2], ALU.add))
                            vsync()
                            P.instr("vector",
                                    lambda E: E.tensor_scalar(ln2[:], ln2[:], 1.0 / D, None, ALU.mult))
                            vsync()
                            P.instr("vector",
                                    lambda E: E.tensor_tensor(ln3[:], ln1[:], ln1[:], ALU.mult))
                            vsync()
                            P.instr("vector",
                                    lambda E: E.tensor_tensor(ln2[:], ln2[:], ln3[:], ALU.subtract))
                            t_var = P.mark("vector", S["sV"])
                            P.wait("scalar", t_var)
                            P.instr("scalar",
                                    lambda E: E.activation(ln2[:], ln2[:], AF.Sqrt,
                                                           bias=c_eps[:], scale=1.0))
                            t_sqr = P.mark("scalar", S["sS"])
                            P.wait("vector", t_sqr)
                            P.instr("vector", lambda E: E.reciprocal(ln3[:], ln2[:]))
                            vsync()
                            P.instr("vector",
                                    lambda E: E.tensor_scalar(xn_b[:], r_b[:], ln1[:], None,
                                                              ALU.subtract))
                            P.instr("vector",
                                    lambda E: E.tensor_scalar(r_b[:], xn_b[:], ln3[:], None,
                                                              ALU.mult))
                            P.instr("vector", lambda E: E.tensor_copy(xnbb[:], r_b[:]))
                            t_xnb = P.mark("vector", S["sV"])
                            P.wait("sync", t_xnb)
                            P.wait("sync", t_xn_consumed)
                            P.instr("sync",
                                    lambda E, d=xn_scr[ts, :, :]: E.dma_start(out=d, in_=r_b[:]))
                            t_xn_stored[ts] = P.mark("sync", S["dSX"], 16)
                            xnfm_r = xnfm[:].rearrange("p (k t) -> p k t", t=8 * 128)
                            for g in range(4):
                                transpose_group_strided(
                                    xnbb, [(4 * g + j) * 128 for j in range(4)],
                                    xnfm_r[:, 4 * g:4 * g + 4, ts * 128:(ts + 1) * 128],
                                    g % 2, t_xnb)
                        t_xnfm = t_PT_free[1] if t_PT_free[1] else t_PT_free[0]
                        t_xnfm = (S["sV"], S["sV"].v)
                        P.flush(block)

                    if "D" not in DBG_PHASES:
                        continue
                    barrier_all()
                    with contextlib.ExitStack() as esD:
                        sb4 = lambda name, shape, dt: esD.enter_context(
                            nc.sbuf_tensor(f"{name}_{seq}_{half}", shape, dt))
                        wgb = [sb4(f"wgb{i}", [128, NK * 512], BF16) for i in range(2)]
                        wub = [sb4(f"wub{i}", [128, NK * 512], BF16) for i in range(2)]
                        wdb = sb4("wdb", [128, 4 * 2048], BF16)
                        g_b = sb4("g_b", [128, 512], F32)
                        h_b = sb4("h_b", [128, 512], F32)
                        a_b = sb4("a_b", [128, 512], F32)
                        tA = sb4("tA", [128, 128], F32)
                        tB = sb4("tB", [128, 128], F32)
                        tC2 = sb4("tC2", [128, 128], F32)
                        tD2 = sb4("tD2", [128, 128], F32)
                        sec = sb4("sec", [128, 128], F32)
                        msk = sb4("msk", [128, 512], BF16)
                        hs_b = sb4("hs_b", [128, 512], BF16)
                        hsT = sb4("hsT", [128, 512], BF16)
                        xtf = [sb4("xtf0", [128, D], F32)]
                        outb = [sb4("outb0", [128, D], F32)]
                        r2sm = sb4("r2sm", [128, 1], F32)
                        q2sm = sb4("q2sm", [128, 1], F32)
                        l21 = sb4("l21", [128, 1], F32)
                        l22 = sb4("l22", [128, 1], F32)
                        l23 = sb4("l23", [128, 1], F32)

                        a_g = a_b[:].rearrange("p (g four) -> p g four", four=4)
                        h_g = h_b[:].rearrange("p (g four) -> p g four", four=4)
                        m_g = msk[:].rearrange("p (g four) -> p g four", four=4)

                        t_wgb_ready = [None, None]

                        def load_wgrp(ig, buf_i, t_free_g, t_free_u):
                            P.wait("gpsimd", t_free_g)
                            P.instr("gpsimd",
                                    lambda E, d=wgb[buf_i][:].rearrange("p (k n) -> p k n", n=512),
                                    s=wg_v[:, :, ig * 512:(ig + 1) * 512]:
                                    E.dma_start(out=d, in_=s))
                            P.mark("gpsimd", S["dWF"], 16)
                            P.wait("gpsimd", t_free_u)
                            P.instr("gpsimd",
                                    lambda E, d=wub[buf_i][:].rearrange("p (k n) -> p k n", n=512),
                                    s=wu_v[:, :, ig * 512:(ig + 1) * 512]:
                                    E.dma_start(out=d, in_=s))
                            t_wgb_ready[buf_i] = P.mark("gpsimd", S["dWF"], 16)
                            return t_wgb_ready[buf_i]

                        t_wg_free = [None, None]
                        load_wgrp(0, 0, None, None)
                        t_wd_free = None
                        t_wd = None
                        t_msk_prev = None
                        t_hs_prev = None
                        t_hs_free = None
                        t_hsT_free = None
                        t_acc_done = [None] * 8
                        t_psum_free = [t_PB_free, t_PA_free]
                        t_g_free = None
                        t_down_done = None
                        NU = NIG * 8

                        def emitA(u):
                            nonlocal t_wd, t_wd_free
                            ig, ts = u // 8, u % 8
                            bi = ig % 2
                            par = u % 2
                            pt = P_B if par == 0 else P_A
                            P.wait("tensor", t_wgb_ready[bi])
                            P.wait("tensor", t_xnfm)
                            P.wait("tensor", t_psum_free[par])
                            for k in range(NK):
                                P.instr("tensor",
                                        lambda E, o=pt[:, 0:512],
                                        l=xnfm[:, k * 1024 + ts * 128:k * 1024 + (ts + 1) * 128],
                                        r=wgb[bi][:, k * 512:(k + 1) * 512],
                                        st=(k == 0), sp=(k == NK - 1):
                                        E.matmul(o, l, r, start=st, stop=sp))
                                P.instr("tensor",
                                        lambda E, o=pt[:, 512:1024],
                                        l=xnfm[:, k * 1024 + ts * 128:k * 1024 + (ts + 1) * 128],
                                        r=wub[bi][:, k * 512:(k + 1) * 512],
                                        st=(k == 0), sp=(k == NK - 1):
                                        E.matmul(o, l, r, start=st, stop=sp))
                            return P.mark("tensor", S["sP"]), pt

                        def emitRest(u, t_gu, pt):
                            nonlocal t_msk_prev, t_hs_prev, t_hs_free, t_hsT_free
                            nonlocal t_psum_free, t_g_free, t_down_done, t_wd_free
                            ig, ts = u // 8, u % 8
                            par = u % 2
                            P.wait("scalar", t_gu)
                            P.wait("scalar", t_g_free)
                            P.instr("scalar",
                                    lambda E: E.activation(g_b[:], pt[:, 0:512], AF.Silu))
                            t_si = P.mark("scalar", S["sS"])
                            P.wait("vector", t_si)
                            P.instr("vector",
                                    lambda E: E.tensor_tensor(h_b[:], g_b[:], pt[:, 512:1024],
                                                              ALU.mult))
                            t_h = P.mark("vector", S["sV"])
                            t_psum_free[par] = t_h
                            t_g_free = t_h
                            P.wait("scalar", t_h)
                            P.wait("scalar", t_msk_prev)
                            P.instr("scalar",
                                    lambda E: E.activation(a_b[:], h_b[:], AF.Square))
                            t_a = P.mark("scalar", S["sS"])
                            P.wait("vector", t_a)
                            P.instr("vector",
                                    lambda E: E.tensor_tensor(tA[:], a_g[:, :, 0], a_g[:, :, 1], ALU.max))
                            P.instr("vector",
                                    lambda E: E.tensor_tensor(tB[:], a_g[:, :, 2], a_g[:, :, 3], ALU.max))
                            P.instr("vector",
                                    lambda E: E.tensor_tensor(tC2[:], a_g[:, :, 0], a_g[:, :, 1], ALU.min))
                            P.instr("vector",
                                    lambda E: E.tensor_tensor(tD2[:], a_g[:, :, 2], a_g[:, :, 3], ALU.min))
                            P.instr("vector",
                                    lambda E: E.tensor_tensor(tA[:], tA[:], tB[:], ALU.min))
                            P.instr("vector",
                                    lambda E: E.tensor_tensor(tC2[:], tC2[:], tD2[:], ALU.max))
                            P.instr("vector",
                                    lambda E: E.tensor_tensor(sec[:], tA[:], tC2[:], ALU.max))
                            for j4 in range(4):
                                P.instr("vector",
                                        lambda E, j4=j4:
                                        E.tensor_tensor(m_g[:, :, j4], a_g[:, :, j4], sec[:],
                                                        ALU.is_ge))
                            t_msk = P.mark("vector", S["sV"])
                            t_msk_prev = t_msk
                            P.wait("vector", t_hs_free)
                            P.instr("vector",
                                    lambda E: E.tensor_tensor(hs_b[:], h_b[:], msk[:], ALU.mult))
                            t_hs = P.mark("vector", S["sV"])
                            t_hs_prev = t_hs
                            if DBG_DUMP_D and ig == 0 and ts == 0:
                                P.wait("vector", t_gu)
                                P.instr("vector", lambda E: E.tensor_copy(xtf[0][:, 0:1024], pt[:]))
                                t_dgu = P.mark("vector", S["sV"])
                                for nm, src in [("g", g_b), ("h", h_b), ("a", a_b),
                                                ("sec", sec), ("msk", msk), ("hs", hs_b)]:
                                    P.wait("sync", t_hs)
                                    P.wait("sync", t_dgu)
                                    P.instr("sync",
                                            lambda E, d=dbgd[nm], s2=src:
                                            E.dma_start(out=d, in_=s2[:]))
                                    P.mark("sync", S["dSO"], 16)
                                P.instr("sync", lambda E: E.dma_start(out=dbgd["gu"], in_=xtf[0][:, 0:1024]))
                                t_dgu2 = P.mark("sync", S["dSO"], 16)
                                P.wait("sync", t_dgu2)
                            t_hsT = transpose_group(hs_b, 0, 4, hsT, 0, ig % 2, t_hs)
                            t_hs_free = t_hsT
                            if DBG_DUMP_D and ig == 0 and ts == 0:
                                P.wait("sync", t_hsT)
                                P.instr("sync", lambda E: E.dma_start(out=dbgd["hsT"], in_=hsT[:]))
                                P.mark("sync", S["dSO"], 16)
                            t_ac = None
                            for qt in range(4):
                                bk = qt % 2
                                P.wait("tensor", t_hsT)
                                P.wait("tensor", t_wd)
                                P.wait("tensor", t_PC_free_l[bk])
                                for c in range(4):
                                    P.instr("tensor",
                                            lambda E, o=P_C[:, bk * 512:(bk + 1) * 512],
                                            l=hsT[:, c * 128:(c + 1) * 128],
                                            r=wdb[:, c * 2048 + qt * 512:
                                                  c * 2048 + (qt + 1) * 512],
                                            st=(c == 0), sp=(c == 3):
                                            E.matmul(o, l, r, start=st, stop=sp))
                                t_dn = P.mark("tensor", S["sP"])
                                P.wait("vector", t_dn)
                                if ig == 0:
                                    P.instr("vector",
                                            lambda E, d=accb[:, ts * D + qt * 512:
                                                           ts * D + (qt + 1) * 512],
                                            sp2=P_C[:, bk * 512:(bk + 1) * 512]:
                                            E.tensor_copy(d, sp2))
                                else:
                                    P.instr("vector",
                                            lambda E, d=accb[:, ts * D + qt * 512:
                                                           ts * D + (qt + 1) * 512],
                                            sp2=P_C[:, bk * 512:(bk + 1) * 512]:
                                            E.tensor_tensor(d, d, sp2, ALU.add))
                                t_ac = P.mark("vector", S["sV"])
                                t_PC_free_l[bk] = t_ac
                            if ig == NIG - 1:
                                t_acc_done[ts] = t_ac
                            t_down_done = t_dn

                        t_PC_free_l = [t_PC_free, t_PC_free]

                        t_xtf_free = [None]
                        t_ob_free = [None]

                        def emit_final(ts):
                            tg = half * 8 + ts
                            sl = 0
                            P.wait("sync", (S["dSX"], S["dSX"].v))
                            P.wait("sync", t_xtf_free[sl])
                            P.instr("sync",
                                    lambda E, d=xtf[sl], s=xn_scr[ts, :, :]:
                                    E.dma_start(out=d[:], in_=s))
                            t_xnl = P.mark("sync", S["dXN"], 16)
                            P.wait("vector", t_xnl)
                            P.wait("vector", t_acc_done[ts])
                            P.wait("vector", t_ob_free[0])
                            ob = outb[0]
                            P.instr("vector",
                                    lambda E, d=ob, a=accb[:, ts * D:(ts + 1) * D], x2=xtf[sl]:
                                    E.scalar_tensor_tensor(d[:], a, 1.0, x2[:], ALU.mult, ALU.add,
                                                           accum_out=r2sm[:]))
                            t_r2 = P.mark("vector", S["sV"])
                            P.wait("scalar", t_r2)
                            P.instr("scalar",
                                    lambda E, s=ob, d=accb[:, ts * D:(ts + 1) * D]:
                                    E.activation(d, s[:], AF.Square, accum_out=q2sm[:]))
                            t_q2 = P.mark("scalar", S["sS"])
                            P.wait("vector", t_q2)
                            P.instr("vector",
                                    lambda E: E.tensor_scalar(l21[:], r2sm[:], 1.0 / D, None, ALU.mult))
                            P.instr("vector",
                                    lambda E: E.tensor_scalar(l22[:], q2sm[:], 1.0 / D, None, ALU.mult))
                            vsync()
                            P.instr("vector",
                                    lambda E: E.tensor_tensor(l23[:], l21[:], l21[:], ALU.mult))
                            vsync()
                            P.instr("vector",
                                    lambda E: E.tensor_tensor(l22[:], l22[:], l23[:], ALU.subtract))
                            t_v2 = P.mark("vector", S["sV"])
                            P.wait("scalar", t_v2)
                            P.instr("scalar",
                                    lambda E: E.activation(l22[:], l22[:], AF.Sqrt,
                                                           bias=c_eps[:], scale=1.0))
                            t_s2 = P.mark("scalar", S["sS"])
                            P.wait("vector", t_s2)
                            P.instr("vector", lambda E: E.reciprocal(l23[:], l22[:]))
                            vsync()
                            P.instr("vector",
                                    lambda E, d=xtf[sl], s=ob: E.tensor_scalar(d[:], s[:], l21[:],
                                                                               None, ALU.subtract))
                            P.instr("vector",
                                    lambda E, d=ob, s=xtf[sl]: E.tensor_scalar(d[:], s[:], l23[:],
                                                                               None, ALU.mult))
                            t_o = P.mark("vector", S["sV"])
                            t_xtf_free[sl] = t_o
                            P.wait("sync", t_o)
                            P.instr("sync",
                                    lambda E, d=y_out[seq, tg * 128:(tg + 1) * 128, :], s=ob:
                                    E.dma_start(out=d, in_=s[:]))
                            t_ob_free[0] = P.mark("sync", S["dOUT"], 16)
                            nonlocal_n_out[0] += 1

                        prev = None
                        for u in range(NU):
                            ig, ts = u // 8, u % 8
                            cur = emitA(u)
                            if (u == 0) or (ts == 1 and ig >= 1):
                                P.wait("gpsimd", t_wd_free)
                                P.instr("gpsimd",
                                        lambda E, d=wdb[:].rearrange("p (c n) -> p c n", n=2048),
                                        s=wd_v[:, 4 * ig:4 * ig + 4, :]:
                                        E.dma_start(out=d, in_=s))
                                t_wd = P.mark("gpsimd", S["dWD"], 16)
                                if ig + 1 < NIG:
                                    load_wgrp(ig + 1, (ig + 1) % 2,
                                              t_wg_free[(ig + 1) % 2], t_wg_free[(ig + 1) % 2])
                            if prev is not None:
                                emitRest(*prev)
                                if ts == 0:
                                    t_wg_free[(ig + 1) % 2] = prev[1]
                                    t_wd_free = t_down_done
                                if prev[0] // 8 == NIG - 1:
                                    emit_final(prev[0] % 8)
                            prev = (u, cur[0], cur[1])
                        emitRest(*prev)
                        emit_final(prev[0] % 8)
                        t_PC_free = [t_PC_free_l[0], t_PC_free_l[1]]
                        t_PA_free = t_psum_free[1]
                        t_PB_free = t_psum_free[0]

                        if DBG_DUMP_D and half == 0:
                            P.wait("sync", (S["sV"], S["sV"].v))
                            P.instr("sync", lambda E: E.dma_start(out=dbgd["acc"], in_=accb[:]))
                            P.mark("sync", S["dSO"], 16)
                            P.instr("sync", lambda E: E.dma_start(out=dbgd["xnfm"], in_=xnfm[:]))
                            P.mark("sync", S["dSO"], 16)
                        t_xn_consumed = (S["dXN"], S["dXN"].v)
                        P.flush(block)

                t_oscr_consumed = (S["dOS"], S["dOS"].v)

        n_out = nonlocal_n_out[0]
        for nm in ("dOUT", "dSO", "dSX"):
            if S[nm].v:
                P.wait("sync", (S[nm], S[nm].v))
        for nm in ("sP", "sV", "sS", "sG"):
            if S[nm].v:
                P.wait("sync", (S[nm], S[nm].v))
        P.flush(block)

    return nc


_NC_CACHE = None


def _get_nc():
    global _NC_CACHE
    if _NC_CACHE is None:
        _NC_CACHE = build_nc()
    return _NC_CACHE


def kernel(L0_attn_x, L0_attn_Wq, L0_attn_Wk, L0_attn_Wv, L0_attn_Wo,
           L0_ffn_w_gate, L0_ffn_w_up, L0_ffn_w_down):
    x = np.ascontiguousarray(np.asarray(L0_attn_x, dtype=np.float32))
    wq = np.ascontiguousarray(np.asarray(L0_attn_Wq, dtype=np.float32))
    wk = np.ascontiguousarray(np.asarray(L0_attn_Wk, dtype=np.float32))
    wv = np.ascontiguousarray(np.asarray(L0_attn_Wv, dtype=np.float32))
    wo = np.ascontiguousarray(np.asarray(L0_attn_Wo, dtype=np.float32))
    wg = np.ascontiguousarray(np.asarray(L0_ffn_w_gate, dtype=np.float32))
    wu = np.ascontiguousarray(np.asarray(L0_ffn_w_up, dtype=np.float32))
    wd = np.ascontiguousarray(np.asarray(L0_ffn_w_down, dtype=np.float32))

    c_id = np.eye(128, dtype=ml_dtypes.bfloat16)
    c_tril = (np.tril(np.ones((128, 128), dtype=np.float32)) * SCALE).astype(np.float32)
    c_eps = np.full((128, 1), EPS, dtype=np.float32)
    c_cnt2 = np.tile(np.array([float(T - (q + 1) * 128) for q in range(NQ)],
                              dtype=np.float32)[None, :], (128, 1))

    nc = _get_nc()
    in_maps = []
    for c in range(NCORES):
        in_maps.append({
            "x": x[c * BLOC:(c + 1) * BLOC],
            "wq": wq, "wk": wk, "wv": wv, "wo": wo,
            "wg": wg, "wu": wu, "wd": wd,
            "c_id": c_id, "c_tril": c_tril, "c_eps": c_eps, "c_cnt2": c_cnt2,
        })
    res = run_bass_kernel_spmd(nc, in_maps, core_ids=list(range(NCORES)))
    out = np.concatenate([r["y"] for r in res.results], axis=0)
    return out.astype(np.float32)
